# version 05
# baseline (speedup 1.0000x reference)
import numpy as np
from contextlib import ExitStack

import concourse.bass as bass
import concourse.mybir as mybir
from concourse.bass_utils import run_bass_kernel_spmd

F32 = mybir.dt.float32
BF16 = mybir.dt.bfloat16
AF = mybir.ActivationFunctionType
ALU = mybir.AluOpType
AX = mybir.AxisListType

TOK, NT = 1024, 8
EXT, NTE = 1280, 10
D, ND = 2048, 16
H = 16
MSLOT, NS = 4096, 8
NE, DFF, NI = 16, 1024, 8
EPS_LN = 1e-5
EPS_RT = 1e-9
ISQD = float(1.0 / np.sqrt(2048.0))
ISQH = float(1.0 / np.sqrt(128.0))


class Sem:
    def __init__(self, ctx, nc, name):
        self.h = ctx.enter_context(nc.semaphore(name))
        self.n = 0

    def inc(self, ins, k=1):
        ins.then_inc(self.h, k)
        self.n += k
        return self.n


def build_nc():
    nc = bass.Bass("TRN2", target_bir_lowering=False)

    x_ext = nc.dram_tensor("x_ext", [EXT, D], F32, kind="ExternalInput").ap()
    wq_t = nc.dram_tensor("wq_t", [D, D], F32, kind="ExternalInput").ap()
    wk_t = nc.dram_tensor("wk_t", [D, D], F32, kind="ExternalInput").ap()
    wv_t = nc.dram_tensor("wv_t", [D, D], F32, kind="ExternalInput").ap()
    wo_t = nc.dram_tensor("wo_t", [D, D], F32, kind="ExternalInput").ap()
    wr_t = nc.dram_tensor("wr_t", [D, NE], F32, kind="ExternalInput").ap()
    gate_w = nc.dram_tensor("gate_w", [1, D], F32, kind="ExternalInput").ap()
    gate_b = nc.dram_tensor("gate_b", [1, 1], F32, kind="ExternalInput").ap()
    memb_t = nc.dram_tensor("memb_t", [D, MSLOT], F32, kind="ExternalInput").ap()
    memb = nc.dram_tensor("memb", [MSLOT, D], F32, kind="ExternalInput").ap()
    Wg_d = nc.dram_tensor("Wg_d", [NE, D, DFF], F32, kind="ExternalInput").ap()
    Wu_d = nc.dram_tensor("Wu_d", [NE, D, DFF], F32, kind="ExternalInput").ap()
    Wd_d = nc.dram_tensor("Wd_d", [NE, DFF, D], F32, kind="ExternalInput").ap()
    sWg_d = nc.dram_tensor("sWg_d", [D, DFF], F32, kind="ExternalInput").ap()
    sWu_d = nc.dram_tensor("sWu_d", [D, DFF], F32, kind="ExternalInput").ap()
    sWd_d = nc.dram_tensor("sWd_d", [DFF, D], F32, kind="ExternalInput").ap()
    fbias = nc.dram_tensor("fbias", [1, NE], F32, kind="ExternalInput").ap()
    masks = nc.dram_tensor("masks", [3, 128, 384], F32, kind="ExternalInput").ap()
    eye_f = nc.dram_tensor("eye_f", [128, 128], F32, kind="ExternalInput").ap()
    ones_r = nc.dram_tensor("ones_r", [1, 128], F32, kind="ExternalInput").ap()
    eps_c = nc.dram_tensor("eps_c", [128, 1], F32, kind="ExternalInput").ap()
    out_d = nc.dram_tensor("out", [TOK, D], F32, kind="ExternalOutput").ap()
    sim_hbm = nc.dram_tensor("sim_hbm", [NT, 128, MSLOT], F32, kind="Internal").ap()
    x1_hbm = nc.dram_tensor("x1_hbm", [TOK, D], F32, kind="Internal").ap()

    ctx = ExitStack()
    with ctx:
        E = ctx.enter_context

        small = E(nc.sbuf_tensor("small", [128, 64], F32))
        top8f = E(nc.sbuf_tensor("top8f", [128, NT, 8], F32))
        e8 = E(nc.sbuf_tensor("e8", [128, NT, 8], F32))
        eyeb = E(nc.sbuf_tensor("eyeb", [128, 128], BF16))
        eyef = E(nc.sbuf_tensor("eyef", [128, 128], F32))
        onesb = E(nc.sbuf_tensor("onesb", [1, 128], BF16))
        maskb = E(nc.sbuf_tensor("maskb", [128, 3, 384], BF16))
        gwsrc = E(nc.sbuf_tensor("gwsrc", [1, D], BF16))
        gwb = E(nc.sbuf_tensor("gwb", [128, D], BF16))
        wrT = E(nc.sbuf_tensor("wrT", [128, ND, NE], BF16))
        fbias_b = E(nc.sbuf_tensor("fbias_b", [1, NE], BF16))
        gate_b_b = E(nc.sbuf_tensor("gate_b_b", [1, 1], BF16))
        WmatT = E(nc.sbuf_tensor("WmatT", [16, NT * 128], BF16))
        logit_sb = E(nc.sbuf_tensor("logit_sb", [128, NT, NE], F32))
        lexp = E(nc.sbuf_tensor("lexp", [128, NT, NE], F32))
        rst = E(nc.sbuf_tensor("rst", [128, NT, 8], F32))
        lnst = E(nc.sbuf_tensor("lnst", [128, NTE, 8], F32))
        wmat_tok = E(nc.sbuf_tensor("wmat_tok", [128, NT, NE], BF16))
        wrow = E(nc.sbuf_tensor("wrow", [1, 2, TOK], BF16))
        msk1 = E(nc.sbuf_tensor("msk1", [128, NT, NE], F32))
        msk2 = E(nc.sbuf_tensor("msk2", [128, NT, NE], F32))
        zrow = E(nc.sbuf_tensor("zrow", [1, 512], BF16))

        QX = Sem(ctx, nc, "QX")
        QW = Sem(ctx, nc, "QW")
        QS = Sem(ctx, nc, "QS")
        TS = Sem(ctx, nc, "TS")
        VS = Sem(ctx, nc, "VS")
        SS = Sem(ctx, nc, "SS")

        block = E(nc.Block())

        def sec(name, fn):
            getattr(block, name)(fn)

        sc_xhat = ExitStack()
        xhatT = sc_xhat.enter_context(nc.sbuf_tensor("xhatT", [128, ND, EXT], BF16, side="right"))

        p0 = ExitStack()
        xext = p0.enter_context(nc.sbuf_tensor("xext", [128, 5, D], F32))
        xhtok = p0.enter_context(nc.sbuf_tensor("xhtok", [128, 5, D], BF16))
        sqs = p0.enter_context(nc.sbuf_tensor("sqs", [128, 2, D], F32))
        ps_tr = [p0.enter_context(nc.psum_tensor(f"p0t{j}", [128, 1024], BF16))
                 for j in range(4)]
        ps_g = p0.enter_context(nc.psum_tensor("p0g", [128, 2048], F32))

        def p0_g(g):
            g.dma_start(out=eyeb[:, :], in_=eye_f[:, :]).then_inc(QW.h, 16)
            g.dma_start(out=onesb[:, :], in_=ones_r[:, :]).then_inc(QW.h, 16)
            g.dma_start(out=maskb[:, :, :],
                        in_=masks.rearrange("m p n -> p m n")).then_inc(QW.h, 16)
            g.dma_start(out=wrT[:, :, :],
                        in_=wr_t.rearrange("(kt p) n -> p kt n", p=128)).then_inc(QW.h, 16)
            g.dma_start(out=fbias_b[:, :], in_=fbias[:, :]).then_inc(QW.h, 16)
            g.dma_start(out=gate_b_b[:, :], in_=gate_b[:, :]).then_inc(QW.h, 16)
            g.dma_start(out=gwsrc[:, :], in_=gate_w[:, :]).then_inc(QW.h, 16)
            QW.n += 7 * 16
        sec("gpsimd", p0_g)
        qw_consts = QW.n

        def p0_s0(s):
            s.dma_start(out=eyef[:, :], in_=eye_f[:, :]).then_inc(QX.h, 16)
            s.dma_start(out=small[:, 0:1], in_=eps_c[:, :]).then_inc(QX.h, 16)
            QX.n += 2 * 16
        sec("sync", p0_s0)

        def p0_t(t):
            t.wait_ge(QW.h, qw_consts)
            for c in range(4):
                ins = t.matmul(ps_g[:, c * 512:(c + 1) * 512], onesb[:, :],
                               gwsrc[:, c * 512:(c + 1) * 512], start=True, stop=True)
            TS.inc(ins)
        sec("tensor", p0_t)
        ts_gw = TS.n

        def p0_v0(v):
            v.wait_ge(TS.h, ts_gw)
            v.memset(zrow[:, :], 0.0)
            ins = v.tensor_copy(gwb[:, :], ps_g[:, :])
            VS.inc(ins)
        sec("vector", p0_v0)
        vs_gw = VS.n

        def p0_t2(t):
            t.wait_ge(VS.h, vs_gw)
            ins = t.matmul(ps_g[:, 0:1], onesb[:, :], gate_b_b[:, :], start=True, stop=True)
            TS.inc(ins)
        sec("tensor", p0_t2)
        ts_gb = TS.n

        def p0_v0b(v):
            v.wait_ge(TS.h, ts_gb)
            ins = v.tensor_copy(small[:, 1:2], ps_g[:, 0:1])
            VS.inc(ins)
        sec("vector", p0_v0b)
        vs_gb = VS.n

        ss_tr = {}
        for hb in range(2):
            tl = [hb * 5 + u for u in range(5)]

            def p0_s(s, hb=hb):
                if hb == 1:
                    s.wait_ge(SS.h, ss_tr[4])
                s.dma_start(out=xext[:, :, :],
                            in_=x_ext[hb * 640:(hb + 1) * 640, :]
                            .rearrange("(t p) d -> p t d", p=128)).then_inc(QX.h, 16)
                QX.n += 16
            sec("sync", p0_s)
            qx_h = QX.n

            def p0_v1(v, hb=hb, w=qx_h):
                v.wait_ge(QX.h, w)
                for u in range(5):
                    ins = v.reduce_sum(lnst[:, hb * 5 + u, 0:1], xext[:, u, :], axis=AX.X)
                VS.inc(ins)
            sec("vector", p0_v1)

            def p0_s1(sc, hb=hb, w=qx_h):
                sc.wait_ge(QX.h, w)
                for u in range(5):
                    ins = sc.activation(sqs[:, u % 2, :], xext[:, u, :], AF.Square,
                                        accum_out=lnst[:, hb * 5 + u, 1:2])
                SS.inc(ins)
            sec("scalar", p0_s1)
            ss_sq1 = SS.n

            def p0_v2(v, hb=hb, w=ss_sq1):
                v.wait_ge(SS.h, w)
                for u in range(5):
                    i = hb * 5 + u
                    v.tensor_scalar(lnst[:, i, 2:3], lnst[:, i, 0:1], 1.0 / D, None,
                                    op0=ALU.mult)
                for u in range(5):
                    i = hb * 5 + u
                    v.tensor_tensor(lnst[:, i, 3:4], lnst[:, i, 2:3], lnst[:, i, 2:3],
                                    op=ALU.mult)
                for u in range(5):
                    i = hb * 5 + u
                    ins = v.scalar_tensor_tensor(lnst[:, i, 4:5], lnst[:, i, 1:2], 1.0 / D,
                                                 lnst[:, i, 3:4], op0=ALU.mult,
                                                 op1=ALU.subtract)
                VS.inc(ins)
            sec("vector", p0_v2)
            vs_var1 = VS.n

            def p0_s2(sc, hb=hb, w=vs_var1):
                sc.wait_ge(VS.h, w)
                for u in range(5):
                    i = hb * 5 + u
                    ins = sc.activation(lnst[:, i, 5:6], lnst[:, i, 4:5], AF.Sqrt,
                                        bias=small[:, 0:1], scale=1.0)
                SS.inc(ins)
            sec("scalar", p0_s2)
            ss_sd1 = SS.n

            def p0_v3(v, hb=hb, w=ss_sd1):
                v.wait_ge(SS.h, w)
                for u in range(5):
                    i = hb * 5 + u
                    v.reciprocal(lnst[:, i, 6:7], lnst[:, i, 5:6])
                for u in range(5):
                    i = hb * 5 + u
                    ins = v.tensor_scalar(xhtok[:, u, :], xext[:, u, :],
                                          lnst[:, i, 2:3], lnst[:, i, 6:7],
                                          op0=ALU.subtract, op1=ALU.mult)
                VS.inc(ins)
            sec("vector", p0_v3)
            vs_xh1 = VS.n

            for u in range(5):
                i = hb * 5 + u
                lo, hi = ps_tr[(2 * i) % 4], ps_tr[(2 * i + 1) % 4]

                def p0_ti(t, u=u, i=i, lo=lo, hi=hi, w=vs_xh1):
                    if u == 0:
                        t.wait_ge(VS.h, w)
                    if i >= 2:
                        t.wait_ge(SS.h, ss_tr[i - 2])
                    for d in range(ND):
                        bank = lo if d < 8 else hi
                        ins = t.transpose(bank[:, (d % 8) * 128:(d % 8 + 1) * 128],
                                          xhtok[:, u, d * 128:(d + 1) * 128], eyeb[:, :])
                    TS.inc(ins)
                sec("tensor", p0_ti)
                w_tr = TS.n

                def p0_si(sc, i=i, lo=lo, hi=hi, w=w_tr):
                    sc.wait_ge(TS.h, w)
                    sc.copy(xhatT[:, 0:8, i * 128:(i + 1) * 128],
                            lo[:, :].rearrange("p (a b) -> p a b", a=8))
                    ins = sc.copy(xhatT[:, 8:16, i * 128:(i + 1) * 128],
                                  hi[:, :].rearrange("p (a b) -> p a b", a=8))
                    SS.inc(ins)
                sec("scalar", p0_si)
                ss_tr[i] = SS.n
        ss_xhatT = SS.n

        p0.close()

        p1 = ExitStack()
        top8p = p1.enter_context(nc.sbuf_tensor("top8p", [128, NT, NS, 8], F32))
        wkT = p1.enter_context(nc.sbuf_tensor("wkT", [128, ND, D], BF16))
        mbt = p1.enter_context(nc.sbuf_tensor("mbt", [128, 2, ND, 512], BF16))
        kmemT = p1.enter_context(nc.sbuf_tensor("kmemT", [128, ND, 512], BF16))
        simf = p1.enter_context(nc.sbuf_tensor("simf", [128, 2, 512], F32))
        ps1 = [p1.enter_context(nc.psum_tensor(f"p1_{j}", [128, 512], F32)) for j in range(6)]

        def p1_g(g):
            g.wait_ge(SS.h, ss_xhatT)
            g.dma_start(out=wkT[:, :, :],
                        in_=wk_t.rearrange("(kt p) n -> p kt n", p=128)).then_inc(QW.h, 16)
            QW.n += 16
        sec("gpsimd", p1_g)
        qw_wk = QW.n

        qw_mbt = {}

        def p1_g2(g):
            g.dma_start(out=mbt[:, 0, :, :],
                        in_=memb_t[:, 0:512].rearrange("(kt p) n -> p kt n", p=128)
                        ).then_inc(QW.h, 16)
            QW.n += 16
        sec("gpsimd", p1_g2)
        qw_mbt[0] = QW.n

        ts_kmem = {}
        ss_kmem = {}
        ss_sim = {}
        vs_max = {}
        qs_sim = {}

        for s in range(NS):
            if s + 1 < NS:
                def p1_gp(g, s=s):
                    if s >= 1:
                        g.wait_ge(TS.h, ts_kmem[(s - 1) * ND + ND - 1])
                    g.dma_start(out=mbt[:, (s + 1) % 2, :, :],
                                in_=memb_t[:, (s + 1) * 512:(s + 2) * 512]
                                .rearrange("(kt p) n -> p kt n", p=128)).then_inc(QW.h, 16)
                    QW.n += 16
                sec("gpsimd", p1_gp)
                qw_mbt[s + 1] = QW.n

            for m in range(ND):
                idx = s * ND + m
                bank = ps1[idx % 2]

                def p1_km(t, s=s, m=m, bank=bank, idx=idx):
                    if m == 0:
                        t.wait_ge(QW.h, qw_mbt[s])
                    if idx >= 2:
                        t.wait_ge(SS.h, ss_kmem[idx - 2])
                    for d in range(ND):
                        ins = t.matmul(bank[:, :],
                                       wkT[:, d, m * 128:(m + 1) * 128],
                                       mbt[:, s % 2, d, :],
                                       start=(d == 0), stop=(d == ND - 1))
                    TS.inc(ins)
                sec("tensor", p1_km)
                ts_kmem[idx] = TS.n

                def p1_kme(sc, m=m, bank=bank, w=TS.n):
                    sc.wait_ge(TS.h, w)
                    ins = sc.copy(kmemT[:, m, :], bank[:, :])
                    SS.inc(ins)
                sec("scalar", p1_kme)
                ss_kmem[idx] = SS.n

            for t_i in range(NT):
                idx = s * NT + t_i
                bank = ps1[2 + idx % 2]

                def p1_sm(t, s=s, t_i=t_i, bank=bank, idx=idx):
                    if t_i == 0:
                        t.wait_ge(SS.h, ss_kmem[s * ND + ND - 1])
                    if idx >= 2:
                        t.wait_ge(SS.h, ss_sim[idx - 2])
                    for d in range(ND):
                        ins = t.matmul(bank[:, :],
                                       xhatT[:, d, 256 + t_i * 128:256 + (t_i + 1) * 128],
                                       kmemT[:, d, :],
                                       start=(d == 0), stop=(d == ND - 1))
                    TS.inc(ins)
                sec("tensor", p1_sm)
                w_sm = TS.n

                def p1_se(sc, bank=bank, idx=idx, w=w_sm):
                    sc.wait_ge(TS.h, w)
                    if idx >= 2:
                        sc.wait_ge(VS.h, vs_max[idx - 2])
                        sc.wait_ge(QS.h, qs_sim[idx - 2])
                    ins = sc.activation(simf[:, idx % 2, :], bank[:, :], AF.Identity,
                                        scale=ISQD)
                    SS.inc(ins)
                sec("scalar", p1_se)
                ss_sim[idx] = SS.n

                def p1_mx(v, s=s, t_i=t_i, idx=idx, w=SS.n):
                    v.wait_ge(SS.h, w)
                    ins = v.max(out=top8p[:, t_i, s, :], in_=simf[:, idx % 2, :])
                    VS.inc(ins)
                sec("vector", p1_mx)
                vs_max[idx] = VS.n

                def p1_st(sy, s=s, t_i=t_i, idx=idx, w=SS.n):
                    sy.wait_ge(SS.h, w)
                    sy.dma_start(out=sim_hbm[t_i, :, s * 512:(s + 1) * 512],
                                 in_=simf[:, idx % 2, :]).then_inc(QS.h, 16)
                    QS.n += 16
                sec("sync", p1_st)
                qs_sim[idx] = QS.n

        def p1_v4(v):
            v.wait_ge(VS.h, vs_max[NS * NT - 1])
            for t_i in range(NT):
                ins = v.max(out=top8f[:, t_i, :], in_=top8p[:, t_i, :, :])
            VS.inc(ins)
        sec("vector", p1_v4)
        vs_t8 = VS.n

        def p1_s4(sc):
            sc.wait_ge(VS.h, vs_t8)
            for t_i in range(NT):
                sc.activation(small[:, 2 + t_i:3 + t_i], top8f[:, t_i, 0:1],
                              AF.Identity, scale=-1.0)
            for t_i in range(NT):
                ins = sc.activation(e8[:, t_i, :], top8f[:, t_i, :], AF.Exp,
                                    bias=small[:, 2 + t_i:3 + t_i], scale=1.0)
            SS.inc(ins)
        sec("scalar", p1_s4)
        ss_e8 = SS.n

        def p1_v5(v):
            v.wait_ge(SS.h, ss_e8)
            for t_i in range(NT):
                v.reduce_sum(rst[:, t_i, 7:8], e8[:, t_i, :], axis=AX.X)
            for t_i in range(NT):
                ins = v.reciprocal(small[:, 10 + t_i:11 + t_i], rst[:, t_i, 7:8])
            VS.inc(ins)
        sec("vector", p1_v5)
        vs_mden = VS.n

        p1.close()

        sc_k = ExitStack()
        kT = sc_k.enter_context(nc.sbuf_tensor("kT", [128, ND, EXT], BF16))

        p1b = ExitStack()
        wkc = p1b.enter_context(nc.sbuf_tensor("wkc", [128, 2, ND, 128], BF16))
        ps1b = [p1b.enter_context(nc.psum_tensor(f"p1b{j}", [128, 512], F32))
                for j in range(2)]

        qw_wk2, ts_k2, ss_kev = {}, {}, {}
        for m in range(ND):
            def p1b_g(g, m=m):
                if m == 0:
                    g.wait_ge(VS.h, vs_mden)
                if m >= 2:
                    g.wait_ge(TS.h, ts_k2[m - 2])
                g.dma_start(out=wkc[:, m % 2, :, :],
                            in_=wk_t[:, m * 128:(m + 1) * 128]
                            .rearrange("(kt p) n -> p kt n", p=128)).then_inc(QW.h, 16)
                QW.n += 16
            sec("gpsimd", p1b_g)
            qw_wk2[m] = QW.n

            for c in range(3):
                ncols = 512 if c < 2 else 256
                idx = m * 3 + c

                def p1b_t(t, m=m, c=c, ncols=ncols, idx=idx):
                    if c == 0:
                        t.wait_ge(QW.h, qw_wk2[m])
                    if idx >= 2:
                        t.wait_ge(SS.h, ss_kev[idx - 2])
                    for d in range(ND):
                        ins = t.matmul(ps1b[idx % 2][:, 0:ncols],
                                       wkc[:, m % 2, d, :],
                                       xhatT[:, d, c * 512:c * 512 + ncols],
                                       start=(d == 0), stop=(d == ND - 1))
                    TS.inc(ins)
                sec("tensor", p1b_t)
                ts_k2[m] = TS.n
                w_k = TS.n

                def p1b_s(sc, m=m, c=c, ncols=ncols, idx=idx, w=w_k):
                    sc.wait_ge(TS.h, w)
                    ins = sc.copy(kT[:, m, c * 512:c * 512 + ncols],
                                  ps1b[idx % 2][:, 0:ncols])
                    SS.inc(ins)
                sec("scalar", p1b_s)
                ss_kev[idx] = SS.n

        p1b.close()

        sc_q = ExitStack()
        qT = sc_q.enter_context(nc.sbuf_tensor("qT", [128, ND, TOK], BF16))

        p2 = ExitStack()
        wqc = p2.enter_context(nc.sbuf_tensor("wqc", [128, 2, ND, 128], BF16))
        ps2 = [p2.enter_context(nc.psum_tensor(f"p2_{j}", [128, 512], F32)) for j in range(2)]

        qw_wq = {}
        ts_q = {}
        ss_qe = {}
        for m in range(ND):
            def p2_g(g, m=m):
                if m == 0:
                    g.wait_ge(TS.h, ts_k2[ND - 1])
                if m >= 2:
                    g.wait_ge(TS.h, ts_q[m - 2])
                g.dma_start(out=wqc[:, m % 2, :, :],
                            in_=wq_t[:, m * 128:(m + 1) * 128]
                            .rearrange("(kt p) n -> p kt n", p=128)).then_inc(QW.h, 16)
                QW.n += 16
            sec("gpsimd", p2_g)
            qw_wq[m] = QW.n

            def p2_t(t, m=m):
                t.wait_ge(QW.h, qw_wq[m])
                for c in range(2):
                    idx = m * 2 + c
                    if idx >= 2:
                        t.wait_ge(SS.h, ss_qe[idx - 2])
                    for d in range(ND):
                        ins = t.matmul(ps2[c][:, :],
                                       wqc[:, m % 2, d, :],
                                       xhatT[:, d, 256 + c * 512:256 + (c + 1) * 512],
                                       start=(d == 0), stop=(d == ND - 1))
                    TS.inc(ins)
            sec("tensor", p2_t)
            ts_q[m] = TS.n

            def p2_s(sc, m=m, w=TS.n):
                sc.wait_ge(TS.h, w)
                for c in range(2):
                    ins = sc.activation(qT[:, m, c * 512:(c + 1) * 512], ps2[c][:, :],
                                        AF.Identity, scale=ISQH)
                    SS.inc(ins)
            sec("scalar", p2_s)
            ss_qe[m * 2] = SS.n - 1
            ss_qe[m * 2 + 1] = SS.n

        p2.close()

        sc_v = ExitStack()
        v_sb = sc_v.enter_context(nc.sbuf_tensor("v_sb", [128, NTE, D], BF16))

        p3 = ExitStack()
        wvc = p3.enter_context(nc.sbuf_tensor("wvc", [128, 2, ND, 512], BF16))
        ps3 = [p3.enter_context(nc.psum_tensor(f"p3_{j}", [128, 512], F32)) for j in range(2)]

        qw_wv = {}
        ts_v = {}
        ss_ve = {}
        for c in range(4):
            def p3_g(g, c=c):
                if c == 0:
                    g.wait_ge(TS.h, ts_q[ND - 1])
                if c >= 2:
                    g.wait_ge(TS.h, ts_v[c - 2])
                g.dma_start(out=wvc[:, c % 2, :, :],
                            in_=wv_t[:, c * 512:(c + 1) * 512]
                            .rearrange("(kt p) n -> p kt n", p=128)).then_inc(QW.h, 16)
                QW.n += 16
            sec("gpsimd", p3_g)
            qw_wv[c] = QW.n

            for t_i in range(NTE):
                idx = c * NTE + t_i
                bank = ps3[idx % 2]

                def p3_t(t, c=c, t_i=t_i, bank=bank, idx=idx):
                    if t_i == 0:
                        t.wait_ge(QW.h, qw_wv[c])
                    if idx == 0:
                        t.wait_ge(SS.h, ss_qe[2 * ND - 1])
                    if idx >= 2:
                        t.wait_ge(SS.h, ss_ve[idx - 2])
                    for d in range(ND):
                        ins = t.matmul(bank[:, :],
                                       xhatT[:, d, t_i * 128:(t_i + 1) * 128],
                                       wvc[:, c % 2, d, :],
                                       start=(d == 0), stop=(d == ND - 1))
                    TS.inc(ins)
                sec("tensor", p3_t)
                ts_v[c] = TS.n

                def p3_s(sc, c=c, t_i=t_i, bank=bank, w=TS.n):
                    sc.wait_ge(TS.h, w)
                    ins = sc.copy(v_sb[:, t_i, c * 512:(c + 1) * 512], bank[:, :])
                    SS.inc(ins)
                sec("scalar", p3_s)
                ss_ve[idx] = SS.n

        p3.close()
        sc_xhat.close()

        sc_ctx = ExitStack()
        ctxT = sc_ctx.enter_context(nc.sbuf_tensor("ctxT", [128, ND, TOK], BF16, side="right"))

        p4 = ExitStack()
        pbuf = p4.enter_context(nc.sbuf_tensor("pbuf", [128, 2, 384], BF16))
        pTb = p4.enter_context(nc.sbuf_tensor("pTb", [128, 2, 3, 128], BF16))
        ctok = p4.enter_context(nc.sbuf_tensor("ctok", [128, 2, D], BF16))
        psS = [p4.enter_context(nc.psum_tensor(f"p4s{j}", [128, 512], F32)) for j in range(2)]
        psP = [p4.enter_context(nc.psum_tensor(f"p4p{j}", [128, 768], BF16)) for j in range(2)]
        psC = [p4.enter_context(nc.psum_tensor(f"p4c{j}", [128, 512], F32)) for j in range(2)]
        psX = [p4.enter_context(nc.psum_tensor(f"p4x{j}", [128, 1024], BF16)) for j in range(2)]

        tsS, ssE, tsP, ssP, tsC, ssC, vsR = {}, {}, {}, {}, {}, {}, {}
        tsXT, ssXT = {}, {}
        for t_i in range(NT):
            for h in range(H):
                idx = t_i * H + h
                mi = min(t_i, 2)

                def p4_smm(t, t_i=t_i, h=h, idx=idx, mi=mi):
                    if idx == 0:
                        t.wait_ge(SS.h, ss_ve[4 * NTE - 1])
                    if idx >= 2:
                        t.wait_ge(SS.h, ssE[idx - 2])
                    t.matmul(psS[idx % 2][:, 0:384],
                             qT[:, h, t_i * 128:(t_i + 1) * 128],
                             kT[:, h, t_i * 128:t_i * 128 + 384],
                             start=True, stop=False)
                    ins = t.matmul(psS[idx % 2][:, 0:384], eyeb[:, :],
                                   maskb[:, mi, :], start=False, stop=True)
                    TS.inc(ins)
                sec("tensor", p4_smm)
                tsS[idx] = TS.n

                def p4_neg(v, idx=idx, w=TS.n):
                    v.wait_ge(TS.h, w)
                    if idx >= 2:
                        v.wait_ge(SS.h, ssE[idx - 2])
                    ins = v.reduce_max(small[:, 30 + idx % 2:31 + idx % 2],
                                       psS[idx % 2][:, 0:384], axis=AX.X, negate=True)
                    VS.inc(ins)
                sec("vector", p4_neg)
                w_neg = VS.n

                def p4_exp(sc, idx=idx, w=w_neg):
                    sc.wait_ge(VS.h, w)
                    if idx >= 2:
                        sc.wait_ge(TS.h, tsP[idx - 2])
                        sc.wait_ge(VS.h, vsR[idx - 2])
                    ins = sc.activation(pbuf[:, idx % 2, :], psS[idx % 2][:, 0:384],
                                        AF.Exp, bias=small[:, 30 + idx % 2:31 + idx % 2],
                                        scale=1.0,
                                        accum_out=small[:, 32 + idx % 2:33 + idx % 2])
                    SS.inc(ins)
                sec("scalar", p4_exp)
                ssE[idx] = SS.n

                def p4_rec(v, idx=idx, w=SS.n):
                    v.wait_ge(SS.h, w)
                    if idx >= 2:
                        v.wait_ge(SS.h, ssC[idx - 2])
                    ins = v.reciprocal(small[:, 34 + idx % 2:35 + idx % 2],
                                       small[:, 32 + idx % 2:33 + idx % 2])
                    VS.inc(ins)
                sec("vector", p4_rec)
                vsR[idx] = VS.n

                def p4_pt(t, idx=idx, w=ssE[idx]):
                    t.wait_ge(SS.h, w)
                    if idx >= 2:
                        t.wait_ge(SS.h, ssP[idx - 2])
                    for j in range(3):
                        ins = t.transpose(psP[idx % 2][:, j * 128:(j + 1) * 128],
                                          pbuf[:, idx % 2, j * 128:(j + 1) * 128],
                                          eyeb[:, :])
                    TS.inc(ins)
                sec("tensor", p4_pt)
                tsP[idx] = TS.n

                def p4_pte(sc, idx=idx, w=TS.n):
                    sc.wait_ge(TS.h, w)
                    if idx >= 2:
                        sc.wait_ge(TS.h, tsC[idx - 2])
                    ins = sc.copy(pTb[:, idx % 2, :, :],
                                  psP[idx % 2][:, 0:384].rearrange("p (a b) -> p a b", a=3))
                    SS.inc(ins)
                sec("scalar", p4_pte)
                ssP[idx] = SS.n

                def p4_ctx(t, t_i=t_i, h=h, idx=idx, w=SS.n):
                    t.wait_ge(SS.h, w)
                    if idx >= 2:
                        t.wait_ge(SS.h, ssC[idx - 2])
                    for j in range(3):
                        ins = t.matmul(psC[idx % 2][:, 0:128],
                                       pTb[:, idx % 2, j, :],
                                       v_sb[:, t_i + j, h * 128:(h + 1) * 128],
                                       start=(j == 0), stop=(j == 2))
                    TS.inc(ins)
                sec("tensor", p4_ctx)
                tsC[idx] = TS.n

                def p4_ce(sc, t_i=t_i, h=h, idx=idx, w=TS.n, wv=vsR[idx]):
                    sc.wait_ge(TS.h, w)
                    sc.wait_ge(VS.h, wv)
                    if h == 0 and t_i >= 2:
                        sc.wait_ge(TS.h, tsXT[t_i - 2])
                    ins = sc.activation(ctok[:, t_i % 2, h * 128:(h + 1) * 128],
                                        psC[idx % 2][:, 0:128], AF.Copy,
                                        scale=small[:, 34 + idx % 2:35 + idx % 2])
                    SS.inc(ins)
                sec("scalar", p4_ce)
                ssC[idx] = SS.n

            def p4_xt(t, t_i=t_i, w=SS.n):
                t.wait_ge(SS.h, w)
                if t_i >= 2:
                    t.wait_ge(SS.h, ssXT[t_i - 2])
                for d in range(ND):
                    bank = psX[0] if d < 8 else psX[1]
                    ins = t.transpose(bank[:, (d % 8) * 128:(d % 8 + 1) * 128],
                                      ctok[:, t_i % 2, d * 128:(d + 1) * 128], eyeb[:, :])
                TS.inc(ins)
            sec("tensor", p4_xt)
            tsXT[t_i] = TS.n

            def p4_xte(sc, t_i=t_i, w=TS.n):
                sc.wait_ge(TS.h, w)
                sc.copy(ctxT[:, 0:8, t_i * 128:(t_i + 1) * 128],
                        psX[0][:, :].rearrange("p (a b) -> p a b", a=8))
                ins = sc.copy(ctxT[:, 8:16, t_i * 128:(t_i + 1) * 128],
                              psX[1][:, :].rearrange("p (a b) -> p a b", a=8))
                SS.inc(ins)
            sec("scalar", p4_xte)
            ssXT[t_i] = SS.n

        p4.close()
        sc_v.close()
        sc_q.close()
        sc_k.close()

        p5 = ExitStack()
        woT = p5.enter_context(nc.sbuf_tensor("woT", [128, ND, D], BF16))
        xt5 = p5.enter_context(nc.sbuf_tensor("xt5", [128, 2, D], F32))
        gsc = p5.enter_context(nc.sbuf_tensor("gsc", [128, 512], F32))
        ps5 = [p5.enter_context(nc.psum_tensor(f"p5_{j}", [128, 512], F32)) for j in range(2)]

        def p5_g(g):
            g.wait_ge(TS.h, tsXT[NT - 1])
            g.dma_start(out=woT[:, :, :],
                        in_=wo_t.rearrange("(kt p) n -> p kt n", p=128)).then_inc(QW.h, 16)
            QW.n += 16
        sec("gpsimd", p5_g)
        qw_wo = QW.n

        qx_x5, vs_ol, qs_x1, ss_byp = {}, {}, {}, {}
        for t_i in range(NT):
            def p5_x(sy, t_i=t_i):
                if t_i == 0:
                    sy.wait_ge(TS.h, tsXT[NT - 1])
                if t_i >= 2:
                    sy.wait_ge(QS.h, qs_x1[t_i - 2])
                sy.dma_start(out=xt5[:, t_i % 2, :],
                             in_=x_ext[256 + t_i * 128:256 + (t_i + 1) * 128, :]
                             ).then_inc(QX.h, 16)
                QX.n += 16
            sec("sync", p5_x)
            qx_x5[t_i] = QX.n

            for c in range(4):
                idx = t_i * 4 + c

                def p5_t(t, t_i=t_i, c=c, idx=idx):
                    if idx == 0:
                        t.wait_ge(QW.h, qw_wo)
                        t.wait_ge(SS.h, ssXT[NT - 1])
                    if idx >= 2:
                        t.wait_ge(VS.h, vs_ol[idx - 2])
                    for d in range(ND):
                        ins = t.matmul(ps5[idx % 2][:, :],
                                       ctxT[:, d, t_i * 128:(t_i + 1) * 128],
                                       woT[:, d, c * 512:(c + 1) * 512],
                                       start=(d == 0), stop=(d == ND - 1))
                    TS.inc(ins)
                sec("tensor", p5_t)
                w_ol = TS.n

                def p5_v(v, t_i=t_i, c=c, idx=idx, w=w_ol):
                    v.wait_ge(TS.h, w)
                    if c == 0:
                        v.wait_ge(QX.h, qx_x5[t_i])
                        if t_i == 0:
                            v.wait_ge(VS.h, vs_gb)
                    v.scalar_tensor_tensor(gsc[:, :], ps5[idx % 2][:, :], 1.0,
                                           gwb[:, c * 512:(c + 1) * 512],
                                           op0=ALU.mult, op1=ALU.mult,
                                           accum_out=small[:, 26 + c:27 + c])
                    ins = v.tensor_tensor(xt5[:, t_i % 2, c * 512:(c + 1) * 512],
                                          xt5[:, t_i % 2, c * 512:(c + 1) * 512],
                                          ps5[idx % 2][:, :], op=ALU.add)
                    VS.inc(ins)
                sec("vector", p5_v)
                vs_ol[idx] = VS.n

            def p5_vg(v, t_i=t_i):
                ins = v.reduce_sum(small[:, 36:37], small[:, 26:30], axis=AX.X)
                VS.inc(ins)
            sec("vector", p5_vg)
            vs_gp = VS.n

            def p5_sg(sc, t_i=t_i, w=vs_gp):
                sc.wait_ge(VS.h, w)
                ins = sc.activation(small[:, 18 + t_i:19 + t_i], small[:, 36:37],
                                    AF.Sigmoid, bias=small[:, 1:2], scale=1.0)
                SS.inc(ins)
            sec("scalar", p5_sg)
            ss_byp[t_i] = SS.n

            def p5_st(sy, t_i=t_i, w=vs_gp):
                sy.wait_ge(VS.h, w)
                sy.dma_start(out=x1_hbm[t_i * 128:(t_i + 1) * 128, :],
                             in_=xt5[:, t_i % 2, :]).then_inc(QS.h, 16)
                QS.n += 16
            sec("sync", p5_st)
            qs_x1[t_i] = QS.n

        ts_p5_last = TS.n
        p5.close()
        sc_ctx.close()

        sc_r = ExitStack()
        rT = sc_r.enter_context(nc.sbuf_tensor("rT", [128, ND, TOK], BF16, side="right"))

        p6a = ExitStack()
        mbk = p6a.enter_context(nc.sbuf_tensor("mbk", [128, 32, D], BF16))
        simt = p6a.enter_context(nc.sbuf_tensor("simt", [128, 2048], F32))
        emsk = p6a.enter_context(nc.sbuf_tensor("emsk", [128, 2048], BF16))
        pmem = p6a.enter_context(nc.sbuf_tensor("pmem", [128, 2048], BF16))
        pTm8 = p6a.enter_context(nc.sbuf_tensor("pTm8", [128, 2, 1024], BF16))
        psT6 = [p6a.enter_context(nc.psum_tensor(f"p6t{j}", [128, 1024], BF16))
                for j in range(2)]
        psR = [p6a.enter_context(nc.psum_tensor(f"p6r{j}", [128, 512], F32))
               for j in range(4)]

        qw_mbk = {}

        def p6_g(g):
            g.wait_ge(TS.h, ts_p5_last)
            for q in range(4):
                g.dma_start(out=mbk[:, q * 8:(q + 1) * 8, :],
                            in_=memb[q * 1024:(q + 1) * 1024, :]
                            .rearrange("(s p) d -> p s d", p=128)).then_inc(QW.h, 16)
                QW.n += 16
                qw_mbk[q] = QW.n
        sec("gpsimd", p6_g)

        vs_p6, ss_p6, ts_r6, ss_r6 = {}, {}, {}, {}
        ts_mm, vs_te6, ts_mm6 = {}, {}, {}
        for t_i in range(NT):
            def p6_seed(t, t_i=t_i):
                if t_i == 0:
                    t.wait_ge(QW.h, qw_mbk[3])
                    t.wait_ge(VS.h, vs_gw)
                if t_i >= 1:
                    t.wait_ge(SS.h, ss_r6[t_i - 1])
                for bank in range(4):
                    ins = t.matmul(psR[bank][:, :], zrow[0:1, 0:128], zrow[:, :],
                                   start=True, stop=True)
                TS.inc(ins)
            sec("tensor", p6_seed)

            for hf in range(2):
                hx = t_i * 2 + hf

                def p6_ld(sy, t_i=t_i, hf=hf, hx=hx):
                    if hx == 0:
                        sy.wait_ge(QS.h, qs_x1[NT - 1])
                    if hx >= 1:
                        sy.wait_ge(VS.h, vs_p6[hx - 1])
                        sy.wait_ge(SS.h, ss_p6[hx - 1])
                    sy.dma_start(out=simt[:, :],
                                 in_=sim_hbm[t_i, :, hf * 2048:(hf + 1) * 2048]
                                 ).then_inc(QX.h, 16)
                    QX.n += 16
                sec("sync", p6_ld)
                w_ld6 = QX.n

                def p6_msk(v, t_i=t_i, w=w_ld6):
                    v.wait_ge(QX.h, w)
                    ins = v.tensor_scalar(emsk[:, :], simt[:, :],
                                          top8f[:, t_i, 7:8], None, op0=ALU.is_ge)
                    VS.inc(ins)
                sec("vector", p6_msk)

                def p6_exp(sc, t_i=t_i, hx=hx, w=w_ld6):
                    sc.wait_ge(QX.h, w)
                    if hx >= 1:
                        sc.wait_ge(TS.h, ts_mm[hx - 1])
                    ins = sc.activation(pmem[:, :], simt[:, :], AF.Exp,
                                        bias=small[:, 2 + t_i:3 + t_i], scale=1.0)
                    SS.inc(ins)
                sec("scalar", p6_exp)
                ss_p6[hx] = SS.n

                def p6_p(v, t_i=t_i, w=SS.n):
                    v.wait_ge(SS.h, w)
                    ins = v.scalar_tensor_tensor(pmem[:, :], pmem[:, :],
                                                 small[:, 10 + t_i:11 + t_i], emsk[:, :],
                                                 op0=ALU.mult, op1=ALU.mult)
                    VS.inc(ins)
                sec("vector", p6_p)
                vs_p6[hx] = VS.n

                for gg in range(2):
                    g6 = hx * 2 + gg
                    pp = g6 % 2

                    def p6_tr(t, gg=gg, g6=g6, pp=pp, w=vs_p6[hx]):
                        if gg == 0:
                            t.wait_ge(VS.h, w)
                        if g6 >= 2:
                            t.wait_ge(VS.h, vs_te6[g6 - 2])
                        for k in range(8):
                            ins = t.transpose(psT6[pp][:, k * 128:(k + 1) * 128],
                                              pmem[:, (gg * 8 + k) * 128:
                                                   (gg * 8 + k + 1) * 128], eyeb[:, :])
                        TS.inc(ins)
                    sec("tensor", p6_tr)
                    w_tr6 = TS.n

                    def p6_te(v, g6=g6, pp=pp, w=w_tr6):
                        v.wait_ge(TS.h, w)
                        if g6 >= 2:
                            v.wait_ge(TS.h, ts_mm6[g6 - 2])
                        ins = v.tensor_copy(pTm8[:, pp, :], psT6[pp][:, :])
                        VS.inc(ins)
                    sec("vector", p6_te)
                    vs_te6[g6] = VS.n

                    def p6_mm(t, hf=hf, gg=gg, pp=pp, w=VS.n):
                        t.wait_ge(VS.h, w)
                        for k in range(8):
                            sk2 = hf * 16 + gg * 8 + k
                            for bank in range(4):
                                for dd in range(4):
                                    d = bank * 4 + dd
                                    ins = t.matmul(psR[bank][:, dd * 128:(dd + 1) * 128],
                                                   mbk[:, sk2, d * 128:(d + 1) * 128],
                                                   pTm8[:, pp, k * 128:(k + 1) * 128],
                                                   start=False, stop=(sk2 == 31),
                                                   skip_group_check=True)
                        TS.inc(ins)
                    sec("tensor", p6_mm)
                    ts_mm6[g6] = TS.n
                ts_mm[hx] = TS.n
            ts_r6[t_i] = TS.n

            def p6_re(sc, t_i=t_i, w=TS.n):
                sc.wait_ge(TS.h, w)
                for bank in range(4):
                    ins = sc.copy(rT[:, bank * 4:(bank + 1) * 4, t_i * 128:(t_i + 1) * 128],
                                  psR[bank][:, :].rearrange("p (a b) -> p a b", a=4))
                SS.inc(ins, 4)
            sec("scalar", p6_re)
            ss_r6[t_i] = SS.n

        p6a.close()

        sc_om = ExitStack()
        omT = sc_om.enter_context(nc.sbuf_tensor("omT", [128, ND, TOK], BF16))

        p6b = ExitStack()
        wvc2 = p6b.enter_context(nc.sbuf_tensor("wvc2", [128, 2, ND, 128], BF16))
        ps6b = [p6b.enter_context(nc.psum_tensor(f"p6b{j}", [128, 512], F32))
                for j in range(2)]

        qw_wv2, ts_om, ss_om = {}, {}, {}
        for m in range(ND):
            def p6b_g(g, m=m):
                if m == 0:
                    g.wait_ge(TS.h, ts_r6[NT - 1])
                if m >= 2:
                    g.wait_ge(TS.h, ts_om[m - 2])
                g.dma_start(out=wvc2[:, m % 2, :, :],
                            in_=wv_t[:, m * 128:(m + 1) * 128]
                            .rearrange("(kt p) n -> p kt n", p=128)).then_inc(QW.h, 16)
                QW.n += 16
            sec("gpsimd", p6b_g)
            qw_wv2[m] = QW.n

            def p6b_t(t, m=m):
                t.wait_ge(QW.h, qw_wv2[m])
                if m == 0:
                    t.wait_ge(SS.h, ss_r6[NT - 1])
                for c in range(2):
                    idx = m * 2 + c
                    if idx >= 2:
                        t.wait_ge(SS.h, ss_om[idx - 2])
                    for d in range(ND):
                        ins = t.matmul(ps6b[c][:, :],
                                       wvc2[:, m % 2, d, :],
                                       rT[:, d, c * 512:(c + 1) * 512],
                                       start=(d == 0), stop=(d == ND - 1))
                    TS.inc(ins)
            sec("tensor", p6b_t)
            ts_om[m] = TS.n

            def p6b_s(sc, m=m, w=TS.n):
                sc.wait_ge(TS.h, w)
                for c in range(2):
                    ins = sc.copy(omT[:, m, c * 512:(c + 1) * 512], ps6b[c][:, :])
                    SS.inc(ins)
            sec("scalar", p6b_s)
            ss_om[m * 2] = SS.n - 1
            ss_om[m * 2 + 1] = SS.n

        p6b.close()
        sc_r.close()

        sc_x2 = ExitStack()
        xhat2T = sc_x2.enter_context(nc.sbuf_tensor("xhat2T", [128, ND, TOK], BF16, side="right"))

        p6c = ExitStack()
        woT2 = p6c.enter_context(nc.sbuf_tensor("woT2", [128, ND, D], BF16))
        xt6 = p6c.enter_context(nc.sbuf_tensor("xt6", [128, 2, D], F32))
        xh6 = p6c.enter_context(nc.sbuf_tensor("xh6", [128, 2, D], BF16))
        sqs6 = p6c.enter_context(nc.sbuf_tensor("sqs6", [128, 2, D], F32))
        ps6c = [p6c.enter_context(nc.psum_tensor(f"p6c{j}", [128, 512], F32))
                for j in range(2)]
        psX6 = [p6c.enter_context(nc.psum_tensor(f"p6x{j}", [128, 1024], BF16))
                for j in range(2)]
        psL = p6c.enter_context(nc.psum_tensor("p6l", [128, 512], F32))

        def p6c_g(g):
            g.wait_ge(TS.h, ts_om[ND - 1])
            g.dma_start(out=woT2[:, :, :],
                        in_=wo_t.rearrange("(kt p) n -> p kt n", p=128)).then_inc(QW.h, 16)
            QW.n += 16
        sec("gpsimd", p6c_g)
        qw_wo2 = QW.n

        qx_x6, vs_om6, qs_x1f = {}, {}, {}
        vs_l3d, ts_xh2, ss_xh2, ss_lg = {}, {}, {}, {}
        for t_i in range(NT):
            def p6c_x(sy, t_i=t_i):
                if t_i == 0:
                    sy.wait_ge(TS.h, ts_om[ND - 1])
                if t_i >= 2:
                    sy.wait_ge(QS.h, qs_x1f[t_i - 2])
                    sy.wait_ge(VS.h, vs_l3d[t_i - 2])
                sy.dma_start(out=xt6[:, t_i % 2, :],
                             in_=x1_hbm[t_i * 128:(t_i + 1) * 128, :]).then_inc(QX.h, 16)
                QX.n += 16
            sec("sync", p6c_x)
            qx_x6[t_i] = QX.n

            for c in range(4):
                idx = t_i * 4 + c

                def p6c_t(t, t_i=t_i, c=c, idx=idx):
                    if idx == 0:
                        t.wait_ge(QW.h, qw_wo2)
                        t.wait_ge(SS.h, ss_om[2 * ND - 1])
                    if idx >= 2:
                        t.wait_ge(VS.h, vs_om6[idx - 2])
                    for d in range(ND):
                        ins = t.matmul(ps6c[idx % 2][:, :],
                                       omT[:, d, t_i * 128:(t_i + 1) * 128],
                                       woT2[:, d, c * 512:(c + 1) * 512],
                                       start=(d == 0), stop=(d == ND - 1))
                    TS.inc(ins)
                sec("tensor", p6c_t)
                w_om = TS.n

                def p6c_v(v, t_i=t_i, c=c, idx=idx, w=w_om):
                    v.wait_ge(TS.h, w)
                    if c == 0:
                        v.wait_ge(QX.h, qx_x6[t_i])
                        if t_i == 0:
                            v.wait_ge(SS.h, ss_byp[NT - 1])
                    ins = v.scalar_tensor_tensor(xt6[:, t_i % 2, c * 512:(c + 1) * 512],
                                                 ps6c[idx % 2][:, :],
                                                 small[:, 18 + t_i:19 + t_i],
                                                 xt6[:, t_i % 2, c * 512:(c + 1) * 512],
                                                 op0=ALU.mult, op1=ALU.add)
                    VS.inc(ins)
                sec("vector", p6c_v)
                vs_om6[idx] = VS.n

            def p6c_st(sy, t_i=t_i, w=VS.n):
                sy.wait_ge(VS.h, w)
                sy.dma_start(out=x1_hbm[t_i * 128:(t_i + 1) * 128, :],
                             in_=xt6[:, t_i % 2, :]).then_inc(QS.h, 16)
                QS.n += 16
            sec("sync", p6c_st)
            qs_x1f[t_i] = QS.n

            def p7_v1(v, t_i=t_i):
                ins = v.reduce_sum(lnst[:, t_i, 0:1], xt6[:, t_i % 2, :], axis=AX.X)
                VS.inc(ins)
            sec("vector", p7_v1)
            w_l1 = VS.n

            def p7_s1(sc, t_i=t_i, w=w_l1):
                sc.wait_ge(VS.h, w)
                ins = sc.activation(sqs6[:, t_i % 2, :], xt6[:, t_i % 2, :], AF.Square,
                                    accum_out=lnst[:, t_i, 1:2])
                SS.inc(ins)
            sec("scalar", p7_s1)
            w_s1 = SS.n

            def p7_v2(v, t_i=t_i, w=w_s1):
                v.wait_ge(SS.h, w)
                v.tensor_scalar(lnst[:, t_i, 2:3], lnst[:, t_i, 0:1], 1.0 / D, None,
                                op0=ALU.mult)
                v.tensor_tensor(lnst[:, t_i, 3:4], lnst[:, t_i, 2:3], lnst[:, t_i, 2:3],
                                op=ALU.mult)
                ins = v.scalar_tensor_tensor(lnst[:, t_i, 4:5], lnst[:, t_i, 1:2], 1.0 / D,
                                             lnst[:, t_i, 3:4], op0=ALU.mult,
                                             op1=ALU.subtract)
                VS.inc(ins)
            sec("vector", p7_v2)
            w_v2 = VS.n

            def p7_s2(sc, t_i=t_i, w=w_v2):
                sc.wait_ge(VS.h, w)
                ins = sc.activation(lnst[:, t_i, 5:6], lnst[:, t_i, 4:5], AF.Sqrt,
                                    bias=small[:, 0:1], scale=1.0)
                SS.inc(ins)
            sec("scalar", p7_s2)
            w_s2 = SS.n

            def p7_v3(v, t_i=t_i, w=w_s2):
                v.wait_ge(SS.h, w)
                if t_i >= 2:
                    v.wait_ge(TS.h, ts_xh2[t_i - 2])
                v.reciprocal(lnst[:, t_i, 6:7], lnst[:, t_i, 5:6])
                v.tensor_scalar(small[:, 38:39], lnst[:, t_i, 6:7], 1.0, None, op0=ALU.mult)
                ins = v.tensor_scalar(xh6[:, t_i % 2, :], xt6[:, t_i % 2, :],
                                      lnst[:, t_i, 2:3], lnst[:, t_i, 6:7],
                                      op0=ALU.subtract, op1=ALU.mult)
                VS.inc(ins)
            sec("vector", p7_v3)
            vs_l3d[t_i] = VS.n

            def p7_t(t, t_i=t_i, w=VS.n):
                t.wait_ge(VS.h, w)
                if t_i >= 2:
                    t.wait_ge(SS.h, ss_xh2[t_i - 2])
                for d in range(ND):
                    bank = psX6[0] if d < 8 else psX6[1]
                    ins = t.transpose(bank[:, (d % 8) * 128:(d % 8 + 1) * 128],
                                      xh6[:, t_i % 2, d * 128:(d + 1) * 128], eyeb[:, :])
                TS.inc(ins)
            sec("tensor", p7_t)
            ts_xh2[t_i] = TS.n

            def p7_se(sc, t_i=t_i, w=TS.n):
                sc.wait_ge(TS.h, w)
                sc.copy(xhat2T[:, 0:8, t_i * 128:(t_i + 1) * 128],
                        psX6[0][:, :].rearrange("p (a b) -> p a b", a=8))
                ins = sc.copy(xhat2T[:, 8:16, t_i * 128:(t_i + 1) * 128],
                              psX6[1][:, :].rearrange("p (a b) -> p a b", a=8))
                SS.inc(ins)
            sec("scalar", p7_se)
            ss_xh2[t_i] = SS.n

            def p7_rt(t, t_i=t_i, w=SS.n):
                t.wait_ge(SS.h, w)
                if t_i >= 1:
                    t.wait_ge(SS.h, ss_lg[t_i - 1])
                for d in range(ND):
                    t.matmul(psL[:, 0:NE], xhat2T[:, d, t_i * 128:(t_i + 1) * 128],
                             wrT[:, d, :], start=(d == 0), stop=False)
                ins = t.matmul(psL[:, 0:NE], onesb[:, :], fbias_b[:, :],
                               start=False, stop=True)
                TS.inc(ins)
            sec("tensor", p7_rt)
            w_rt = TS.n

            def p7_le(sc, t_i=t_i, w=w_rt):
                sc.wait_ge(TS.h, w)
                ins = sc.copy(logit_sb[:, t_i, :], psL[:, 0:NE])
                SS.inc(ins)
            sec("scalar", p7_le)
            ss_lg[t_i] = SS.n

        p6c.close()

        def p7_r1(v):
            v.wait_ge(SS.h, ss_lg[NT - 1])
            for t_i in range(NT):
                ins = v.max(out=top8f[:, t_i, :], in_=logit_sb[:, t_i, :])
            VS.inc(ins)
        sec("vector", p7_r1)
        vs_r1 = VS.n

        def p7_r2(sc):
            sc.wait_ge(VS.h, vs_r1)
            for t_i in range(NT):
                sc.activation(rst[:, t_i, 0:1], top8f[:, t_i, 0:1], AF.Identity, scale=-1.0)
            for t_i in range(NT):
                sc.activation(rst[:, t_i, 1:2], top8f[:, t_i, 1:2], AF.Exp,
                              bias=rst[:, t_i, 0:1], scale=1.0)
            for t_i in range(NT):
                ins = sc.activation(lexp[:, t_i, :], logit_sb[:, t_i, :], AF.Exp,
                                    bias=rst[:, t_i, 0:1], scale=1.0,
                                    accum_out=rst[:, t_i, 2:3])
            SS.inc(ins)
        sec("scalar", p7_r2)
        ss_r2 = SS.n

        def p7_r3(v):
            v.wait_ge(SS.h, ss_r2)
            for t_i in range(NT):
                v.scalar_tensor_tensor(rst[:, t_i, 3:4], rst[:, t_i, 2:3], EPS_RT,
                                       rst[:, t_i, 1:2], op0=ALU.mult, op1=ALU.add)
            for t_i in range(NT):
                v.tensor_scalar(rst[:, t_i, 3:4], rst[:, t_i, 3:4], 1.0, None, op0=ALU.add)
            for t_i in range(NT):
                v.reciprocal(rst[:, t_i, 4:5], rst[:, t_i, 3:4])
            for t_i in range(NT):
                ins = v.tensor_tensor(rst[:, t_i, 5:6], rst[:, t_i, 1:2], rst[:, t_i, 4:5],
                                      op=ALU.mult)
            VS.inc(ins)
            for t_i in range(NT):
                v.tensor_scalar(msk1[:, t_i, :], logit_sb[:, t_i, :],
                                top8f[:, t_i, 0:1], None, op0=ALU.is_ge)
            for t_i in range(NT):
                v.tensor_scalar(msk2[:, t_i, :], logit_sb[:, t_i, :],
                                top8f[:, t_i, 1:2], None, op0=ALU.is_ge)
            for t_i in range(NT):
                v.tensor_tensor(msk2[:, t_i, :], msk2[:, t_i, :], msk1[:, t_i, :],
                                op=ALU.subtract)
            for t_i in range(NT):
                v.tensor_scalar(msk1[:, t_i, :], msk1[:, t_i, :], rst[:, t_i, 4:5],
                                None, op0=ALU.mult)
            for t_i in range(NT):
                v.tensor_scalar(msk2[:, t_i, :], msk2[:, t_i, :], rst[:, t_i, 5:6],
                                None, op0=ALU.mult)
            for t_i in range(NT):
                ins = v.tensor_tensor(wmat_tok[:, t_i, :], msk1[:, t_i, :],
                                      msk2[:, t_i, :], op=ALU.add)
            VS.inc(ins)
        sec("vector", p7_r3)
        vs_wm = VS.n

        p_w = ExitStack()
        pwt = p_w.enter_context(nc.psum_tensor("pwt", [128, 1024], BF16))

        def p7_wt(t):
            t.wait_ge(VS.h, vs_wm)
            for t_i in range(NT):
                ins = t.transpose(pwt[0:NE, t_i * 128:(t_i + 1) * 128],
                                  wmat_tok[:, t_i, :], eyeb[:, :])
            TS.inc(ins)
        sec("tensor", p7_wt)
        ts_wt = TS.n

        def p7_we(sc):
            sc.wait_ge(TS.h, ts_wt)
            ins = sc.copy(WmatT[:, :], pwt[0:NE, :])
            SS.inc(ins)
        sec("scalar", p7_we)
        ss_wm = SS.n
        p_w.close()
        sc_om.close()

        sc_acc = ExitStack()
        acc = sc_acc.enter_context(nc.sbuf_tensor("acc", [128, ND, TOK], F32))

        p8 = ExitStack()
        wsl = p8.enter_context(nc.sbuf_tensor("wsl", [128, 4, ND * 512], BF16))
        z_sb = p8.enter_context(nc.sbuf_tensor("z_sb", [128, NI, TOK], BF16))
        gs_sb = p8.enter_context(nc.sbuf_tensor("gs_sb", [128, 2, TOK], BF16))
        wb_sb = p8.enter_context(nc.sbuf_tensor("wb_sb", [128, TOK], BF16))
        tmp8 = p8.enter_context(nc.sbuf_tensor("tmp8", [128, 512], BF16))
        ps8 = [p8.enter_context(nc.psum_tensor(f"p8_{j}", [128, 512], F32))
               for j in range(8)]

        def wsl_gu(k, d, itl):
            return wsl[:, k % 4, d * 512 + itl * 128:d * 512 + itl * 128 + 128]

        def wsl_wd(k, itl, col):
            return wsl[:, k % 4, itl * 2048 + col:itl * 2048 + col + 128]

        slot_seq = []
        for e in range(NE + 1):
            for kind in ("g", "u", "d"):
                for hf in range(2):
                    slot_seq.append((e, kind, hf))

        qw_slot = {}
        ts_slot_done = {}

        def emit_slot_load(k):
            e, kind, hf = slot_seq[k]

            def _g(g, k=k, e=e, kind=kind, hf=hf):
                if k == 0:
                    g.wait_ge(SS.h, ss_wm)
                if k >= 4:
                    g.wait_ge(TS.h, ts_slot_done[k - 4])
                if kind == "g":
                    ap = (sWg_d if e == NE else Wg_d[e, :, :])
                    ap = ap[:, hf * 512:(hf + 1) * 512].rearrange("(kt p) n -> p kt n", p=128)
                elif kind == "u":
                    ap = (sWu_d if e == NE else Wu_d[e, :, :])
                    ap = ap[:, hf * 512:(hf + 1) * 512].rearrange("(kt p) n -> p kt n", p=128)
                else:
                    ap = (sWd_d if e == NE else Wd_d[e, :, :])
                    ap = ap[hf * 512:(hf + 1) * 512, :].rearrange("(it p) h -> p it h", p=128)
                dst = wsl[:, k % 4, :].rearrange("p (a b) -> p a b",
                                                 a=ND if kind != "d" else 4)
                g.dma_start(out=dst, in_=ap).then_inc(QW.h, 16)
                QW.n += 16
            sec("gpsimd", _g)
            qw_slot[k] = QW.n

        for k in range(4):
            emit_slot_load(k)

        vs_z, vs_acc, ss_silu, ss_wbd = {}, {}, {}, {}
        qw_wrow, ts_wbd = {}, {}
        for e in range(NE + 1):
            base = e * 6
            shared = (e == NE)

            if not shared:
                def p8_wr(g, e=e):
                    if e == 0:
                        g.wait_ge(SS.h, ss_wm)
                    if e >= 2:
                        g.wait_ge(TS.h, ts_wbd[e - 2])
                    g.dma_start(out=wrow[0:1, e % 2, :],
                                in_=WmatT[e:e + 1, :]).then_inc(QW.h, 16)
                    QW.n += 16
                sec("gpsimd", p8_wr)
                qw_wrow[e] = QW.n

                def p8_wb(t, e=e):
                    t.wait_ge(QW.h, qw_wrow[e])
                    if e == 0:
                        t.wait_ge(SS.h, ss_wm)
                    else:
                        t.wait_ge(VS.h, vs_acc[(e - 1, 3)])
                    for c in range(2):
                        ins = t.matmul(ps8[6 + c][:, :], onesb[:, :],
                                       wrow[0:1, e % 2, c * 512:(c + 1) * 512],
                                       start=True, stop=True)
                    TS.inc(ins)
                sec("tensor", p8_wb)
                w_wb = TS.n
                ts_wbd[e] = TS.n

                def p8_wbe(sc, w=w_wb):
                    sc.wait_ge(TS.h, w)
                    for c in range(2):
                        ins = sc.copy(wb_sb[:, c * 512:(c + 1) * 512], ps8[6 + c][:, :])
                        SS.inc(ins)
                sec("scalar", p8_wbe)
            ss_wbd[e] = SS.n

            for it in range(NI):
                hf = it // 4
                par = it % 2
                gb0, ub0 = (0, 2) if par == 0 else (4, 6)
                kg = base + hf
                ku = base + 2 + hf

                def p8_gu(t, e=e, it=it, hf=hf, par=par, gb0=gb0, ub0=ub0,
                          kg=kg, ku=ku):
                    if it == 0 or it == 4:
                        t.wait_ge(QW.h, qw_slot[kg])
                        t.wait_ge(QW.h, qw_slot[ku])
                    if it == 0 and e == 0:
                        t.wait_ge(SS.h, ss_xh2[NT - 1])
                    if it <= 1 and e > 0:
                        t.wait_ge(VS.h, vs_acc[(e - 1, 3)])
                    if it == 1 and not (e == NE):
                        t.wait_ge(SS.h, ss_wbd[e])
                    if it >= 2:
                        t.wait_ge(SS.h, ss_silu[(e, it - 2)])
                        t.wait_ge(VS.h, vs_z[(e, it - 2)])
                    for th in range(2):
                        for d in range(ND):
                            t.matmul(ps8[gb0 + th][:, :],
                                     wsl_gu(kg, d, it % 4),
                                     xhat2T[:, d, th * 512:(th + 1) * 512],
                                     start=(d == 0), stop=(d == ND - 1))
                    for th in range(2):
                        for d in range(ND):
                            ins = t.matmul(ps8[ub0 + th][:, :],
                                           wsl_gu(ku, d, it % 4),
                                           xhat2T[:, d, th * 512:(th + 1) * 512],
                                           start=(d == 0), stop=(d == ND - 1))
                    TS.inc(ins)
                sec("tensor", p8_gu)
                w_gu = TS.n
                if it == 3:
                    ts_slot_done[base + 0] = TS.n
                    ts_slot_done[base + 2] = TS.n
                if it == 7:
                    ts_slot_done[base + 1] = TS.n
                    ts_slot_done[base + 3] = TS.n

                def p8_silu(sc, e=e, it=it, par=par, gb0=gb0, w=w_gu):
                    sc.wait_ge(TS.h, w)
                    for th in range(2):
                        ins = sc.activation(gs_sb[:, par, th * 512:(th + 1) * 512],
                                            ps8[gb0 + th][:, :], AF.Silu)
                    SS.inc(ins)
                sec("scalar", p8_silu)
                ss_silu[(e, it)] = SS.n

                def p8_z(v, e=e, it=it, par=par, ub0=ub0, shared=shared, w=SS.n):
                    v.wait_ge(SS.h, w)
                    for th in range(2):
                        if shared:
                            ins = v.tensor_tensor(z_sb[:, it, th * 512:(th + 1) * 512],
                                                  gs_sb[:, par, th * 512:(th + 1) * 512],
                                                  ps8[ub0 + th][:, :], op=ALU.mult)
                        else:
                            v.tensor_tensor(tmp8[:, :],
                                            gs_sb[:, par, th * 512:(th + 1) * 512],
                                            ps8[ub0 + th][:, :], op=ALU.mult)
                            ins = v.tensor_tensor(z_sb[:, it, th * 512:(th + 1) * 512],
                                                  tmp8[:, :],
                                                  wb_sb[:, th * 512:(th + 1) * 512],
                                                  op=ALU.mult)
                    VS.inc(ins)
                sec("vector", p8_z)
                vs_z[(e, it)] = VS.n

            kd1, kd2 = base + 4, base + 5
            emit_slot_load(kd1)
            emit_slot_load(kd2)

            for hq in range(4):
                def p8_d(t, e=e, hq=hq, kd1=kd1, kd2=kd2):
                    if hq == 0:
                        t.wait_ge(VS.h, vs_z[(e, NI - 1)])
                        t.wait_ge(QW.h, qw_slot[kd2])
                        if not (e == NE):
                            pass
                    if hq >= 1:
                        t.wait_ge(VS.h, vs_acc[(e, hq - 1)])
                    elif e > 0:
                        t.wait_ge(VS.h, vs_acc[(e - 1, 3)])
                    for ht in range(4):
                        for th in range(2):
                            for it in range(NI):
                                k = kd1 if it < 4 else kd2
                                ins = t.matmul(ps8[ht * 2 + th][:, :],
                                               wsl_wd(k, it % 4, hq * 512 + ht * 128),
                                               z_sb[:, it, th * 512:(th + 1) * 512],
                                               start=(it == 0), stop=(it == NI - 1))
                    TS.inc(ins)
                sec("tensor", p8_d)
                if hq == 3:
                    ts_slot_done[kd1] = TS.n
                    ts_slot_done[kd2] = TS.n

                def p8_acc(v, e=e, hq=hq, w=TS.n):
                    v.wait_ge(TS.h, w)
                    for ht in range(4):
                        for th in range(2):
                            dd = hq * 4 + ht
                            if e == 0:
                                ins = v.tensor_copy(acc[:, dd, th * 512:(th + 1) * 512],
                                                    ps8[ht * 2 + th][:, :])
                            else:
                                ins = v.tensor_tensor(acc[:, dd, th * 512:(th + 1) * 512],
                                                      acc[:, dd, th * 512:(th + 1) * 512],
                                                      ps8[ht * 2 + th][:, :], op=ALU.add)
                    VS.inc(ins)
                sec("vector", p8_acc)
                vs_acc[(e, hq)] = VS.n

            if e + 1 <= NE:
                nb = (e + 1) * 6
                for k in (nb, nb + 1, nb + 2, nb + 3):
                    emit_slot_load(k)

        p8.close()
        sc_x2.close()

        p9 = ExitStack()
        xt9 = p9.enter_context(nc.sbuf_tensor("xt9", [128, 2, D], F32))
        o9 = p9.enter_context(nc.sbuf_tensor("o9", [128, 2, D], F32))
        ps9 = [p9.enter_context(nc.psum_tensor(f"p9_{j}", [128, 512], F32))
               for j in range(8)]

        qx_x9, vs_o9, qs_out = {}, {}, {}
        for t_i in range(NT):
            def p9_x(sy, t_i=t_i):
                if t_i == 0:
                    sy.wait_ge(QS.h, qs_x1f[NT - 1])
                    sy.wait_ge(VS.h, vs_acc[(NE, 3)])
                if t_i >= 2:
                    sy.wait_ge(QS.h, qs_out[t_i - 2])
                sy.dma_start(out=xt9[:, t_i % 2, :],
                             in_=x1_hbm[t_i * 128:(t_i + 1) * 128, :]).then_inc(QX.h, 16)
                QX.n += 16
            sec("sync", p9_x)
            qx_x9[t_i] = QX.n

            def p9_t(t, t_i=t_i):
                if t_i == 0:
                    t.wait_ge(VS.h, vs_acc[(NE, 3)])
                if t_i >= 2:
                    t.wait_ge(VS.h, vs_o9[t_i - 2])
                for d in range(ND):
                    ins = t.transpose(ps9[4 * (t_i % 2) + d // 4]
                                      [:, (d % 4) * 128:(d % 4 + 1) * 128],
                                      acc[:, d, t_i * 128:(t_i + 1) * 128], eyef[:, :])
                TS.inc(ins)
            sec("tensor", p9_t)
            w_o9 = TS.n

            def p9_v(v, t_i=t_i, w=w_o9):
                v.wait_ge(TS.h, w)
                v.wait_ge(QX.h, qx_x9[t_i])
                for grp in range(4):
                    ins = v.tensor_tensor(o9[:, t_i % 2, grp * 512:(grp + 1) * 512],
                                          xt9[:, t_i % 2, grp * 512:(grp + 1) * 512],
                                          ps9[4 * (t_i % 2) + grp][:, :], op=ALU.add)
                VS.inc(ins)
            sec("vector", p9_v)
            vs_o9[t_i] = VS.n

            def p9_s(sy, t_i=t_i, w=VS.n):
                sy.wait_ge(VS.h, w)
                sy.dma_start(out=out_d[t_i * 128:(t_i + 1) * 128, :],
                             in_=o9[:, t_i % 2, :]).then_inc(QS.h, 16)
                QS.n += 16
            sec("sync", p9_s)
            qs_out[t_i] = QS.n

        p9.close()
        sc_acc.close()

        def p_final(sy):
            sy.wait_ge(QS.h, QS.n)
        sec("sync", p_final)

    return nc



_NC_CACHE = {}


def _get_nc():
    if "nc" not in _NC_CACHE:
        _NC_CACHE["nc"] = build_nc()
    return _NC_CACHE["nc"]


def kernel(L0_attn_hidden_states, L0_attn_wq, L0_attn_wk, L0_attn_wv, L0_attn_wo,
           L0_attn_gate_w, L0_attn_gate_b, L0_attn_mem_bank,
           L0_ffn_Wr, L0_ffn_bias, L0_ffn_Wg, L0_ffn_Wu, L0_ffn_Wd,
           L0_ffn_sWg, L0_ffn_sWu, L0_ffn_sWd):
    x = np.ascontiguousarray(np.asarray(L0_attn_hidden_states, dtype=np.float32))
    wq = np.asarray(L0_attn_wq, np.float32)
    wk = np.asarray(L0_attn_wk, np.float32)
    wv = np.asarray(L0_attn_wv, np.float32)
    wo = np.asarray(L0_attn_wo, np.float32)
    gw = np.ascontiguousarray(np.asarray(L0_attn_gate_w, np.float32).reshape(1, D))
    gb = np.ascontiguousarray(np.asarray(L0_attn_gate_b, np.float32).reshape(1, 1))
    mb = np.ascontiguousarray(np.asarray(L0_attn_mem_bank, np.float32))
    Wr = np.asarray(L0_ffn_Wr, np.float32)
    fb = np.ascontiguousarray(np.asarray(L0_ffn_bias, np.float32).reshape(1, NE))
    Wg = np.ascontiguousarray(np.asarray(L0_ffn_Wg, np.float32))
    Wu = np.ascontiguousarray(np.asarray(L0_ffn_Wu, np.float32))
    Wd = np.ascontiguousarray(np.asarray(L0_ffn_Wd, np.float32))
    sWg = np.ascontiguousarray(np.asarray(L0_ffn_sWg, np.float32))
    sWu = np.ascontiguousarray(np.asarray(L0_ffn_sWu, np.float32))
    sWd = np.ascontiguousarray(np.asarray(L0_ffn_sWd, np.float32))

    wq_t = np.ascontiguousarray(wq.T)
    wk_t = np.ascontiguousarray(wk.T)
    wv_t = np.ascontiguousarray(wv.T)
    wo_t = np.ascontiguousarray(wo.T)
    wr_t = np.ascontiguousarray(Wr.T)
    memb_t = np.ascontiguousarray(mb.T)

    eye = np.eye(128, dtype=np.float32)
    ones = np.ones((1, 128), np.float32)
    epsc = np.full((128, 1), EPS_LN, np.float32)

    i = np.arange(128)[:, None]
    j = np.arange(384)[None, :]
    band = (j > i) & (j <= i + 256)

    in_maps = []
    B, T = 2, 4096
    for core in range(8):
        b = core // 4
        t0 = (core % 4) * 1024
        lo = t0 - 256
        if lo < 0:
            xe = np.concatenate([np.zeros((-lo, D), np.float32),
                                 x[b, 0:t0 + 1024]], axis=0)
        else:
            xe = x[b, lo:t0 + 1024]
        xe = np.ascontiguousarray(xe)
        edge = (t0 == 0)
        mks = np.empty((3, 128, 384), np.float32)
        for mi in range(3):
            allowed = band.copy()
            if edge and mi < 2:
                allowed = allowed & (j >= (256 - mi * 128))
            mks[mi] = np.where(allowed, 0.0, -1e30)
        in_maps.append({
            "x_ext": xe, "wq_t": wq_t, "wk_t": wk_t, "wv_t": wv_t, "wo_t": wo_t,
            "wr_t": wr_t, "gate_w": gw, "gate_b": gb, "memb_t": memb_t, "memb": mb,
            "Wg_d": Wg, "Wu_d": Wu, "Wd_d": Wd, "sWg_d": sWg, "sWu_d": sWu,
            "sWd_d": sWd, "fbias": fb, "masks": np.ascontiguousarray(mks),
            "eye_f": eye, "ones_r": ones, "eps_c": epsc,
        })

    nc = _get_nc()
    res = run_bass_kernel_spmd(nc, in_maps, core_ids=list(range(8)))
    outs = [np.asarray(r["out"]) for r in res.results]
    full = np.empty((B, T, D), np.float32)
    for core in range(8):
        b = core // 4
        t0 = (core % 4) * 1024
        full[b, t0:t0 + 1024] = outs[core]
    return full


# version 07
# speedup vs baseline: 1.0014x; 1.0014x over previous
import numpy as np
from contextlib import ExitStack

import concourse.bass as bass
import concourse.mybir as mybir
from concourse.bass_utils import run_bass_kernel_spmd

F32 = mybir.dt.float32
BF16 = mybir.dt.bfloat16
AF = mybir.ActivationFunctionType
ALU = mybir.AluOpType
AX = mybir.AxisListType

TOK, NT = 1024, 8
EXT, NTE = 1280, 10
D, ND = 2048, 16
H = 16
MSLOT, NS = 4096, 8
NE, DFF, NI = 16, 1024, 8
EPS_LN = 1e-5
EPS_RT = 1e-9
ISQD = float(1.0 / np.sqrt(2048.0))
ISQH = float(1.0 / np.sqrt(128.0))


class Sem:
    def __init__(self, ctx, nc, name):
        self.h = ctx.enter_context(nc.semaphore(name))
        self.n = 0

    def inc(self, ins, k=1):
        ins.then_inc(self.h, k)
        self.n += k
        return self.n


def build_nc():
    nc = bass.Bass("TRN2", target_bir_lowering=False)

    x_ext = nc.dram_tensor("x_ext", [EXT, D], F32, kind="ExternalInput").ap()
    wq_t = nc.dram_tensor("wq_t", [D, D], F32, kind="ExternalInput").ap()
    wk_t = nc.dram_tensor("wk_t", [D, D], F32, kind="ExternalInput").ap()
    wk_d = nc.dram_tensor("wk_d", [D, D], F32, kind="ExternalInput").ap()
    wv_t = nc.dram_tensor("wv_t", [D, D], F32, kind="ExternalInput").ap()
    wo_t = nc.dram_tensor("wo_t", [D, D], F32, kind="ExternalInput").ap()
    wr_t = nc.dram_tensor("wr_t", [D, NE], F32, kind="ExternalInput").ap()
    gate_w = nc.dram_tensor("gate_w", [1, D], F32, kind="ExternalInput").ap()
    gate_b = nc.dram_tensor("gate_b", [1, 1], F32, kind="ExternalInput").ap()
    memb_t = nc.dram_tensor("memb_t", [D, MSLOT], F32, kind="ExternalInput").ap()
    memb = nc.dram_tensor("memb", [MSLOT, D], F32, kind="ExternalInput").ap()
    Wg_d = nc.dram_tensor("Wg_d", [NE, D, DFF], F32, kind="ExternalInput").ap()
    Wu_d = nc.dram_tensor("Wu_d", [NE, D, DFF], F32, kind="ExternalInput").ap()
    Wd_d = nc.dram_tensor("Wd_d", [NE, DFF, D], F32, kind="ExternalInput").ap()
    sWg_d = nc.dram_tensor("sWg_d", [D, DFF], F32, kind="ExternalInput").ap()
    sWu_d = nc.dram_tensor("sWu_d", [D, DFF], F32, kind="ExternalInput").ap()
    sWd_d = nc.dram_tensor("sWd_d", [DFF, D], F32, kind="ExternalInput").ap()
    fbias = nc.dram_tensor("fbias", [1, NE], F32, kind="ExternalInput").ap()
    masks = nc.dram_tensor("masks", [3, 128, 384], F32, kind="ExternalInput").ap()
    eye_f = nc.dram_tensor("eye_f", [128, 128], F32, kind="ExternalInput").ap()
    ones_r = nc.dram_tensor("ones_r", [1, 128], F32, kind="ExternalInput").ap()
    eps_c = nc.dram_tensor("eps_c", [128, 1], F32, kind="ExternalInput").ap()
    out_d = nc.dram_tensor("out", [TOK, D], F32, kind="ExternalOutput").ap()
    sim_hbm = nc.dram_tensor("sim_hbm", [NT, 128, MSLOT], BF16, kind="Internal").ap()
    x1_hbm = nc.dram_tensor("x1_hbm", [TOK, D], F32, kind="Internal").ap()

    ctx = ExitStack()
    with ctx:
        E = ctx.enter_context

        small = E(nc.sbuf_tensor("small", [128, 64], F32))
        top8f = E(nc.sbuf_tensor("top8f", [128, NT, 8], F32))
        e8 = E(nc.sbuf_tensor("e8", [128, NT, 8], F32))
        eyeb = E(nc.sbuf_tensor("eyeb", [128, 128], BF16))
        eyef = E(nc.sbuf_tensor("eyef", [128, 128], F32))
        onesb = E(nc.sbuf_tensor("onesb", [1, 128], BF16))
        maskb = E(nc.sbuf_tensor("maskb", [128, 3, 384], BF16))
        gwsrc = E(nc.sbuf_tensor("gwsrc", [1, D], BF16))
        gwb = E(nc.sbuf_tensor("gwb", [128, D], BF16))
        wrT = E(nc.sbuf_tensor("wrT", [128, ND, NE], BF16))
        fbias_b = E(nc.sbuf_tensor("fbias_b", [1, NE], BF16))
        gate_b_b = E(nc.sbuf_tensor("gate_b_b", [1, 1], BF16))
        WmatT = E(nc.sbuf_tensor("WmatT", [16, NT * 128], BF16))
        logit_sb = E(nc.sbuf_tensor("logit_sb", [128, NT, NE], F32))
        lexp = E(nc.sbuf_tensor("lexp", [128, NT, NE], F32))
        rst = E(nc.sbuf_tensor("rst", [128, NT, 8], F32))
        lnst = E(nc.sbuf_tensor("lnst", [128, NTE, 8], F32))
        wmat_tok = E(nc.sbuf_tensor("wmat_tok", [128, NT, NE], BF16))
        wrow = E(nc.sbuf_tensor("wrow", [1, 2, TOK], BF16))
        msk1 = E(nc.sbuf_tensor("msk1", [128, NT, NE], F32))
        msk2 = E(nc.sbuf_tensor("msk2", [128, NT, NE], F32))
        zrow = E(nc.sbuf_tensor("zrow", [1, 512], BF16))

        QX = Sem(ctx, nc, "QX")
        QW = Sem(ctx, nc, "QW")
        QS = Sem(ctx, nc, "QS")
        TS = Sem(ctx, nc, "TS")
        VS = Sem(ctx, nc, "VS")
        SS = Sem(ctx, nc, "SS")

        block = E(nc.Block())

        def sec(name, fn):
            getattr(block, name)(fn)

        sc_xhat = ExitStack()
        xhatT = sc_xhat.enter_context(nc.sbuf_tensor("xhatT", [128, ND, EXT], BF16, side="right"))

        p0 = ExitStack()
        xext = p0.enter_context(nc.sbuf_tensor("xext", [128, 5, D], F32))
        xhtok = p0.enter_context(nc.sbuf_tensor("xhtok", [128, 5, D], BF16))
        sqs = p0.enter_context(nc.sbuf_tensor("sqs", [128, 2, D], F32))
        ps_tr = [p0.enter_context(nc.psum_tensor(f"p0t{j}", [128, 1024], BF16))
                 for j in range(4)]
        ps_g = p0.enter_context(nc.psum_tensor("p0g", [128, 2048], F32))

        def p0_g(g):
            g.dma_start(out=eyeb[:, :], in_=eye_f[:, :]).then_inc(QW.h, 16)
            g.dma_start(out=onesb[:, :], in_=ones_r[:, :]).then_inc(QW.h, 16)
            g.dma_start(out=maskb[:, :, :],
                        in_=masks.rearrange("m p n -> p m n")).then_inc(QW.h, 16)
            g.dma_start(out=wrT[:, :, :],
                        in_=wr_t.rearrange("(kt p) n -> p kt n", p=128)).then_inc(QW.h, 16)
            g.dma_start(out=fbias_b[:, :], in_=fbias[:, :]).then_inc(QW.h, 16)
            g.dma_start(out=gate_b_b[:, :], in_=gate_b[:, :]).then_inc(QW.h, 16)
            g.dma_start(out=gwsrc[:, :], in_=gate_w[:, :]).then_inc(QW.h, 16)
            QW.n += 7 * 16
        sec("gpsimd", p0_g)
        qw_consts = QW.n

        def p0_s0(s):
            s.dma_start(out=eyef[:, :], in_=eye_f[:, :]).then_inc(QX.h, 16)
            s.dma_start(out=small[:, 0:1], in_=eps_c[:, :]).then_inc(QX.h, 16)
            QX.n += 2 * 16
        sec("sync", p0_s0)

        def p0_t(t):
            t.wait_ge(QW.h, qw_consts)
            for c in range(4):
                ins = t.matmul(ps_g[:, c * 512:(c + 1) * 512], onesb[:, :],
                               gwsrc[:, c * 512:(c + 1) * 512], start=True, stop=True)
            TS.inc(ins)
        sec("tensor", p0_t)
        ts_gw = TS.n

        def p0_v0(v):
            v.wait_ge(TS.h, ts_gw)
            v.memset(zrow[:, :], 0.0)
            ins = v.tensor_copy(gwb[:, :], ps_g[:, :])
            VS.inc(ins)
        sec("vector", p0_v0)
        vs_gw = VS.n

        def p0_t2(t):
            t.wait_ge(VS.h, vs_gw)
            ins = t.matmul(ps_g[:, 0:1], onesb[:, :], gate_b_b[:, :], start=True, stop=True)
            TS.inc(ins)
        sec("tensor", p0_t2)
        ts_gb = TS.n

        def p0_v0b(v):
            v.wait_ge(TS.h, ts_gb)
            ins = v.tensor_copy(small[:, 1:2], ps_g[:, 0:1])
            VS.inc(ins)
        sec("vector", p0_v0b)
        vs_gb = VS.n

        ss_tr = {}
        for hb in range(2):
            tl = [hb * 5 + u for u in range(5)]

            def p0_s(s, hb=hb):
                if hb == 1:
                    s.wait_ge(SS.h, ss_tr[4])
                s.dma_start(out=xext[:, :, :],
                            in_=x_ext[hb * 640:(hb + 1) * 640, :]
                            .rearrange("(t p) d -> p t d", p=128)).then_inc(QX.h, 16)
                QX.n += 16
            sec("sync", p0_s)
            qx_h = QX.n

            def p0_v1(v, hb=hb, w=qx_h):
                v.wait_ge(QX.h, w)
                for u in range(5):
                    ins = v.reduce_sum(lnst[:, hb * 5 + u, 0:1], xext[:, u, :], axis=AX.X)
                VS.inc(ins)
            sec("vector", p0_v1)

            def p0_s1(sc, hb=hb, w=qx_h):
                sc.wait_ge(QX.h, w)
                for u in range(5):
                    ins = sc.activation(sqs[:, u % 2, :], xext[:, u, :], AF.Square,
                                        accum_out=lnst[:, hb * 5 + u, 1:2])
                SS.inc(ins)
            sec("scalar", p0_s1)
            ss_sq1 = SS.n

            def p0_v2(v, hb=hb, w=ss_sq1):
                v.wait_ge(SS.h, w)
                for u in range(5):
                    i = hb * 5 + u
                    v.tensor_scalar(lnst[:, i, 2:3], lnst[:, i, 0:1], 1.0 / D, None,
                                    op0=ALU.mult)
                for u in range(5):
                    i = hb * 5 + u
                    v.tensor_tensor(lnst[:, i, 3:4], lnst[:, i, 2:3], lnst[:, i, 2:3],
                                    op=ALU.mult)
                for u in range(5):
                    i = hb * 5 + u
                    ins = v.scalar_tensor_tensor(lnst[:, i, 4:5], lnst[:, i, 1:2], 1.0 / D,
                                                 lnst[:, i, 3:4], op0=ALU.mult,
                                                 op1=ALU.subtract)
                VS.inc(ins)
            sec("vector", p0_v2)
            vs_var1 = VS.n

            def p0_s2(sc, hb=hb, w=vs_var1):
                sc.wait_ge(VS.h, w)
                for u in range(5):
                    i = hb * 5 + u
                    ins = sc.activation(lnst[:, i, 5:6], lnst[:, i, 4:5], AF.Sqrt,
                                        bias=small[:, 0:1], scale=1.0)
                SS.inc(ins)
            sec("scalar", p0_s2)
            ss_sd1 = SS.n

            def p0_v3(v, hb=hb, w=ss_sd1):
                v.wait_ge(SS.h, w)
                for u in range(5):
                    i = hb * 5 + u
                    v.reciprocal(lnst[:, i, 6:7], lnst[:, i, 5:6])
                for u in range(5):
                    i = hb * 5 + u
                    ins = v.tensor_scalar(xhtok[:, u, :], xext[:, u, :],
                                          lnst[:, i, 2:3], lnst[:, i, 6:7],
                                          op0=ALU.subtract, op1=ALU.mult)
                VS.inc(ins)
            sec("vector", p0_v3)
            vs_xh1 = VS.n

            for u in range(5):
                i = hb * 5 + u
                lo, hi = ps_tr[(2 * i) % 4], ps_tr[(2 * i + 1) % 4]

                def p0_ti(t, u=u, i=i, lo=lo, hi=hi, w=vs_xh1):
                    if u == 0:
                        t.wait_ge(VS.h, w)
                    if i >= 2:
                        t.wait_ge(SS.h, ss_tr[i - 2])
                    for d in range(ND):
                        bank = lo if d < 8 else hi
                        ins = t.transpose(bank[:, (d % 8) * 128:(d % 8 + 1) * 128],
                                          xhtok[:, u, d * 128:(d + 1) * 128], eyeb[:, :])
                    TS.inc(ins)
                sec("tensor", p0_ti)
                w_tr = TS.n

                def p0_si(sc, i=i, lo=lo, hi=hi, w=w_tr):
                    sc.wait_ge(TS.h, w)
                    sc.copy(xhatT[:, 0:8, i * 128:(i + 1) * 128],
                            lo[:, :].rearrange("p (a b) -> p a b", a=8))
                    ins = sc.copy(xhatT[:, 8:16, i * 128:(i + 1) * 128],
                                  hi[:, :].rearrange("p (a b) -> p a b", a=8))
                    SS.inc(ins)
                sec("scalar", p0_si)
                ss_tr[i] = SS.n
        ss_xhatT = SS.n

        p0.close()

        sc_k = ExitStack()
        kT = sc_k.enter_context(nc.sbuf_tensor("kT", [128, ND, EXT], BF16))
        sc_wqv = ExitStack()
        wqc = sc_wqv.enter_context(nc.sbuf_tensor("wqc", [128, 2, ND, 128], BF16,
                                                  side="right"))
        sc_p1 = ExitStack()
        top8p = sc_p1.enter_context(nc.sbuf_tensor("top8p", [128, NT, NS, 8], F32))
        mbt = sc_p1.enter_context(nc.sbuf_tensor("mbt", [128, 2, ND, 512], BF16))
        simf = sc_p1.enter_context(nc.sbuf_tensor("simf", [128, 4, 512], BF16))
        kqT = sc_p1.enter_context(nc.sbuf_tensor("kqT", [128, ND, TOK], BF16))
        ps1s = [sc_p1.enter_context(nc.psum_tensor(f"p1s{j}", [128, 512], F32))
                for j in range(4)]

        p1b = ExitStack()
        wkc = p1b.enter_context(nc.sbuf_tensor("wkc", [128, 2, ND, 128], BF16))
        wkqc = p1b.enter_context(nc.sbuf_tensor("wkqc", [128, 2, ND, 128], BF16))
        ps1b = [p1b.enter_context(nc.psum_tensor(f"p1b{j}", [128, 512], F32))
                for j in range(2)]

        qw_mbt = {}
        qw_wk2, ts_k2, vs_kev = {}, {}, {}
        for m in range(ND):
            def p1b_g(g, m=m):
                if m == 0:
                    g.wait_ge(SS.h, ss_xhatT)
                if m >= 2:
                    g.wait_ge(TS.h, ts_k2[m - 2])
                g.dma_start(out=wkc[:, m % 2, :, :],
                            in_=wk_t[:, m * 128:(m + 1) * 128]
                            .rearrange("(kt p) n -> p kt n", p=128)).then_inc(QW.h, 16)
                g.dma_start(out=wkqc[:, m % 2, :, :],
                            in_=wk_d[:, m * 128:(m + 1) * 128]
                            .rearrange("(kt p) n -> p kt n", p=128)).then_inc(QW.h, 16)
                QW.n += 32
            sec("gpsimd", p1b_g)
            qw_wk2[m] = QW.n

            for c in range(5):
                kq = (c >= 3)
                ncols = 256 if c == 2 else 512
                col0 = (c - 3) * 512 + 256 if kq else c * 512
                idx = m * 5 + c

                def p1b_t(t, m=m, c=c, kq=kq, ncols=ncols, col0=col0, idx=idx):
                    if c == 0:
                        t.wait_ge(QW.h, qw_wk2[m])
                    if idx == 0:
                        t.wait_ge(SS.h, ss_xhatT)
                        t.wait_ge(VS.h, vs_gb)
                    if idx >= 2:
                        t.wait_ge(VS.h, vs_kev[idx - 2])
                    wsrc = wkqc if kq else wkc
                    for d in range(ND):
                        ins = t.matmul(ps1b[idx % 2][:, 0:ncols],
                                       wsrc[:, m % 2, d, :],
                                       xhatT[:, d, col0:col0 + ncols],
                                       start=(d == 0), stop=(d == ND - 1))
                    TS.inc(ins)
                sec("tensor", p1b_t)
                ts_k2[m] = TS.n
                w_k = TS.n

                def p1b_v(v, m=m, c=c, kq=kq, ncols=ncols, idx=idx, w=w_k):
                    v.wait_ge(TS.h, w)
                    if kq:
                        dst = kqT[:, m, (c - 3) * 512:(c - 3) * 512 + ncols]
                    else:
                        dst = kT[:, m, c * 512:c * 512 + ncols]
                    ins = v.tensor_copy(dst, ps1b[idx % 2][:, 0:ncols])
                    VS.inc(ins)
                sec("vector", p1b_v)
                vs_kev[idx] = VS.n

        p1b.close()

        def p1_g2(g):
            g.dma_start(out=mbt[:, 0, :, :],
                        in_=memb_t[:, 0:512].rearrange("(kt p) n -> p kt n", p=128)
                        ).then_inc(QW.h, 16)
            QW.n += 16
        sec("gpsimd", p1_g2)
        qw_mbt[0] = QW.n

        ts_sim = {}
        ss_sevac = {}
        vs_sevac = {}
        qs_sim = {}

        for s in range(NS):
            if s + 1 < NS:
                def p1_gp(g, s=s):
                    if s >= 1:
                        g.wait_ge(TS.h, ts_sim[(s - 1) * NT + NT - 1])
                    g.dma_start(out=mbt[:, (s + 1) % 2, :, :],
                                in_=memb_t[:, (s + 1) * 512:(s + 2) * 512]
                                .rearrange("(kt p) n -> p kt n", p=128)).then_inc(QW.h, 16)
                    QW.n += 16
                sec("gpsimd", p1_gp)
                qw_mbt[s + 1] = QW.n

            for t_i in range(NT):
                idx = s * NT + t_i
                bank = ps1s[idx % 4]

                def p1_sm(t, s=s, t_i=t_i, bank=bank, idx=idx):
                    if t_i == 0:
                        t.wait_ge(QW.h, qw_mbt[s])
                    if idx == 0:
                        t.wait_ge(VS.h, vs_kev[ND * 5 - 1])
                    if idx >= 4:
                        t.wait_ge(SS.h, ss_sevac[idx - 4])
                    for d in range(ND):
                        ins = t.matmul(bank[:, :],
                                       kqT[:, d, t_i * 128:(t_i + 1) * 128],
                                       mbt[:, s % 2, d, :],
                                       start=(d == 0), stop=(d == ND - 1))
                    TS.inc(ins)
                sec("tensor", p1_sm)
                ts_sim[idx] = TS.n

                def p1_se(sc, bank=bank, idx=idx, w=TS.n):
                    sc.wait_ge(TS.h, w)
                    if idx >= 4:
                        sc.wait_ge(QS.h, qs_sim[idx - 4])
                        sc.wait_ge(VS.h, vs_sevac[idx - 4])
                    ins = sc.copy(simf[:, idx % 4, :], bank[:, :])
                    SS.inc(ins)
                sec("scalar", p1_se)
                ss_sevac[idx] = SS.n

                def p1_mx(v, s=s, t_i=t_i, idx=idx, w=SS.n):
                    v.wait_ge(SS.h, w)
                    ins = v.max(out=top8p[:, t_i, s, :], in_=simf[:, idx % 4, :])
                    VS.inc(ins)
                sec("vector", p1_mx)
                vs_sevac[idx] = VS.n

                def p1_st(sy, s=s, t_i=t_i, idx=idx, w=SS.n):
                    sy.wait_ge(SS.h, w)
                    sy.dma_start(out=sim_hbm[t_i, :, s * 512:(s + 1) * 512],
                                 in_=simf[:, idx % 4, :]).then_inc(QS.h, 16)
                    QS.n += 16
                sec("sync", p1_st)
                qs_sim[idx] = QS.n

        def p1_v4(v):
            for t_i in range(NT):
                ins = v.max(out=top8f[:, t_i, :], in_=top8p[:, t_i, :, :])
            VS.inc(ins)
        sec("vector", p1_v4)
        vs_t8 = VS.n

        def p1_s4(sc):
            sc.wait_ge(VS.h, vs_t8)
            for t_i in range(NT):
                sc.activation(small[:, 2 + t_i:3 + t_i], top8f[:, t_i, 0:1],
                              AF.Identity, scale=-ISQD)
            for t_i in range(NT):
                ins = sc.activation(e8[:, t_i, :], top8f[:, t_i, :], AF.Exp,
                                    bias=small[:, 2 + t_i:3 + t_i], scale=ISQD)
            SS.inc(ins)
        sec("scalar", p1_s4)
        ss_e8 = SS.n

        def p1_v5(v):
            v.wait_ge(SS.h, ss_e8)
            for t_i in range(NT):
                v.reduce_sum(rst[:, t_i, 7:8], e8[:, t_i, :], axis=AX.X)
            for t_i in range(NT):
                ins = v.reciprocal(small[:, 10 + t_i:11 + t_i], rst[:, t_i, 7:8])
            VS.inc(ins)
        sec("vector", p1_v5)
        vs_mden = VS.n

        sc_p1.close()

        sc_q = ExitStack()
        qT = sc_q.enter_context(nc.sbuf_tensor("qT", [128, ND, TOK], BF16))

        p2 = ExitStack()
        ps2 = [p2.enter_context(nc.psum_tensor(f"p2_{j}", [128, 512], F32)) for j in range(2)]

        qw_wq = {}
        ts_q = {}
        vs_qe = {}
        for m in range(ND):
            def p2_g(g, m=m):
                if m == 0:
                    g.wait_ge(SS.h, ss_xhatT)
                if m >= 2:
                    g.wait_ge(TS.h, ts_q[m - 2])
                g.dma_start(out=wqc[:, m % 2, :, :],
                            in_=wq_t[:, m * 128:(m + 1) * 128]
                            .rearrange("(kt p) n -> p kt n", p=128)).then_inc(QW.h, 16)
                QW.n += 16
            sec("gpsimd", p2_g)
            qw_wq[m] = QW.n

            def p2_t(t, m=m):
                t.wait_ge(QW.h, qw_wq[m])
                if m == 0:
                    t.wait_ge(SS.h, ss_sevac[NS * NT - 1])
                    t.wait_ge(VS.h, vs_sevac[NS * NT - 1])
                for c in range(2):
                    idx = m * 2 + c
                    if idx >= 2:
                        t.wait_ge(VS.h, vs_qe[idx - 2])
                    for d in range(ND):
                        ins = t.matmul(ps2[c][:, :],
                                       wqc[:, m % 2, d, :],
                                       xhatT[:, d, 256 + c * 512:256 + (c + 1) * 512],
                                       start=(d == 0), stop=(d == ND - 1))
                    TS.inc(ins)
            sec("tensor", p2_t)
            ts_q[m] = TS.n

            def p2_v(v, m=m, w=TS.n):
                v.wait_ge(TS.h, w)
                if m == 0:
                    v.wait_ge(QS.h, qs_sim[NS * NT - 1])
                for c in range(2):
                    ins = v.tensor_scalar(qT[:, m, c * 512:(c + 1) * 512], ps2[c][:, :],
                                          ISQH, None, op0=ALU.mult)
                    VS.inc(ins)
            sec("vector", p2_v)
            vs_qe[m * 2] = VS.n - 1
            vs_qe[m * 2 + 1] = VS.n

        p2.close()
        sc_wqv.close()

        sc_v = ExitStack()
        v_sb = sc_v.enter_context(nc.sbuf_tensor("v_sb", [128, NTE, D], BF16))

        p3 = ExitStack()
        wvc = p3.enter_context(nc.sbuf_tensor("wvc", [128, 2, ND, 512], BF16))
        ps3 = [p3.enter_context(nc.psum_tensor(f"p3_{j}", [128, 512], F32)) for j in range(2)]

        qw_wv = {}
        ts_v = {}
        vs_ve = {}
        for c in range(4):
            def p3_g(g, c=c):
                if c == 0:
                    g.wait_ge(TS.h, ts_q[ND - 1])
                    g.wait_ge(QS.h, qs_sim[NS * NT - 1])
                if c >= 2:
                    g.wait_ge(TS.h, ts_v[c - 2])
                g.dma_start(out=wvc[:, c % 2, :, :],
                            in_=wv_t[:, c * 512:(c + 1) * 512]
                            .rearrange("(kt p) n -> p kt n", p=128)).then_inc(QW.h, 16)
                QW.n += 16
            sec("gpsimd", p3_g)
            qw_wv[c] = QW.n

            for t_i in range(NTE):
                idx = c * NTE + t_i
                bank = ps3[idx % 2]

                def p3_t(t, c=c, t_i=t_i, bank=bank, idx=idx):
                    if t_i == 0:
                        t.wait_ge(QW.h, qw_wv[c])
                    if idx == 0:
                        t.wait_ge(VS.h, vs_qe[2 * ND - 1])
                    if idx >= 2:
                        t.wait_ge(VS.h, vs_ve[idx - 2])
                    for d in range(ND):
                        ins = t.matmul(bank[:, :],
                                       xhatT[:, d, t_i * 128:(t_i + 1) * 128],
                                       wvc[:, c % 2, d, :],
                                       start=(d == 0), stop=(d == ND - 1))
                    TS.inc(ins)
                sec("tensor", p3_t)
                ts_v[c] = TS.n

                def p3_v(v, c=c, t_i=t_i, bank=bank, w=TS.n):
                    v.wait_ge(TS.h, w)
                    ins = v.tensor_copy(v_sb[:, t_i, c * 512:(c + 1) * 512], bank[:, :])
                    VS.inc(ins)
                sec("vector", p3_v)
                vs_ve[idx] = VS.n

        p3.close()
        sc_wqv.close()
        sc_xhat.close()

        sc_ctx = ExitStack()
        ctxT = sc_ctx.enter_context(nc.sbuf_tensor("ctxT", [128, ND, TOK], BF16, side="right"))

        p4 = ExitStack()
        pbuf = p4.enter_context(nc.sbuf_tensor("pbuf", [128, 2, 384], BF16))
        pTb = p4.enter_context(nc.sbuf_tensor("pTb", [128, 2, 3, 128], BF16))
        ctok = p4.enter_context(nc.sbuf_tensor("ctok", [128, 2, D], BF16))
        psS = [p4.enter_context(nc.psum_tensor(f"p4s{j}", [128, 512], F32)) for j in range(2)]
        psP = [p4.enter_context(nc.psum_tensor(f"p4p{j}", [128, 768], BF16)) for j in range(2)]
        psC = [p4.enter_context(nc.psum_tensor(f"p4c{j}", [128, 512], F32)) for j in range(2)]
        psX = [p4.enter_context(nc.psum_tensor(f"p4x{j}", [128, 1024], BF16)) for j in range(2)]

        tsS, ssE, tsP, ssP, tsC, ssC, vsR = {}, {}, {}, {}, {}, {}, {}
        tsXT, ssXT = {}, {}
        for t_i in range(NT):
            for h in range(H):
                idx = t_i * H + h
                mi = min(t_i, 2)

                def p4_smm(t, t_i=t_i, h=h, idx=idx, mi=mi):
                    if idx == 0:
                        t.wait_ge(VS.h, vs_ve[4 * NTE - 1])
                    if idx >= 2:
                        t.wait_ge(SS.h, ssE[idx - 2])
                    t.matmul(psS[idx % 2][:, 0:384],
                             qT[:, h, t_i * 128:(t_i + 1) * 128],
                             kT[:, h, t_i * 128:t_i * 128 + 384],
                             start=True, stop=False)
                    ins = t.matmul(psS[idx % 2][:, 0:384], eyeb[:, :],
                                   maskb[:, mi, :], start=False, stop=True)
                    TS.inc(ins)
                sec("tensor", p4_smm)
                tsS[idx] = TS.n

                def p4_neg(v, idx=idx, w=TS.n):
                    v.wait_ge(TS.h, w)
                    if idx >= 2:
                        v.wait_ge(SS.h, ssE[idx - 2])
                    ins = v.reduce_max(small[:, 30 + idx % 2:31 + idx % 2],
                                       psS[idx % 2][:, 0:384], axis=AX.X, negate=True)
                    VS.inc(ins)
                sec("vector", p4_neg)
                w_neg = VS.n

                def p4_exp(sc, idx=idx, w=w_neg):
                    sc.wait_ge(VS.h, w)
                    if idx >= 2:
                        sc.wait_ge(TS.h, tsP[idx - 2])
                        sc.wait_ge(VS.h, vsR[idx - 2])
                    ins = sc.activation(pbuf[:, idx % 2, :], psS[idx % 2][:, 0:384],
                                        AF.Exp, bias=small[:, 30 + idx % 2:31 + idx % 2],
                                        scale=1.0,
                                        accum_out=small[:, 32 + idx % 2:33 + idx % 2])
                    SS.inc(ins)
                sec("scalar", p4_exp)
                ssE[idx] = SS.n

                def p4_rec(v, idx=idx, w=SS.n):
                    v.wait_ge(SS.h, w)
                    if idx >= 2:
                        v.wait_ge(SS.h, ssC[idx - 2])
                    ins = v.reciprocal(small[:, 34 + idx % 2:35 + idx % 2],
                                       small[:, 32 + idx % 2:33 + idx % 2])
                    VS.inc(ins)
                sec("vector", p4_rec)
                vsR[idx] = VS.n

                def p4_pt(t, idx=idx, w=ssE[idx]):
                    t.wait_ge(SS.h, w)
                    if idx >= 2:
                        t.wait_ge(SS.h, ssP[idx - 2])
                    for j in range(3):
                        ins = t.transpose(psP[idx % 2][:, j * 128:(j + 1) * 128],
                                          pbuf[:, idx % 2, j * 128:(j + 1) * 128],
                                          eyeb[:, :])
                    TS.inc(ins)
                sec("tensor", p4_pt)
                tsP[idx] = TS.n

                def p4_pte(sc, idx=idx, w=TS.n):
                    sc.wait_ge(TS.h, w)
                    if idx >= 2:
                        sc.wait_ge(TS.h, tsC[idx - 2])
                    ins = sc.copy(pTb[:, idx % 2, :, :],
                                  psP[idx % 2][:, 0:384].rearrange("p (a b) -> p a b", a=3))
                    SS.inc(ins)
                sec("scalar", p4_pte)
                ssP[idx] = SS.n

                def p4_ctx(t, t_i=t_i, h=h, idx=idx, w=SS.n):
                    t.wait_ge(SS.h, w)
                    if idx >= 2:
                        t.wait_ge(SS.h, ssC[idx - 2])
                    for j in range(3):
                        ins = t.matmul(psC[idx % 2][:, 0:128],
                                       pTb[:, idx % 2, j, :],
                                       v_sb[:, t_i + j, h * 128:(h + 1) * 128],
                                       start=(j == 0), stop=(j == 2))
                    TS.inc(ins)
                sec("tensor", p4_ctx)
                tsC[idx] = TS.n

                def p4_ce(sc, t_i=t_i, h=h, idx=idx, w=TS.n, wv=vsR[idx]):
                    sc.wait_ge(TS.h, w)
                    sc.wait_ge(VS.h, wv)
                    if h == 0 and t_i >= 2:
                        sc.wait_ge(TS.h, tsXT[t_i - 2])
                    ins = sc.activation(ctok[:, t_i % 2, h * 128:(h + 1) * 128],
                                        psC[idx % 2][:, 0:128], AF.Copy,
                                        scale=small[:, 34 + idx % 2:35 + idx % 2])
                    SS.inc(ins)
                sec("scalar", p4_ce)
                ssC[idx] = SS.n

            def p4_xt(t, t_i=t_i, w=SS.n):
                t.wait_ge(SS.h, w)
                if t_i >= 2:
                    t.wait_ge(SS.h, ssXT[t_i - 2])
                for d in range(ND):
                    bank = psX[0] if d < 8 else psX[1]
                    ins = t.transpose(bank[:, (d % 8) * 128:(d % 8 + 1) * 128],
                                      ctok[:, t_i % 2, d * 128:(d + 1) * 128], eyeb[:, :])
                TS.inc(ins)
            sec("tensor", p4_xt)
            tsXT[t_i] = TS.n

            def p4_xte(sc, t_i=t_i, w=TS.n):
                sc.wait_ge(TS.h, w)
                sc.copy(ctxT[:, 0:8, t_i * 128:(t_i + 1) * 128],
                        psX[0][:, :].rearrange("p (a b) -> p a b", a=8))
                ins = sc.copy(ctxT[:, 8:16, t_i * 128:(t_i + 1) * 128],
                              psX[1][:, :].rearrange("p (a b) -> p a b", a=8))
                SS.inc(ins)
            sec("scalar", p4_xte)
            ssXT[t_i] = SS.n

        p4.close()
        sc_v.close()
        sc_q.close()
        sc_k.close()

        p5 = ExitStack()
        woT = p5.enter_context(nc.sbuf_tensor("woT", [128, ND, D], BF16))
        xt5 = p5.enter_context(nc.sbuf_tensor("xt5", [128, 2, D], F32))
        gsc = p5.enter_context(nc.sbuf_tensor("gsc", [128, 512], F32))
        ps5 = [p5.enter_context(nc.psum_tensor(f"p5_{j}", [128, 512], F32)) for j in range(2)]

        def p5_g(g):
            g.wait_ge(TS.h, tsXT[NT - 1])
            g.dma_start(out=woT[:, :, :],
                        in_=wo_t.rearrange("(kt p) n -> p kt n", p=128)).then_inc(QW.h, 16)
            QW.n += 16
        sec("gpsimd", p5_g)
        qw_wo = QW.n

        qx_x5, vs_ol, qs_x1, ss_byp = {}, {}, {}, {}
        for t_i in range(NT):
            def p5_x(sy, t_i=t_i):
                if t_i == 0:
                    sy.wait_ge(TS.h, tsXT[NT - 1])
                if t_i >= 2:
                    sy.wait_ge(QS.h, qs_x1[t_i - 2])
                sy.dma_start(out=xt5[:, t_i % 2, :],
                             in_=x_ext[256 + t_i * 128:256 + (t_i + 1) * 128, :]
                             ).then_inc(QX.h, 16)
                QX.n += 16
            sec("sync", p5_x)
            qx_x5[t_i] = QX.n

            for c in range(4):
                idx = t_i * 4 + c

                def p5_t(t, t_i=t_i, c=c, idx=idx):
                    if idx == 0:
                        t.wait_ge(QW.h, qw_wo)
                        t.wait_ge(SS.h, ssXT[NT - 1])
                    if idx >= 2:
                        t.wait_ge(VS.h, vs_ol[idx - 2])
                    for d in range(ND):
                        ins = t.matmul(ps5[idx % 2][:, :],
                                       ctxT[:, d, t_i * 128:(t_i + 1) * 128],
                                       woT[:, d, c * 512:(c + 1) * 512],
                                       start=(d == 0), stop=(d == ND - 1))
                    TS.inc(ins)
                sec("tensor", p5_t)
                w_ol = TS.n

                def p5_v(v, t_i=t_i, c=c, idx=idx, w=w_ol):
                    v.wait_ge(TS.h, w)
                    if c == 0:
                        v.wait_ge(QX.h, qx_x5[t_i])
                        if t_i == 0:
                            v.wait_ge(VS.h, vs_gb)
                    v.scalar_tensor_tensor(gsc[:, :], ps5[idx % 2][:, :], 1.0,
                                           gwb[:, c * 512:(c + 1) * 512],
                                           op0=ALU.mult, op1=ALU.mult,
                                           accum_out=small[:, 26 + c:27 + c])
                    ins = v.tensor_tensor(xt5[:, t_i % 2, c * 512:(c + 1) * 512],
                                          xt5[:, t_i % 2, c * 512:(c + 1) * 512],
                                          ps5[idx % 2][:, :], op=ALU.add)
                    VS.inc(ins)
                sec("vector", p5_v)
                vs_ol[idx] = VS.n

            def p5_vg(v, t_i=t_i):
                ins = v.reduce_sum(small[:, 36:37], small[:, 26:30], axis=AX.X)
                VS.inc(ins)
            sec("vector", p5_vg)
            vs_gp = VS.n

            def p5_sg(sc, t_i=t_i, w=vs_gp):
                sc.wait_ge(VS.h, w)
                ins = sc.activation(small[:, 18 + t_i:19 + t_i], small[:, 36:37],
                                    AF.Sigmoid, bias=small[:, 1:2], scale=1.0)
                SS.inc(ins)
            sec("scalar", p5_sg)
            ss_byp[t_i] = SS.n

            def p5_st(sy, t_i=t_i, w=vs_gp):
                sy.wait_ge(VS.h, w)
                sy.dma_start(out=x1_hbm[t_i * 128:(t_i + 1) * 128, :],
                             in_=xt5[:, t_i % 2, :]).then_inc(QS.h, 16)
                QS.n += 16
            sec("sync", p5_st)
            qs_x1[t_i] = QS.n

        ts_p5_last = TS.n
        p5.close()
        sc_ctx.close()

        sc_r = ExitStack()
        rT = sc_r.enter_context(nc.sbuf_tensor("rT", [128, ND, TOK], BF16, side="right"))

        p6a = ExitStack()
        mbk = p6a.enter_context(nc.sbuf_tensor("mbk", [128, 32, D], BF16))
        simt = p6a.enter_context(nc.sbuf_tensor("simt", [128, 2048], BF16))
        emsk = p6a.enter_context(nc.sbuf_tensor("emsk", [128, 2048], BF16))
        pmem = p6a.enter_context(nc.sbuf_tensor("pmem", [128, 2048], BF16))
        pTm8 = p6a.enter_context(nc.sbuf_tensor("pTm8", [128, 2, 1024], BF16))
        psT6 = [p6a.enter_context(nc.psum_tensor(f"p6t{j}", [128, 1024], BF16))
                for j in range(2)]
        psR = [p6a.enter_context(nc.psum_tensor(f"p6r{j}", [128, 512], F32))
               for j in range(4)]

        qw_mbk = {}

        def p6_g(g):
            g.wait_ge(TS.h, ts_p5_last)
            for q in range(4):
                g.dma_start(out=mbk[:, q * 8:(q + 1) * 8, :],
                            in_=memb[q * 1024:(q + 1) * 1024, :]
                            .rearrange("(s p) d -> p s d", p=128)).then_inc(QW.h, 16)
                QW.n += 16
                qw_mbk[q] = QW.n
        sec("gpsimd", p6_g)

        vs_p6, ss_p6, ts_r6, vs_r6 = {}, {}, {}, {}
        ts_mm, vs_te6, ts_mm6 = {}, {}, {}
        for t_i in range(NT):
            def p6_seed(t, t_i=t_i):
                if t_i == 0:
                    t.wait_ge(QW.h, qw_mbk[3])
                    t.wait_ge(VS.h, vs_gw)
                if t_i >= 1:
                    t.wait_ge(VS.h, vs_r6[t_i - 1])
                for bank in range(4):
                    ins = t.matmul(psR[bank][:, :], zrow[0:1, 0:128], zrow[:, :],
                                   start=True, stop=True)
                TS.inc(ins)
            sec("tensor", p6_seed)

            for hf in range(2):
                hx = t_i * 2 + hf

                def p6_ld(sy, t_i=t_i, hf=hf, hx=hx):
                    if hx == 0:
                        sy.wait_ge(QS.h, qs_x1[NT - 1])
                    if hx >= 1:
                        sy.wait_ge(VS.h, vs_p6[hx - 1])
                        sy.wait_ge(SS.h, ss_p6[hx - 1])
                    sy.dma_start(out=simt[:, :],
                                 in_=sim_hbm[t_i, :, hf * 2048:(hf + 1) * 2048]
                                 ).then_inc(QX.h, 16)
                    QX.n += 16
                sec("sync", p6_ld)
                w_ld6 = QX.n

                def p6_msk(v, t_i=t_i, w=w_ld6):
                    v.wait_ge(QX.h, w)
                    ins = v.tensor_scalar(emsk[:, :], simt[:, :],
                                          top8f[:, t_i, 7:8], None, op0=ALU.is_ge)
                    VS.inc(ins)
                sec("vector", p6_msk)

                def p6_exp(sc, t_i=t_i, hx=hx, w=w_ld6):
                    sc.wait_ge(QX.h, w)
                    if hx >= 1:
                        sc.wait_ge(TS.h, ts_mm[hx - 1])
                    ins = sc.activation(pmem[:, :], simt[:, :], AF.Exp,
                                        bias=small[:, 2 + t_i:3 + t_i], scale=ISQD)
                    SS.inc(ins)
                sec("scalar", p6_exp)
                ss_p6[hx] = SS.n

                def p6_p(v, t_i=t_i, w=SS.n):
                    v.wait_ge(SS.h, w)
                    ins = v.scalar_tensor_tensor(pmem[:, :], pmem[:, :],
                                                 small[:, 10 + t_i:11 + t_i], emsk[:, :],
                                                 op0=ALU.mult, op1=ALU.mult)
                    VS.inc(ins)
                sec("vector", p6_p)
                vs_p6[hx] = VS.n

                for gg in range(2):
                    g6 = hx * 2 + gg
                    pp = g6 % 2

                    def p6_tr(t, gg=gg, g6=g6, pp=pp, w=vs_p6[hx]):
                        if gg == 0:
                            t.wait_ge(VS.h, w)
                        if g6 >= 2:
                            t.wait_ge(VS.h, vs_te6[g6 - 2])
                        for k in range(8):
                            ins = t.transpose(psT6[pp][:, k * 128:(k + 1) * 128],
                                              pmem[:, (gg * 8 + k) * 128:
                                                   (gg * 8 + k + 1) * 128], eyeb[:, :])
                        TS.inc(ins)
                    sec("tensor", p6_tr)
                    w_tr6 = TS.n

                    def p6_te(v, g6=g6, pp=pp, w=w_tr6):
                        v.wait_ge(TS.h, w)
                        if g6 >= 2:
                            v.wait_ge(TS.h, ts_mm6[g6 - 2])
                        ins = v.tensor_copy(pTm8[:, pp, :], psT6[pp][:, :])
                        VS.inc(ins)
                    sec("vector", p6_te)
                    vs_te6[g6] = VS.n

                    def p6_mm(t, hf=hf, gg=gg, pp=pp, w=VS.n):
                        t.wait_ge(VS.h, w)
                        for k in range(8):
                            sk2 = hf * 16 + gg * 8 + k
                            for bank in range(4):
                                for dd in range(4):
                                    d = bank * 4 + dd
                                    ins = t.matmul(psR[bank][:, dd * 128:(dd + 1) * 128],
                                                   mbk[:, sk2, d * 128:(d + 1) * 128],
                                                   pTm8[:, pp, k * 128:(k + 1) * 128],
                                                   start=False, stop=(sk2 == 31),
                                                   skip_group_check=True)
                        TS.inc(ins)
                    sec("tensor", p6_mm)
                    ts_mm6[g6] = TS.n
                ts_mm[hx] = TS.n
            ts_r6[t_i] = TS.n

            def p6_re(v, t_i=t_i, w=TS.n):
                v.wait_ge(TS.h, w)
                for bank in range(4):
                    ins = v.tensor_copy(rT[:, bank * 4:(bank + 1) * 4,
                                           t_i * 128:(t_i + 1) * 128],
                                        psR[bank][:, :].rearrange("p (a b) -> p a b", a=4))
                VS.inc(ins, 4)
            sec("vector", p6_re)
            vs_r6[t_i] = VS.n

        p6a.close()

        sc_om = ExitStack()
        omT = sc_om.enter_context(nc.sbuf_tensor("omT", [128, ND, TOK], BF16))

        p6b = ExitStack()
        wvc2 = p6b.enter_context(nc.sbuf_tensor("wvc2", [128, 2, ND, 128], BF16))
        ps6b = [p6b.enter_context(nc.psum_tensor(f"p6b{j}", [128, 512], F32))
                for j in range(2)]

        qw_wv2, ts_om, vs_om = {}, {}, {}
        for m in range(ND):
            def p6b_g(g, m=m):
                if m == 0:
                    g.wait_ge(TS.h, ts_r6[NT - 1])
                if m >= 2:
                    g.wait_ge(TS.h, ts_om[m - 2])
                g.dma_start(out=wvc2[:, m % 2, :, :],
                            in_=wv_t[:, m * 128:(m + 1) * 128]
                            .rearrange("(kt p) n -> p kt n", p=128)).then_inc(QW.h, 16)
                QW.n += 16
            sec("gpsimd", p6b_g)
            qw_wv2[m] = QW.n

            def p6b_t(t, m=m):
                t.wait_ge(QW.h, qw_wv2[m])
                if m == 0:
                    t.wait_ge(VS.h, vs_r6[NT - 1])
                for c in range(2):
                    idx = m * 2 + c
                    if idx >= 2:
                        t.wait_ge(VS.h, vs_om[idx - 2])
                    for d in range(ND):
                        ins = t.matmul(ps6b[c][:, :],
                                       wvc2[:, m % 2, d, :],
                                       rT[:, d, c * 512:(c + 1) * 512],
                                       start=(d == 0), stop=(d == ND - 1))
                    TS.inc(ins)
            sec("tensor", p6b_t)
            ts_om[m] = TS.n

            def p6b_v(v, m=m, w=TS.n):
                v.wait_ge(TS.h, w)
                for c in range(2):
                    ins = v.tensor_copy(omT[:, m, c * 512:(c + 1) * 512], ps6b[c][:, :])
                    VS.inc(ins)
            sec("vector", p6b_v)
            vs_om[m * 2] = VS.n - 1
            vs_om[m * 2 + 1] = VS.n

        p6b.close()
        sc_r.close()

        sc_x2 = ExitStack()
        xhat2T = sc_x2.enter_context(nc.sbuf_tensor("xhat2T", [128, ND, TOK], BF16, side="right"))

        p6c = ExitStack()
        woT2 = p6c.enter_context(nc.sbuf_tensor("woT2", [128, ND, D], BF16))
        xt6 = p6c.enter_context(nc.sbuf_tensor("xt6", [128, 2, D], F32))
        xh6 = p6c.enter_context(nc.sbuf_tensor("xh6", [128, 2, D], BF16))
        sqs6 = p6c.enter_context(nc.sbuf_tensor("sqs6", [128, 2, D], F32))
        ps6c = [p6c.enter_context(nc.psum_tensor(f"p6c{j}", [128, 512], F32))
                for j in range(2)]
        psX6 = [p6c.enter_context(nc.psum_tensor(f"p6x{j}", [128, 1024], BF16))
                for j in range(2)]
        psL = p6c.enter_context(nc.psum_tensor("p6l", [128, 512], F32))

        def p6c_g(g):
            g.wait_ge(TS.h, ts_om[ND - 1])
            g.dma_start(out=woT2[:, :, :],
                        in_=wo_t.rearrange("(kt p) n -> p kt n", p=128)).then_inc(QW.h, 16)
            QW.n += 16
        sec("gpsimd", p6c_g)
        qw_wo2 = QW.n

        qx_x6, vs_om6, qs_x1f = {}, {}, {}
        vs_l3d, ts_xh2, vs_xh2, vs_lg = {}, {}, {}, {}
        for t_i in range(NT):
            def p6c_x(sy, t_i=t_i):
                if t_i == 0:
                    sy.wait_ge(TS.h, ts_om[ND - 1])
                if t_i >= 2:
                    sy.wait_ge(QS.h, qs_x1f[t_i - 2])
                    sy.wait_ge(VS.h, vs_l3d[t_i - 2])
                sy.dma_start(out=xt6[:, t_i % 2, :],
                             in_=x1_hbm[t_i * 128:(t_i + 1) * 128, :]).then_inc(QX.h, 16)
                QX.n += 16
            sec("sync", p6c_x)
            qx_x6[t_i] = QX.n

            for c in range(4):
                idx = t_i * 4 + c

                def p6c_t(t, t_i=t_i, c=c, idx=idx):
                    if idx == 0:
                        t.wait_ge(QW.h, qw_wo2)
                        t.wait_ge(VS.h, vs_om[2 * ND - 1])
                    if idx >= 2:
                        t.wait_ge(VS.h, vs_om6[idx - 2])
                    for d in range(ND):
                        ins = t.matmul(ps6c[idx % 2][:, :],
                                       omT[:, d, t_i * 128:(t_i + 1) * 128],
                                       woT2[:, d, c * 512:(c + 1) * 512],
                                       start=(d == 0), stop=(d == ND - 1))
                    TS.inc(ins)
                sec("tensor", p6c_t)
                w_om = TS.n

                def p6c_v(v, t_i=t_i, c=c, idx=idx, w=w_om):
                    v.wait_ge(TS.h, w)
                    if c == 0:
                        v.wait_ge(QX.h, qx_x6[t_i])
                        if t_i == 0:
                            v.wait_ge(SS.h, ss_byp[NT - 1])
                    ins = v.scalar_tensor_tensor(xt6[:, t_i % 2, c * 512:(c + 1) * 512],
                                                 ps6c[idx % 2][:, :],
                                                 small[:, 18 + t_i:19 + t_i],
                                                 xt6[:, t_i % 2, c * 512:(c + 1) * 512],
                                                 op0=ALU.mult, op1=ALU.add)
                    VS.inc(ins)
                sec("vector", p6c_v)
                vs_om6[idx] = VS.n

            def p6c_st(sy, t_i=t_i, w=VS.n):
                sy.wait_ge(VS.h, w)
                sy.dma_start(out=x1_hbm[t_i * 128:(t_i + 1) * 128, :],
                             in_=xt6[:, t_i % 2, :]).then_inc(QS.h, 16)
                QS.n += 16
            sec("sync", p6c_st)
            qs_x1f[t_i] = QS.n

            def p7_v1(v, t_i=t_i):
                ins = v.reduce_sum(lnst[:, t_i, 0:1], xt6[:, t_i % 2, :], axis=AX.X)
                VS.inc(ins)
            sec("vector", p7_v1)
            w_l1 = VS.n

            def p7_s1(sc, t_i=t_i, w=w_l1):
                sc.wait_ge(VS.h, w)
                ins = sc.activation(sqs6[:, t_i % 2, :], xt6[:, t_i % 2, :], AF.Square,
                                    accum_out=lnst[:, t_i, 1:2])
                SS.inc(ins)
            sec("scalar", p7_s1)
            w_s1 = SS.n

            def p7_v2(v, t_i=t_i, w=w_s1):
                v.wait_ge(SS.h, w)
                v.tensor_scalar(lnst[:, t_i, 2:3], lnst[:, t_i, 0:1], 1.0 / D, None,
                                op0=ALU.mult)
                v.tensor_tensor(lnst[:, t_i, 3:4], lnst[:, t_i, 2:3], lnst[:, t_i, 2:3],
                                op=ALU.mult)
                ins = v.scalar_tensor_tensor(lnst[:, t_i, 4:5], lnst[:, t_i, 1:2], 1.0 / D,
                                             lnst[:, t_i, 3:4], op0=ALU.mult,
                                             op1=ALU.subtract)
                VS.inc(ins)
            sec("vector", p7_v2)
            w_v2 = VS.n

            def p7_s2(sc, t_i=t_i, w=w_v2):
                sc.wait_ge(VS.h, w)
                ins = sc.activation(lnst[:, t_i, 5:6], lnst[:, t_i, 4:5], AF.Sqrt,
                                    bias=small[:, 0:1], scale=1.0)
                SS.inc(ins)
            sec("scalar", p7_s2)
            w_s2 = SS.n

            def p7_v3(v, t_i=t_i, w=w_s2):
                v.wait_ge(SS.h, w)
                if t_i >= 2:
                    v.wait_ge(TS.h, ts_xh2[t_i - 2])
                v.reciprocal(lnst[:, t_i, 6:7], lnst[:, t_i, 5:6])
                v.tensor_scalar(small[:, 38:39], lnst[:, t_i, 6:7], 1.0, None, op0=ALU.mult)
                ins = v.tensor_scalar(xh6[:, t_i % 2, :], xt6[:, t_i % 2, :],
                                      lnst[:, t_i, 2:3], lnst[:, t_i, 6:7],
                                      op0=ALU.subtract, op1=ALU.mult)
                VS.inc(ins)
            sec("vector", p7_v3)
            vs_l3d[t_i] = VS.n

            def p7_t(t, t_i=t_i, w=VS.n):
                t.wait_ge(VS.h, w)
                if t_i >= 2:
                    t.wait_ge(VS.h, vs_xh2[t_i - 2])
                for d in range(ND):
                    bank = psX6[0] if d < 8 else psX6[1]
                    ins = t.transpose(bank[:, (d % 8) * 128:(d % 8 + 1) * 128],
                                      xh6[:, t_i % 2, d * 128:(d + 1) * 128], eyeb[:, :])
                TS.inc(ins)
            sec("tensor", p7_t)
            ts_xh2[t_i] = TS.n

            def p7_se(v, t_i=t_i, w=TS.n):
                v.wait_ge(TS.h, w)
                v.tensor_copy(xhat2T[:, 0:8, t_i * 128:(t_i + 1) * 128],
                              psX6[0][:, :].rearrange("p (a b) -> p a b", a=8))
                ins = v.tensor_copy(xhat2T[:, 8:16, t_i * 128:(t_i + 1) * 128],
                                    psX6[1][:, :].rearrange("p (a b) -> p a b", a=8))
                VS.inc(ins)
            sec("vector", p7_se)
            vs_xh2[t_i] = VS.n

            def p7_rt(t, t_i=t_i, w=VS.n):
                t.wait_ge(VS.h, w)
                if t_i >= 1:
                    t.wait_ge(VS.h, vs_lg[t_i - 1])
                for d in range(ND):
                    t.matmul(psL[:, 0:NE], xhat2T[:, d, t_i * 128:(t_i + 1) * 128],
                             wrT[:, d, :], start=(d == 0), stop=False)
                ins = t.matmul(psL[:, 0:NE], onesb[:, :], fbias_b[:, :],
                               start=False, stop=True)
                TS.inc(ins)
            sec("tensor", p7_rt)
            w_rt = TS.n

            def p7_le(v, t_i=t_i, w=w_rt):
                v.wait_ge(TS.h, w)
                ins = v.tensor_copy(logit_sb[:, t_i, :], psL[:, 0:NE])
                VS.inc(ins)
            sec("vector", p7_le)
            vs_lg[t_i] = VS.n

        p6c.close()

        def p7_r1(v):
            for t_i in range(NT):
                ins = v.max(out=top8f[:, t_i, :], in_=logit_sb[:, t_i, :])
            VS.inc(ins)
        sec("vector", p7_r1)
        vs_r1 = VS.n

        def p7_r2(sc):
            sc.wait_ge(VS.h, vs_r1)
            for t_i in range(NT):
                sc.activation(rst[:, t_i, 0:1], top8f[:, t_i, 0:1], AF.Identity, scale=-1.0)
            for t_i in range(NT):
                sc.activation(rst[:, t_i, 1:2], top8f[:, t_i, 1:2], AF.Exp,
                              bias=rst[:, t_i, 0:1], scale=1.0)
            for t_i in range(NT):
                ins = sc.activation(lexp[:, t_i, :], logit_sb[:, t_i, :], AF.Exp,
                                    bias=rst[:, t_i, 0:1], scale=1.0,
                                    accum_out=rst[:, t_i, 2:3])
            SS.inc(ins)
        sec("scalar", p7_r2)
        ss_r2 = SS.n

        def p7_r3(v):
            v.wait_ge(SS.h, ss_r2)
            for t_i in range(NT):
                v.scalar_tensor_tensor(rst[:, t_i, 3:4], rst[:, t_i, 2:3], EPS_RT,
                                       rst[:, t_i, 1:2], op0=ALU.mult, op1=ALU.add)
            for t_i in range(NT):
                v.tensor_scalar(rst[:, t_i, 3:4], rst[:, t_i, 3:4], 1.0, None, op0=ALU.add)
            for t_i in range(NT):
                v.reciprocal(rst[:, t_i, 4:5], rst[:, t_i, 3:4])
            for t_i in range(NT):
                ins = v.tensor_tensor(rst[:, t_i, 5:6], rst[:, t_i, 1:2], rst[:, t_i, 4:5],
                                      op=ALU.mult)
            VS.inc(ins)
            for t_i in range(NT):
                v.tensor_scalar(msk1[:, t_i, :], logit_sb[:, t_i, :],
                                top8f[:, t_i, 0:1], None, op0=ALU.is_ge)
            for t_i in range(NT):
                v.tensor_scalar(msk2[:, t_i, :], logit_sb[:, t_i, :],
                                top8f[:, t_i, 1:2], None, op0=ALU.is_ge)
            for t_i in range(NT):
                v.tensor_tensor(msk2[:, t_i, :], msk2[:, t_i, :], msk1[:, t_i, :],
                                op=ALU.subtract)
            for t_i in range(NT):
                v.tensor_scalar(msk1[:, t_i, :], msk1[:, t_i, :], rst[:, t_i, 4:5],
                                None, op0=ALU.mult)
            for t_i in range(NT):
                v.tensor_scalar(msk2[:, t_i, :], msk2[:, t_i, :], rst[:, t_i, 5:6],
                                None, op0=ALU.mult)
            for t_i in range(NT):
                ins = v.tensor_tensor(wmat_tok[:, t_i, :], msk1[:, t_i, :],
                                      msk2[:, t_i, :], op=ALU.add)
            VS.inc(ins)
        sec("vector", p7_r3)
        vs_wm = VS.n

        p_w = ExitStack()
        pwt = p_w.enter_context(nc.psum_tensor("pwt", [128, 1024], BF16))

        def p7_wt(t):
            t.wait_ge(VS.h, vs_wm)
            for t_i in range(NT):
                ins = t.transpose(pwt[0:NE, t_i * 128:(t_i + 1) * 128],
                                  wmat_tok[:, t_i, :], eyeb[:, :])
            TS.inc(ins)
        sec("tensor", p7_wt)
        ts_wt = TS.n

        def p7_we(sc):
            sc.wait_ge(TS.h, ts_wt)
            ins = sc.copy(WmatT[:, :], pwt[0:NE, :])
            SS.inc(ins)
        sec("scalar", p7_we)
        ss_wm = SS.n
        p_w.close()
        sc_om.close()

        sc_acc = ExitStack()
        acc = sc_acc.enter_context(nc.sbuf_tensor("acc", [128, ND, TOK], F32))

        p8 = ExitStack()
        wsl = p8.enter_context(nc.sbuf_tensor("wsl", [128, 4, ND * 512], BF16))
        z_sb = p8.enter_context(nc.sbuf_tensor("z_sb", [128, NI, TOK], BF16))
        gs_sb = p8.enter_context(nc.sbuf_tensor("gs_sb", [128, 2, TOK], BF16))
        wb_sb = p8.enter_context(nc.sbuf_tensor("wb_sb", [128, TOK], BF16))
        tmp8 = p8.enter_context(nc.sbuf_tensor("tmp8", [128, 512], BF16))
        ps8 = [p8.enter_context(nc.psum_tensor(f"p8_{j}", [128, 512], F32))
               for j in range(8)]

        def wsl_gu(buf, d, itl):
            return wsl[:, buf, d * 512 + itl * 128:d * 512 + itl * 128 + 128]

        def wsl_wd(buf, itl, col):
            return wsl[:, buf, itl * 2048 + col:itl * 2048 + col + 128]

        def slot_buf(e, name):
            P = (e % 2) * 2
            return {"g0": P, "u0": P + 1, "g1": 2 - P, "u1": 3 - P,
                    "d1": P, "d2": P + 1}[name]

        qw_slot = {}

        def emit_slot_load(e, name, wait_ts=None, wait_ss=None):
            buf = slot_buf(e, name)
            kind, hf = name[0], int(name[1]) - (1 if name[0] == "d" else 0)

            def _g(g, e=e, kind=kind, hf=hf, buf=buf, wait_ts=wait_ts,
                   wait_ss=wait_ss):
                if wait_ss is not None:
                    g.wait_ge(SS.h, wait_ss)
                if wait_ts is not None:
                    g.wait_ge(TS.h, wait_ts)
                if kind == "g":
                    ap = (sWg_d if e == NE else Wg_d[e, :, :])
                    ap = ap[:, hf * 512:(hf + 1) * 512].rearrange("(kt p) n -> p kt n", p=128)
                elif kind == "u":
                    ap = (sWu_d if e == NE else Wu_d[e, :, :])
                    ap = ap[:, hf * 512:(hf + 1) * 512].rearrange("(kt p) n -> p kt n", p=128)
                else:
                    ap = (sWd_d if e == NE else Wd_d[e, :, :])
                    ap = ap[hf * 512:(hf + 1) * 512, :].rearrange("(it p) h -> p it h", p=128)
                dst = wsl[:, buf, :].rearrange("p (a b) -> p a b",
                                               a=ND if kind != "d" else 4)
                g.dma_start(out=dst, in_=ap).then_inc(QW.h, 16)
                QW.n += 16
            sec("gpsimd", _g)
            qw_slot[(e, name)] = QW.n

        emit_slot_load(0, "g0", wait_ss=ss_wm)
        emit_slot_load(0, "u0")
        emit_slot_load(0, "g1")
        emit_slot_load(0, "u1")

        vs_z, vs_acc, ss_silu, ss_wbd = {}, {}, {}, {}
        qw_wrow, ts_wbd, ts_gu, ts_d = {}, {}, {}, {}
        for e in range(NE + 1):
            shared = (e == NE)

            if not shared:
                def p8_wr(g, e=e):
                    if e == 0:
                        g.wait_ge(SS.h, ss_wm)
                    if e >= 2:
                        g.wait_ge(TS.h, ts_wbd[e - 2])
                    g.dma_start(out=wrow[0:1, e % 2, :],
                                in_=WmatT[e:e + 1, :]).then_inc(QW.h, 16)
                    QW.n += 16
                sec("gpsimd", p8_wr)
                qw_wrow[e] = QW.n

                def p8_wb(t, e=e):
                    t.wait_ge(QW.h, qw_wrow[e])
                    if e == 0:
                        t.wait_ge(SS.h, ss_wm)
                    else:
                        t.wait_ge(VS.h, vs_acc[(e - 1, 15)])
                    for c in range(2):
                        ins = t.matmul(ps8[6 + c][:, :], onesb[:, :],
                                       wrow[0:1, e % 2, c * 512:(c + 1) * 512],
                                       start=True, stop=True)
                    TS.inc(ins)
                sec("tensor", p8_wb)
                w_wb = TS.n
                ts_wbd[e] = TS.n

                def p8_wbe(sc, w=w_wb):
                    sc.wait_ge(TS.h, w)
                    for c in range(2):
                        ins = sc.copy(wb_sb[:, c * 512:(c + 1) * 512], ps8[6 + c][:, :])
                        SS.inc(ins)
                sec("scalar", p8_wbe)
            ss_wbd[e] = SS.n

            for it in range(NI):
                hf = it // 4
                par = it % 2
                gb0, ub0 = (0, 2) if par == 0 else (4, 6)
                bg = slot_buf(e, "g0" if hf == 0 else "g1")
                bu = slot_buf(e, "u0" if hf == 0 else "u1")

                def p8_gu(t, e=e, it=it, hf=hf, par=par, gb0=gb0, ub0=ub0,
                          bg=bg, bu=bu):
                    if it == 0:
                        t.wait_ge(QW.h, qw_slot[(e, "u0")])
                    if it == 4:
                        t.wait_ge(QW.h, qw_slot[(e, "u1")])
                    if it == 0 and e == 0:
                        t.wait_ge(VS.h, vs_xh2[NT - 1])
                    if it == 0 and e > 0:
                        t.wait_ge(VS.h, vs_acc[(e - 1, 13)])
                    if it == 1 and e > 0:
                        t.wait_ge(VS.h, vs_acc[(e - 1, 15)])
                    if it == 1 and not (e == NE):
                        t.wait_ge(SS.h, ss_wbd[e])
                    if it >= 2:
                        t.wait_ge(SS.h, ss_silu[(e, it - 2)])
                        t.wait_ge(VS.h, vs_z[(e, it - 2)])
                    for th in range(2):
                        for d in range(ND):
                            t.matmul(ps8[gb0 + th][:, :],
                                     wsl_gu(bg, d, it % 4),
                                     xhat2T[:, d, th * 512:(th + 1) * 512],
                                     start=(d == 0), stop=(d == ND - 1))
                    for th in range(2):
                        for d in range(ND):
                            ins = t.matmul(ps8[ub0 + th][:, :],
                                           wsl_gu(bu, d, it % 4),
                                           xhat2T[:, d, th * 512:(th + 1) * 512],
                                           start=(d == 0), stop=(d == ND - 1))
                    TS.inc(ins)
                sec("tensor", p8_gu)
                w_gu = TS.n
                ts_gu[(e, it)] = TS.n
                if it == 3:
                    emit_slot_load(e, "d1", wait_ts=TS.n)
                    emit_slot_load(e, "d2")
                if it == 7 and e + 1 <= NE:
                    emit_slot_load(e + 1, "g0", wait_ts=TS.n)
                    emit_slot_load(e + 1, "u0")

                def p8_silu(sc, e=e, it=it, par=par, gb0=gb0, w=w_gu):
                    sc.wait_ge(TS.h, w)
                    for th in range(2):
                        ins = sc.activation(gs_sb[:, par, th * 512:(th + 1) * 512],
                                            ps8[gb0 + th][:, :], AF.Silu)
                    SS.inc(ins)
                sec("scalar", p8_silu)
                ss_silu[(e, it)] = SS.n

                def p8_z(v, e=e, it=it, par=par, ub0=ub0, shared=shared, w=SS.n):
                    v.wait_ge(SS.h, w)
                    if it == 0 and e > 0:
                        v.wait_ge(TS.h, ts_d[(e - 1, 15)])
                    for th in range(2):
                        if shared:
                            ins = v.tensor_tensor(z_sb[:, it, th * 512:(th + 1) * 512],
                                                  gs_sb[:, par, th * 512:(th + 1) * 512],
                                                  ps8[ub0 + th][:, :], op=ALU.mult)
                        else:
                            v.tensor_tensor(tmp8[:, :],
                                            gs_sb[:, par, th * 512:(th + 1) * 512],
                                            ps8[ub0 + th][:, :], op=ALU.mult)
                            ins = v.tensor_tensor(z_sb[:, it, th * 512:(th + 1) * 512],
                                                  tmp8[:, :],
                                                  wb_sb[:, th * 512:(th + 1) * 512],
                                                  op=ALU.mult)
                    VS.inc(ins)
                sec("vector", p8_z)
                vs_z[(e, it)] = VS.n

            bd1 = slot_buf(e, "d1")
            bd2 = slot_buf(e, "d2")
            for hb in range(16):
                pr = hb % 4

                def p8_d(t, e=e, hb=hb, pr=pr, bd1=bd1, bd2=bd2):
                    if hb == 0:
                        t.wait_ge(VS.h, vs_z[(e, NI - 1)])
                        t.wait_ge(SS.h, ss_silu[(e, NI - 1)])
                        t.wait_ge(QW.h, qw_slot[(e, "d2")])
                    if hb >= 4:
                        t.wait_ge(VS.h, vs_acc[(e, hb - 4)])
                    for th in range(2):
                        for it in range(NI):
                            buf = bd1 if it < 4 else bd2
                            ins = t.matmul(ps8[2 * pr + th][:, :],
                                           wsl_wd(buf, it % 4, hb * 128),
                                           z_sb[:, it, th * 512:(th + 1) * 512],
                                           start=(it == 0), stop=(it == NI - 1))
                    TS.inc(ins)
                sec("tensor", p8_d)
                ts_d[(e, hb)] = TS.n

                def p8_acc(v, e=e, hb=hb, pr=pr, w=TS.n):
                    v.wait_ge(TS.h, w)
                    for th in range(2):
                        if e == 0:
                            ins = v.tensor_copy(acc[:, hb, th * 512:(th + 1) * 512],
                                                ps8[2 * pr + th][:, :])
                        else:
                            ins = v.tensor_tensor(acc[:, hb, th * 512:(th + 1) * 512],
                                                  acc[:, hb, th * 512:(th + 1) * 512],
                                                  ps8[2 * pr + th][:, :], op=ALU.add)
                    VS.inc(ins)
                sec("vector", p8_acc)
                vs_acc[(e, hb)] = VS.n

            if e + 1 <= NE:
                emit_slot_load(e + 1, "g1", wait_ts=ts_d[(e, 15)])
                emit_slot_load(e + 1, "u1")

        p8.close()
        sc_x2.close()

        p9 = ExitStack()
        xt9 = p9.enter_context(nc.sbuf_tensor("xt9", [128, 2, D], F32))
        o9 = p9.enter_context(nc.sbuf_tensor("o9", [128, 2, D], F32))
        ps9 = [p9.enter_context(nc.psum_tensor(f"p9_{j}", [128, 512], F32))
               for j in range(8)]

        qx_x9, vs_o9, qs_out = {}, {}, {}
        for t_i in range(NT):
            def p9_x(sy, t_i=t_i):
                if t_i == 0:
                    sy.wait_ge(QS.h, qs_x1f[NT - 1])
                    sy.wait_ge(VS.h, vs_acc[(NE, 15)])
                if t_i >= 2:
                    sy.wait_ge(QS.h, qs_out[t_i - 2])
                sy.dma_start(out=xt9[:, t_i % 2, :],
                             in_=x1_hbm[t_i * 128:(t_i + 1) * 128, :]).then_inc(QX.h, 16)
                QX.n += 16
            sec("sync", p9_x)
            qx_x9[t_i] = QX.n

            def p9_t(t, t_i=t_i):
                if t_i == 0:
                    t.wait_ge(VS.h, vs_acc[(NE, 15)])
                if t_i >= 2:
                    t.wait_ge(VS.h, vs_o9[t_i - 2])
                for d in range(ND):
                    ins = t.transpose(ps9[4 * (t_i % 2) + d // 4]
                                      [:, (d % 4) * 128:(d % 4 + 1) * 128],
                                      acc[:, d, t_i * 128:(t_i + 1) * 128], eyef[:, :])
                TS.inc(ins)
            sec("tensor", p9_t)
            w_o9 = TS.n

            def p9_v(v, t_i=t_i, w=w_o9):
                v.wait_ge(TS.h, w)
                v.wait_ge(QX.h, qx_x9[t_i])
                for grp in range(4):
                    ins = v.tensor_tensor(o9[:, t_i % 2, grp * 512:(grp + 1) * 512],
                                          xt9[:, t_i % 2, grp * 512:(grp + 1) * 512],
                                          ps9[4 * (t_i % 2) + grp][:, :], op=ALU.add)
                VS.inc(ins)
            sec("vector", p9_v)
            vs_o9[t_i] = VS.n

            def p9_s(sy, t_i=t_i, w=VS.n):
                sy.wait_ge(VS.h, w)
                sy.dma_start(out=out_d[t_i * 128:(t_i + 1) * 128, :],
                             in_=o9[:, t_i % 2, :]).then_inc(QS.h, 16)
                QS.n += 16
            sec("sync", p9_s)
            qs_out[t_i] = QS.n

        p9.close()
        sc_acc.close()

        def p_final(sy):
            sy.wait_ge(QS.h, QS.n)
        sec("sync", p_final)

    return nc



_NC_CACHE = {}


def _get_nc():
    if "nc" not in _NC_CACHE:
        _NC_CACHE["nc"] = build_nc()
    return _NC_CACHE["nc"]


def kernel(L0_attn_hidden_states, L0_attn_wq, L0_attn_wk, L0_attn_wv, L0_attn_wo,
           L0_attn_gate_w, L0_attn_gate_b, L0_attn_mem_bank,
           L0_ffn_Wr, L0_ffn_bias, L0_ffn_Wg, L0_ffn_Wu, L0_ffn_Wd,
           L0_ffn_sWg, L0_ffn_sWu, L0_ffn_sWd):
    x = np.ascontiguousarray(np.asarray(L0_attn_hidden_states, dtype=np.float32))
    wq = np.asarray(L0_attn_wq, np.float32)
    wk = np.asarray(L0_attn_wk, np.float32)
    wv = np.asarray(L0_attn_wv, np.float32)
    wo = np.asarray(L0_attn_wo, np.float32)
    gw = np.ascontiguousarray(np.asarray(L0_attn_gate_w, np.float32).reshape(1, D))
    gb = np.ascontiguousarray(np.asarray(L0_attn_gate_b, np.float32).reshape(1, 1))
    mb = np.ascontiguousarray(np.asarray(L0_attn_mem_bank, np.float32))
    Wr = np.asarray(L0_ffn_Wr, np.float32)
    fb = np.ascontiguousarray(np.asarray(L0_ffn_bias, np.float32).reshape(1, NE))
    Wg = np.ascontiguousarray(np.asarray(L0_ffn_Wg, np.float32))
    Wu = np.ascontiguousarray(np.asarray(L0_ffn_Wu, np.float32))
    Wd = np.ascontiguousarray(np.asarray(L0_ffn_Wd, np.float32))
    sWg = np.ascontiguousarray(np.asarray(L0_ffn_sWg, np.float32))
    sWu = np.ascontiguousarray(np.asarray(L0_ffn_sWu, np.float32))
    sWd = np.ascontiguousarray(np.asarray(L0_ffn_sWd, np.float32))

    wq_t = np.ascontiguousarray(wq.T)
    wk_t = np.ascontiguousarray(wk.T)
    wv_t = np.ascontiguousarray(wv.T)
    wo_t = np.ascontiguousarray(wo.T)
    wr_t = np.ascontiguousarray(Wr.T)
    memb_t = np.ascontiguousarray(mb.T)

    eye = np.eye(128, dtype=np.float32)
    ones = np.ones((1, 128), np.float32)
    epsc = np.full((128, 1), EPS_LN, np.float32)

    i = np.arange(128)[:, None]
    j = np.arange(384)[None, :]
    band = (j > i) & (j <= i + 256)

    in_maps = []
    B, T = 2, 4096
    for core in range(8):
        b = core // 4
        t0 = (core % 4) * 1024
        lo = t0 - 256
        if lo < 0:
            xe = np.concatenate([np.zeros((-lo, D), np.float32),
                                 x[b, 0:t0 + 1024]], axis=0)
        else:
            xe = x[b, lo:t0 + 1024]
        xe = np.ascontiguousarray(xe)
        edge = (t0 == 0)
        mks = np.empty((3, 128, 384), np.float32)
        for mi in range(3):
            allowed = band.copy()
            if edge and mi < 2:
                allowed = allowed & (j >= (256 - mi * 128))
            mks[mi] = np.where(allowed, 0.0, -1e30)
        in_maps.append({
            "x_ext": xe, "wq_t": wq_t, "wk_t": wk_t, "wk_d": wk, "wv_t": wv_t,
            "wo_t": wo_t,
            "wr_t": wr_t, "gate_w": gw, "gate_b": gb, "memb_t": memb_t, "memb": mb,
            "Wg_d": Wg, "Wu_d": Wu, "Wd_d": Wd, "sWg_d": sWg, "sWu_d": sWu,
            "sWd_d": sWd, "fbias": fb, "masks": np.ascontiguousarray(mks),
            "eye_f": eye, "ones_r": ones, "eps_c": epsc,
        })

    nc = _get_nc()
    res = run_bass_kernel_spmd(nc, in_maps, core_ids=list(range(8)))
    outs = [np.asarray(r["out"]) for r in res.results]
    full = np.empty((B, T, D), np.float32)
    for core in range(8):
        b = core // 4
        t0 = (core % 4) * 1024
        full[b, t0:t0 + 1024] = outs[core]
    return full


# version 09
# speedup vs baseline: 1.4619x; 1.4598x over previous
import numpy as np
from contextlib import ExitStack

import concourse.bass as bass
import concourse.mybir as mybir
from concourse.bass_utils import run_bass_kernel_spmd

F32 = mybir.dt.float32
BF16 = mybir.dt.bfloat16
AF = mybir.ActivationFunctionType
ALU = mybir.AluOpType
AX = mybir.AxisListType

TOK, NT = 1024, 8
EXT, NTE = 1280, 10
D, ND = 2048, 16
H = 16
MSLOT, NS = 4096, 8
NE, DFF, NI = 16, 1024, 8
EPS_LN = 1e-5
EPS_RT = 1e-9
ISQD = float(1.0 / np.sqrt(2048.0))
ISQH = float(1.0 / np.sqrt(128.0))


class Sem:
    def __init__(self, ctx, nc, name):
        self.h = ctx.enter_context(nc.semaphore(name))
        self.n = 0

    def inc(self, ins, k=1):
        ins.then_inc(self.h, k)
        self.n += k
        return self.n


def build_nc():
    nc = bass.Bass("TRN2", target_bir_lowering=False)

    x_ext = nc.dram_tensor("x_ext", [EXT, D], F32, kind="ExternalInput").ap()
    wq_t = nc.dram_tensor("wq_t", [D, D], F32, kind="ExternalInput").ap()
    wk_t = nc.dram_tensor("wk_t", [D, D], F32, kind="ExternalInput").ap()
    wk_d = nc.dram_tensor("wk_d", [D, D], F32, kind="ExternalInput").ap()
    wv_t = nc.dram_tensor("wv_t", [D, D], F32, kind="ExternalInput").ap()
    wo_t = nc.dram_tensor("wo_t", [D, D], F32, kind="ExternalInput").ap()
    wr_t = nc.dram_tensor("wr_t", [D, NE], F32, kind="ExternalInput").ap()
    gate_w = nc.dram_tensor("gate_w", [1, D], F32, kind="ExternalInput").ap()
    gate_b = nc.dram_tensor("gate_b", [1, 1], F32, kind="ExternalInput").ap()
    memb_t = nc.dram_tensor("memb_t", [D, MSLOT], F32, kind="ExternalInput").ap()
    memb = nc.dram_tensor("memb", [MSLOT, D], F32, kind="ExternalInput").ap()
    Wg_d = nc.dram_tensor("Wg_d", [NE, D, DFF], F32, kind="ExternalInput").ap()
    Wu_d = nc.dram_tensor("Wu_d", [NE, D, DFF], F32, kind="ExternalInput").ap()
    Wd_d = nc.dram_tensor("Wd_d", [NE, DFF, D], F32, kind="ExternalInput").ap()
    sWg_d = nc.dram_tensor("sWg_d", [D, DFF], F32, kind="ExternalInput").ap()
    sWu_d = nc.dram_tensor("sWu_d", [D, DFF], F32, kind="ExternalInput").ap()
    sWd_d = nc.dram_tensor("sWd_d", [DFF, D], F32, kind="ExternalInput").ap()
    fbias = nc.dram_tensor("fbias", [1, NE], F32, kind="ExternalInput").ap()
    masks = nc.dram_tensor("masks", [3, 128, 384], F32, kind="ExternalInput").ap()
    eye_f = nc.dram_tensor("eye_f", [128, 128], F32, kind="ExternalInput").ap()
    ones_r = nc.dram_tensor("ones_r", [1, 128], F32, kind="ExternalInput").ap()
    eps_c = nc.dram_tensor("eps_c", [128, 1], F32, kind="ExternalInput").ap()
    out_d = nc.dram_tensor("out", [TOK, D], F32, kind="ExternalOutput").ap()
    sim_hbm = nc.dram_tensor("sim_hbm", [NT, 128, MSLOT], BF16, kind="Internal").ap()
    x1_hbm = nc.dram_tensor("x1_hbm", [TOK, D], F32, kind="Internal").ap()

    ctx = ExitStack()
    with ctx:
        E = ctx.enter_context

        small = E(nc.sbuf_tensor("small", [128, 64], F32))
        top8f = E(nc.sbuf_tensor("top8f", [128, NT, 8], F32))
        e8 = E(nc.sbuf_tensor("e8", [128, NT, 8], F32))
        eyeb = E(nc.sbuf_tensor("eyeb", [128, 128], BF16))
        eyef = E(nc.sbuf_tensor("eyef", [128, 128], F32))
        onesb = E(nc.sbuf_tensor("onesb", [1, 128], BF16))
        maskb = E(nc.sbuf_tensor("maskb", [128, 3, 384], BF16))
        gwsrc = E(nc.sbuf_tensor("gwsrc", [1, D], BF16))
        gwb = E(nc.sbuf_tensor("gwb", [128, D], BF16))
        wrT = E(nc.sbuf_tensor("wrT", [128, ND, NE], BF16))
        fbias_b = E(nc.sbuf_tensor("fbias_b", [1, NE], BF16))
        gate_b_b = E(nc.sbuf_tensor("gate_b_b", [1, 1], BF16))
        WmatT = E(nc.sbuf_tensor("WmatT", [16, NT * 128], BF16))
        logit_sb = E(nc.sbuf_tensor("logit_sb", [128, NT, NE], F32))
        lexp = E(nc.sbuf_tensor("lexp", [128, NT, NE], F32))
        rst = E(nc.sbuf_tensor("rst", [128, NT, 8], F32))
        lnst = E(nc.sbuf_tensor("lnst", [128, NTE, 8], F32))
        wmat_tok = E(nc.sbuf_tensor("wmat_tok", [128, NT, NE], BF16))
        wrow = E(nc.sbuf_tensor("wrow", [1, 2, TOK], BF16))
        msk1 = E(nc.sbuf_tensor("msk1", [128, NT, NE], F32))
        msk2 = E(nc.sbuf_tensor("msk2", [128, NT, NE], F32))
        zrow = E(nc.sbuf_tensor("zrow", [1, 512], BF16))

        QX = Sem(ctx, nc, "QX")
        QW = Sem(ctx, nc, "QW")
        QS = Sem(ctx, nc, "QS")
        SQ = Sem(ctx, nc, "SQ")
        TS = Sem(ctx, nc, "TS")
        VS = Sem(ctx, nc, "VS")
        SS = Sem(ctx, nc, "SS")

        block = E(nc.Block())

        def sec(name, fn):
            getattr(block, name)(fn)

        sc_xhat = ExitStack()
        xhatT = sc_xhat.enter_context(nc.sbuf_tensor("xhatT", [128, ND, EXT], BF16, side="right"))

        p0 = ExitStack()
        xext = p0.enter_context(nc.sbuf_tensor("xext", [128, 5, D], F32))
        xhtok = p0.enter_context(nc.sbuf_tensor("xhtok", [128, 5, D], BF16))
        sqs = p0.enter_context(nc.sbuf_tensor("sqs", [128, 2, D], F32))
        ps_tr = [p0.enter_context(nc.psum_tensor(f"p0t{j}", [128, 1024], BF16))
                 for j in range(4)]
        ps_g = p0.enter_context(nc.psum_tensor("p0g", [128, 2048], F32))

        def p0_g(g):
            g.dma_start(out=eyeb[:, :], in_=eye_f[:, :]).then_inc(QW.h, 16)
            g.dma_start(out=onesb[:, :], in_=ones_r[:, :]).then_inc(QW.h, 16)
            g.dma_start(out=maskb[:, :, :],
                        in_=masks.rearrange("m p n -> p m n")).then_inc(QW.h, 16)
            g.dma_start(out=wrT[:, :, :],
                        in_=wr_t.rearrange("(kt p) n -> p kt n", p=128)).then_inc(QW.h, 16)
            g.dma_start(out=fbias_b[:, :], in_=fbias[:, :]).then_inc(QW.h, 16)
            g.dma_start(out=gate_b_b[:, :], in_=gate_b[:, :]).then_inc(QW.h, 16)
            g.dma_start(out=gwsrc[:, :], in_=gate_w[:, :]).then_inc(QW.h, 16)
            QW.n += 7 * 16
        sec("gpsimd", p0_g)
        qw_consts = QW.n

        def p0_s0(s):
            s.dma_start(out=eyef[:, :], in_=eye_f[:, :]).then_inc(QX.h, 16)
            s.dma_start(out=small[:, 0:1], in_=eps_c[:, :]).then_inc(QX.h, 16)
            QX.n += 2 * 16
        sec("sync", p0_s0)

        def p0_t(t):
            t.wait_ge(QW.h, qw_consts)
            for c in range(4):
                ins = t.matmul(ps_g[:, c * 512:(c + 1) * 512], onesb[:, :],
                               gwsrc[:, c * 512:(c + 1) * 512], start=True, stop=True)
            TS.inc(ins)
        sec("tensor", p0_t)
        ts_gw = TS.n

        def p0_v0(v):
            v.wait_ge(TS.h, ts_gw)
            v.memset(zrow[:, :], 0.0)
            ins = v.tensor_copy(gwb[:, :], ps_g[:, :])
            VS.inc(ins)
        sec("vector", p0_v0)
        vs_gw = VS.n

        def p0_t2(t):
            t.wait_ge(VS.h, vs_gw)
            ins = t.matmul(ps_g[:, 0:1], onesb[:, :], gate_b_b[:, :], start=True, stop=True)
            TS.inc(ins)
        sec("tensor", p0_t2)
        ts_gb = TS.n

        def p0_v0b(v):
            v.wait_ge(TS.h, ts_gb)
            ins = v.tensor_copy(small[:, 1:2], ps_g[:, 0:1])
            VS.inc(ins)
        sec("vector", p0_v0b)
        vs_gb = VS.n

        ss_tr = {}
        for hb in range(2):
            tl = [hb * 5 + u for u in range(5)]

            def p0_s(s, hb=hb):
                if hb == 1:
                    s.wait_ge(SS.h, ss_tr[4])
                s.dma_start(out=xext[:, :, :],
                            in_=x_ext[hb * 640:(hb + 1) * 640, :]
                            .rearrange("(t p) d -> p t d", p=128)).then_inc(QX.h, 16)
                QX.n += 16
            sec("sync", p0_s)
            qx_h = QX.n

            def p0_v1(v, hb=hb, w=qx_h):
                v.wait_ge(QX.h, w)
                for u in range(5):
                    ins = v.reduce_sum(lnst[:, hb * 5 + u, 0:1], xext[:, u, :], axis=AX.X)
                VS.inc(ins)
            sec("vector", p0_v1)

            def p0_s1(sc, hb=hb, w=qx_h):
                sc.wait_ge(QX.h, w)
                for u in range(5):
                    ins = sc.activation(sqs[:, u % 2, :], xext[:, u, :], AF.Square,
                                        accum_out=lnst[:, hb * 5 + u, 1:2])
                SS.inc(ins)
            sec("scalar", p0_s1)
            ss_sq1 = SS.n

            def p0_v2(v, hb=hb, w=ss_sq1):
                v.wait_ge(SS.h, w)
                for u in range(5):
                    i = hb * 5 + u
                    v.tensor_scalar(lnst[:, i, 2:3], lnst[:, i, 0:1], 1.0 / D, None,
                                    op0=ALU.mult)
                for u in range(5):
                    i = hb * 5 + u
                    v.tensor_tensor(lnst[:, i, 3:4], lnst[:, i, 2:3], lnst[:, i, 2:3],
                                    op=ALU.mult)
                for u in range(5):
                    i = hb * 5 + u
                    ins = v.scalar_tensor_tensor(lnst[:, i, 4:5], lnst[:, i, 1:2], 1.0 / D,
                                                 lnst[:, i, 3:4], op0=ALU.mult,
                                                 op1=ALU.subtract)
                VS.inc(ins)
            sec("vector", p0_v2)
            vs_var1 = VS.n

            def p0_s2(sc, hb=hb, w=vs_var1):
                sc.wait_ge(VS.h, w)
                for u in range(5):
                    i = hb * 5 + u
                    ins = sc.activation(lnst[:, i, 5:6], lnst[:, i, 4:5], AF.Sqrt,
                                        bias=small[:, 0:1], scale=1.0)
                SS.inc(ins)
            sec("scalar", p0_s2)
            ss_sd1 = SS.n

            def p0_v3(v, hb=hb, w=ss_sd1):
                v.wait_ge(SS.h, w)
                for u in range(5):
                    i = hb * 5 + u
                    v.reciprocal(lnst[:, i, 6:7], lnst[:, i, 5:6])
                for u in range(5):
                    i = hb * 5 + u
                    ins = v.tensor_scalar(xhtok[:, u, :], xext[:, u, :],
                                          lnst[:, i, 2:3], lnst[:, i, 6:7],
                                          op0=ALU.subtract, op1=ALU.mult)
                VS.inc(ins)
            sec("vector", p0_v3)
            vs_xh1 = VS.n

            for u in range(5):
                i = hb * 5 + u
                lo, hi = ps_tr[(2 * i) % 4], ps_tr[(2 * i + 1) % 4]

                def p0_ti(t, u=u, i=i, lo=lo, hi=hi, w=vs_xh1):
                    if u == 0:
                        t.wait_ge(VS.h, w)
                    if i >= 2:
                        t.wait_ge(SS.h, ss_tr[i - 2])
                    for d in range(ND):
                        bank = lo if d < 8 else hi
                        ins = t.transpose(bank[:, (d % 8) * 128:(d % 8 + 1) * 128],
                                          xhtok[:, u, d * 128:(d + 1) * 128], eyeb[:, :])
                    TS.inc(ins)
                sec("tensor", p0_ti)
                w_tr = TS.n

                def p0_si(sc, i=i, lo=lo, hi=hi, w=w_tr):
                    sc.wait_ge(TS.h, w)
                    sc.copy(xhatT[:, 0:8, i * 128:(i + 1) * 128],
                            lo[:, :].rearrange("p (a b) -> p a b", a=8))
                    ins = sc.copy(xhatT[:, 8:16, i * 128:(i + 1) * 128],
                                  hi[:, :].rearrange("p (a b) -> p a b", a=8))
                    SS.inc(ins)
                sec("scalar", p0_si)
                ss_tr[i] = SS.n
        ss_xhatT = SS.n

        p0.close()

        sc_k = ExitStack()
        kT = sc_k.enter_context(nc.sbuf_tensor("kT", [128, ND, EXT], BF16))
        sc_wqv = ExitStack()
        wqc = sc_wqv.enter_context(nc.sbuf_tensor("wqc", [128, 2, ND, 128], BF16,
                                                  side="right"))
        sc_p1 = ExitStack()
        top8p = sc_p1.enter_context(nc.sbuf_tensor("top8p", [128, NT, NS, 8], F32))
        mbt = sc_p1.enter_context(nc.sbuf_tensor("mbt", [128, 2, ND, 512], BF16))
        simf = sc_p1.enter_context(nc.sbuf_tensor("simf", [128, 4, 512], BF16))
        kqT = sc_p1.enter_context(nc.sbuf_tensor("kqT", [128, ND, TOK], BF16))
        ps1s = [sc_p1.enter_context(nc.psum_tensor(f"p1s{j}", [128, 512], F32))
                for j in range(4)]

        p1b = ExitStack()
        wkc = p1b.enter_context(nc.sbuf_tensor("wkc", [128, 2, ND, 128], BF16))
        wkqc = p1b.enter_context(nc.sbuf_tensor("wkqc", [128, 2, ND, 128], BF16))
        ps1b = [p1b.enter_context(nc.psum_tensor(f"p1b{j}", [128, 512], F32))
                for j in range(2)]

        qw_mbt = {}
        qw_wk2, ts_k2, vs_kev = {}, {}, {}
        for m in range(ND):
            def p1b_g(g, m=m):
                if m == 0:
                    g.wait_ge(SS.h, ss_xhatT)
                if m >= 2:
                    g.wait_ge(TS.h, ts_k2[m - 2])
                g.dma_start(out=wkc[:, m % 2, :, :],
                            in_=wk_t[:, m * 128:(m + 1) * 128]
                            .rearrange("(kt p) n -> p kt n", p=128)).then_inc(QW.h, 16)
                g.dma_start(out=wkqc[:, m % 2, :, :],
                            in_=wk_d[:, m * 128:(m + 1) * 128]
                            .rearrange("(kt p) n -> p kt n", p=128)).then_inc(QW.h, 16)
                QW.n += 32
            sec("gpsimd", p1b_g)
            qw_wk2[m] = QW.n

            for c in range(5):
                kq = (c >= 3)
                ncols = 256 if c == 2 else 512
                col0 = (c - 3) * 512 + 256 if kq else c * 512
                idx = m * 5 + c

                def p1b_t(t, m=m, c=c, kq=kq, ncols=ncols, col0=col0, idx=idx):
                    if c == 0:
                        t.wait_ge(QW.h, qw_wk2[m])
                    if idx == 0:
                        t.wait_ge(SS.h, ss_xhatT)
                        t.wait_ge(VS.h, vs_gb)
                    if idx >= 2:
                        t.wait_ge(VS.h, vs_kev[idx - 2])
                    wsrc = wkqc if kq else wkc
                    for d in range(ND):
                        ins = t.matmul(ps1b[idx % 2][:, 0:ncols],
                                       wsrc[:, m % 2, d, :],
                                       xhatT[:, d, col0:col0 + ncols],
                                       start=(d == 0), stop=(d == ND - 1))
                    TS.inc(ins)
                sec("tensor", p1b_t)
                ts_k2[m] = TS.n
                w_k = TS.n

                def p1b_v(v, m=m, c=c, kq=kq, ncols=ncols, idx=idx, w=w_k):
                    v.wait_ge(TS.h, w)
                    if kq:
                        dst = kqT[:, m, (c - 3) * 512:(c - 3) * 512 + ncols]
                    else:
                        dst = kT[:, m, c * 512:c * 512 + ncols]
                    ins = v.tensor_copy(dst, ps1b[idx % 2][:, 0:ncols])
                    VS.inc(ins)
                sec("vector", p1b_v)
                vs_kev[idx] = VS.n

        p1b.close()

        def p1_g2(g):
            g.dma_start(out=mbt[:, 0, :, :],
                        in_=memb_t[:, 0:512].rearrange("(kt p) n -> p kt n", p=128)
                        ).then_inc(QW.h, 16)
            QW.n += 16
        sec("gpsimd", p1_g2)
        qw_mbt[0] = QW.n

        ts_sim = {}
        ss_sevac = {}
        vs_sevac = {}
        qs_sim = {}

        for s in range(NS):
            if s + 1 < NS:
                def p1_gp(g, s=s):
                    if s >= 1:
                        g.wait_ge(TS.h, ts_sim[(s - 1) * NT + NT - 1])
                    g.dma_start(out=mbt[:, (s + 1) % 2, :, :],
                                in_=memb_t[:, (s + 1) * 512:(s + 2) * 512]
                                .rearrange("(kt p) n -> p kt n", p=128)).then_inc(QW.h, 16)
                    QW.n += 16
                sec("gpsimd", p1_gp)
                qw_mbt[s + 1] = QW.n

            for t_i in range(NT):
                idx = s * NT + t_i
                bank = ps1s[idx % 4]

                def p1_sm(t, s=s, t_i=t_i, bank=bank, idx=idx):
                    if t_i == 0:
                        t.wait_ge(QW.h, qw_mbt[s])
                    if idx == 0:
                        t.wait_ge(VS.h, vs_kev[ND * 5 - 1])
                    if idx >= 4:
                        t.wait_ge(SS.h, ss_sevac[idx - 4])
                    for d in range(ND):
                        ins = t.matmul(bank[:, :],
                                       kqT[:, d, t_i * 128:(t_i + 1) * 128],
                                       mbt[:, s % 2, d, :],
                                       start=(d == 0), stop=(d == ND - 1))
                    TS.inc(ins)
                sec("tensor", p1_sm)
                ts_sim[idx] = TS.n

                def p1_se(sc, bank=bank, idx=idx, w=TS.n):
                    sc.wait_ge(TS.h, w)
                    if idx >= 4:
                        sc.wait_ge(QS.h, qs_sim[idx - 4])
                        sc.wait_ge(VS.h, vs_sevac[idx - 4])
                    ins = sc.copy(simf[:, idx % 4, :], bank[:, :])
                    SS.inc(ins)
                sec("scalar", p1_se)
                ss_sevac[idx] = SS.n

                def p1_mx(v, s=s, t_i=t_i, idx=idx, w=SS.n):
                    v.wait_ge(SS.h, w)
                    ins = v.max(out=top8p[:, t_i, s, :], in_=simf[:, idx % 4, :])
                    VS.inc(ins)
                sec("vector", p1_mx)
                vs_sevac[idx] = VS.n

                def p1_st(sy, s=s, t_i=t_i, idx=idx, w=SS.n):
                    sy.wait_ge(SS.h, w)
                    sy.dma_start(out=sim_hbm[t_i, :, s * 512:(s + 1) * 512],
                                 in_=simf[:, idx % 4, :]).then_inc(QS.h, 16)
                    QS.n += 16
                sec("sync", p1_st)
                qs_sim[idx] = QS.n

        def p1_v4(v):
            for t_i in range(NT):
                ins = v.max(out=top8f[:, t_i, :], in_=top8p[:, t_i, :, :])
            VS.inc(ins)
        sec("vector", p1_v4)
        vs_t8 = VS.n

        def p1_s4(sc):
            sc.wait_ge(VS.h, vs_t8)
            for t_i in range(NT):
                sc.activation(small[:, 2 + t_i:3 + t_i], top8f[:, t_i, 0:1],
                              AF.Identity, scale=-ISQD)
            for t_i in range(NT):
                ins = sc.activation(e8[:, t_i, :], top8f[:, t_i, :], AF.Exp,
                                    bias=small[:, 2 + t_i:3 + t_i], scale=ISQD)
            SS.inc(ins)
        sec("scalar", p1_s4)
        ss_e8 = SS.n

        def p1_v5(v):
            v.wait_ge(SS.h, ss_e8)
            for t_i in range(NT):
                v.reduce_sum(rst[:, t_i, 7:8], e8[:, t_i, :], axis=AX.X)
            for t_i in range(NT):
                ins = v.reciprocal(small[:, 10 + t_i:11 + t_i], rst[:, t_i, 7:8])
            VS.inc(ins)
        sec("vector", p1_v5)
        vs_mden = VS.n

        sc_p1.close()

        sc_q = ExitStack()
        qT = sc_q.enter_context(nc.sbuf_tensor("qT", [128, ND, TOK], BF16))

        p2 = ExitStack()
        ps2 = [p2.enter_context(nc.psum_tensor(f"p2_{j}", [128, 512], F32)) for j in range(2)]

        qw_wq = {}
        ts_q = {}
        vs_qe = {}
        for m in range(ND):
            def p2_g(g, m=m):
                if m == 0:
                    g.wait_ge(SS.h, ss_xhatT)
                if m >= 2:
                    g.wait_ge(TS.h, ts_q[m - 2])
                g.dma_start(out=wqc[:, m % 2, :, :],
                            in_=wq_t[:, m * 128:(m + 1) * 128]
                            .rearrange("(kt p) n -> p kt n", p=128)).then_inc(QW.h, 16)
                QW.n += 16
            sec("gpsimd", p2_g)
            qw_wq[m] = QW.n

            def p2_t(t, m=m):
                t.wait_ge(QW.h, qw_wq[m])
                if m == 0:
                    t.wait_ge(SS.h, ss_sevac[NS * NT - 1])
                    t.wait_ge(VS.h, vs_sevac[NS * NT - 1])
                for c in range(2):
                    idx = m * 2 + c
                    if idx >= 2:
                        t.wait_ge(VS.h, vs_qe[idx - 2])
                    for d in range(ND):
                        ins = t.matmul(ps2[c][:, :],
                                       wqc[:, m % 2, d, :],
                                       xhatT[:, d, 256 + c * 512:256 + (c + 1) * 512],
                                       start=(d == 0), stop=(d == ND - 1))
                    TS.inc(ins)
            sec("tensor", p2_t)
            ts_q[m] = TS.n

            def p2_v(v, m=m, w=TS.n):
                v.wait_ge(TS.h, w)
                if m == 0:
                    v.wait_ge(QS.h, qs_sim[NS * NT - 1])
                for c in range(2):
                    ins = v.tensor_scalar(qT[:, m, c * 512:(c + 1) * 512], ps2[c][:, :],
                                          ISQH, None, op0=ALU.mult)
                    VS.inc(ins)
            sec("vector", p2_v)
            vs_qe[m * 2] = VS.n - 1
            vs_qe[m * 2 + 1] = VS.n

        p2.close()
        sc_wqv.close()

        sc_v = ExitStack()
        v_sb = sc_v.enter_context(nc.sbuf_tensor("v_sb", [128, NTE, D], BF16))

        p3 = ExitStack()
        wvc = p3.enter_context(nc.sbuf_tensor("wvc", [128, 2, ND, 512], BF16))
        ps3 = [p3.enter_context(nc.psum_tensor(f"p3_{j}", [128, 512], F32)) for j in range(2)]

        qw_wv = {}
        ts_v = {}
        vs_ve = {}
        for c in range(4):
            def p3_g(g, c=c):
                if c == 0:
                    g.wait_ge(TS.h, ts_q[ND - 1])
                    g.wait_ge(QS.h, qs_sim[NS * NT - 1])
                if c >= 2:
                    g.wait_ge(TS.h, ts_v[c - 2])
                g.dma_start(out=wvc[:, c % 2, :, :],
                            in_=wv_t[:, c * 512:(c + 1) * 512]
                            .rearrange("(kt p) n -> p kt n", p=128)).then_inc(QW.h, 16)
                QW.n += 16
            sec("gpsimd", p3_g)
            qw_wv[c] = QW.n

            for t_i in range(NTE):
                idx = c * NTE + t_i
                bank = ps3[idx % 2]

                def p3_t(t, c=c, t_i=t_i, bank=bank, idx=idx):
                    if t_i == 0:
                        t.wait_ge(QW.h, qw_wv[c])
                    if idx == 0:
                        t.wait_ge(VS.h, vs_qe[2 * ND - 1])
                    if idx >= 2:
                        t.wait_ge(VS.h, vs_ve[idx - 2])
                    for d in range(ND):
                        ins = t.matmul(bank[:, :],
                                       xhatT[:, d, t_i * 128:(t_i + 1) * 128],
                                       wvc[:, c % 2, d, :],
                                       start=(d == 0), stop=(d == ND - 1))
                    TS.inc(ins)
                sec("tensor", p3_t)
                ts_v[c] = TS.n

                def p3_v(v, c=c, t_i=t_i, bank=bank, w=TS.n):
                    v.wait_ge(TS.h, w)
                    ins = v.tensor_copy(v_sb[:, t_i, c * 512:(c + 1) * 512], bank[:, :])
                    VS.inc(ins)
                sec("vector", p3_v)
                vs_ve[idx] = VS.n

        p3.close()
        sc_wqv.close()
        sc_xhat.close()

        sc_ctx = ExitStack()
        ctxT = sc_ctx.enter_context(nc.sbuf_tensor("ctxT", [128, ND, TOK], BF16, side="right"))

        p4 = ExitStack()
        pbuf = p4.enter_context(nc.sbuf_tensor("pbuf", [128, 2, 2, 384], BF16))
        pTb = p4.enter_context(nc.sbuf_tensor("pTb", [128, 2, 2, 3, 128], BF16))
        ctile = p4.enter_context(nc.sbuf_tensor("ctile", [128, 2, 2, 128], BF16))
        psS = [p4.enter_context(nc.psum_tensor(f"p4s{j}", [128, 384], F32)) for j in range(4)]
        psP = [p4.enter_context(nc.psum_tensor(f"p4p{j}", [128, 1024], BF16)) for j in range(2)]
        psC = [p4.enter_context(nc.psum_tensor(f"p4c{j}", [128, 256], F32)) for j in range(2)]

        NU = NT * 8
        NEGM0, DEN0, RD0 = 40, 44, 48

        ts_a, ts_b, ss_exp, vs_a, vs_b = {}, {}, {}, {}, {}
        ts_a[-1] = ts_b[-1] = TS.n
        for k in range(-3, 0):
            ss_exp[k] = SS.n
            vs_a[k] = vs_b[k] = VS.n

        def unit(k):
            return k // 8, (k % 8)

        for it in range(NU + 3):
            if it < NU:
                def p4_ta(t, it=it):
                    if it == 0:
                        t.wait_ge(VS.h, vs_ve[4 * NTE - 1])
                    t.wait_ge(SS.h, ss_exp[it - 2])
                    t_i, u = unit(it)
                    mi = min(t_i, 2)
                    par = it % 2
                    for j in range(2):
                        h = 2 * u + j
                        t.matmul(psS[2 * par + j][:, :],
                                 qT[:, h, t_i * 128:(t_i + 1) * 128],
                                 kT[:, h, t_i * 128:t_i * 128 + 384],
                                 start=True, stop=False)
                        ins = t.matmul(psS[2 * par + j][:, :], eyeb[:, :],
                                       maskb[:, mi, :], start=False, stop=True)
                    TS.inc(ins)
                sec("tensor", p4_ta)
            ts_a[it] = TS.n

            def p4_tb(t, it=it):
                ins = None
                if 1 <= it <= NU:
                    t.wait_ge(SS.h, ss_exp[it - 1])
                if 2 <= it:
                    t.wait_ge(VS.h, vs_b[it - 1])
                if 1 <= it <= NU:
                    par = (it - 1) % 2
                    for j in range(2):
                        for cc in range(3):
                            ins = t.transpose(psP[par][:, (j * 3 + cc) * 128:
                                                       (j * 3 + cc + 1) * 128],
                                              pbuf[:, par, j, cc * 128:(cc + 1) * 128],
                                              eyeb[:, :])
                if 2 <= it <= NU + 1:
                    k = it - 2
                    t_i, u = unit(k)
                    par = k % 2
                    for j in range(2):
                        h = 2 * u + j
                        for cc in range(3):
                            ins = t.matmul(psC[par][:, j * 128:(j + 1) * 128],
                                           pTb[:, par, j, cc, :],
                                           v_sb[:, t_i + cc, h * 128:(h + 1) * 128],
                                           start=(cc == 0), stop=(cc == 2))
                if 3 <= it <= NU + 2:
                    k = it - 3
                    par = k % 2
                    for j in range(2):
                        ins = t.transpose(psP[par][:, 768 + j * 128:768 + (j + 1) * 128],
                                          ctile[:, par, j, :], eyeb[:, :])
                if ins is not None:
                    TS.inc(ins)
            sec("tensor", p4_tb)
            ts_b[it] = TS.n

            if it < NU:
                def p4_va(v, it=it):
                    v.wait_ge(TS.h, ts_a[it])
                    par = it % 2
                    for j in range(2):
                        q = par * 2 + j
                        ins = v.reduce_max(small[:, NEGM0 + q:NEGM0 + q + 1],
                                           psS[2 * par + j][:, :], axis=AX.X,
                                           negate=True)
                    VS.inc(ins)
                sec("vector", p4_va)
            vs_a[it] = VS.n

            if it < NU:
                def p4_s(sc, it=it):
                    sc.wait_ge(TS.h, ts_a[it])
                    sc.wait_ge(VS.h, vs_a[it])
                    par = it % 2
                    for j in range(2):
                        q = par * 2 + j
                        ins = sc.activation(pbuf[:, par, j, :], psS[2 * par + j][:, :],
                                            AF.Exp,
                                            bias=small[:, NEGM0 + q:NEGM0 + q + 1],
                                            scale=1.0,
                                            accum_out=small[:, DEN0 + q:DEN0 + q + 1])
                        SS.inc(ins)
                sec("scalar", p4_s)
            ss_exp[it] = SS.n

            def p4_vb(v, it=it):
                ins = None
                if 1 <= it:
                    v.wait_ge(TS.h, ts_b[it])
                    v.wait_ge(SS.h, ss_exp[it - 1])
                if 1 <= it <= NU:
                    p1_ = (it - 1) % 2
                    v.reciprocal(small[:, RD0 + p1_ * 2:RD0 + p1_ * 2 + 2],
                                 small[:, DEN0 + p1_ * 2:DEN0 + p1_ * 2 + 2])
                    ins = v.tensor_copy(pTb[:, p1_, :, :, :], psP[p1_][:, 0:768]
                                        .rearrange("p (a b c) -> p a b c", a=2, b=3))
                if 2 <= it <= NU + 1:
                    k = it - 2
                    par = k % 2
                    for j in range(2):
                        q = par * 2 + j
                        ins = v.tensor_scalar(ctile[:, par, j, :],
                                              psC[par][:, j * 128:(j + 1) * 128],
                                              small[:, RD0 + q:RD0 + q + 1], None,
                                              op0=ALU.mult)
                if 3 <= it <= NU + 2:
                    k = it - 3
                    t_i, u = unit(k)
                    par = k % 2
                    ins = v.tensor_copy(ctxT[:, 2 * u:2 * u + 2,
                                             t_i * 128:(t_i + 1) * 128],
                                        psP[par][:, 768:1024]
                                        .rearrange("p (a b) -> p a b", a=2))
                if ins is not None:
                    VS.inc(ins)
            sec("vector", p4_vb)
            vs_b[it] = VS.n

        ts_p4_done = ts_b[NU + 2]
        vs_p4_done = vs_b[NU + 2]
        p4.close()
        sc_v.close()
        sc_q.close()
        sc_k.close()

        p5 = ExitStack()
        woc5 = p5.enter_context(nc.sbuf_tensor("woc5", [128, 2, ND, 512], BF16))
        xt5 = p5.enter_context(nc.sbuf_tensor("xt5", [128, NT, D], F32))
        gacc = p5.enter_context(nc.sbuf_tensor("gacc", [128, NT, 4], F32))
        gsc = p5.enter_context(nc.sbuf_tensor("gsc", [128, 512], F32))
        ps5 = [p5.enter_context(nc.psum_tensor(f"p5_{j}", [128, 512], F32)) for j in range(2)]

        qw_woc, ts_ol = {}, {}
        qx_x5, vs_ol, qs_x1 = {}, {}, {}

        def p5_x(sy):
            sy.wait_ge(TS.h, ts_p4_done)
            sy.dma_start(out=xt5[:, :, :],
                         in_=x_ext[256:1280, :].rearrange("(t p) d -> p t d", p=128)
                         ).then_inc(QX.h, 16)
            QX.n += 16
        sec("sync", p5_x)
        qx_x5[0] = QX.n

        for c in range(4):
            def p5_g(g, c=c):
                if c == 0:
                    g.wait_ge(TS.h, ts_b[NU + 1])
                if c >= 2:
                    g.wait_ge(TS.h, ts_ol[c - 2])
                g.dma_start(out=woc5[:, c % 2, :, :],
                            in_=wo_t[:, c * 512:(c + 1) * 512]
                            .rearrange("(kt p) n -> p kt n", p=128)).then_inc(QW.h, 16)
                QW.n += 16
            sec("gpsimd", p5_g)
            qw_woc[c] = QW.n

            for t_i in range(NT):
                idx = c * NT + t_i

                def p5_t(t, c=c, t_i=t_i, idx=idx):
                    if t_i == 0:
                        t.wait_ge(QW.h, qw_woc[c])
                        if c == 0:
                            t.wait_ge(VS.h, vs_p4_done)
                    if idx >= 2:
                        t.wait_ge(VS.h, vs_ol[idx - 2])
                    for d in range(ND):
                        ins = t.matmul(ps5[idx % 2][:, :],
                                       ctxT[:, d, t_i * 128:(t_i + 1) * 128],
                                       woc5[:, c % 2, d, :],
                                       start=(d == 0), stop=(d == ND - 1))
                    TS.inc(ins)
                sec("tensor", p5_t)
                w_ol = TS.n

                def p5_v(v, c=c, t_i=t_i, idx=idx, w=w_ol):
                    v.wait_ge(TS.h, w)
                    if idx == 0:
                        v.wait_ge(QX.h, qx_x5[0])
                        v.wait_ge(VS.h, vs_gb)
                    v.scalar_tensor_tensor(gsc[:, :], ps5[idx % 2][:, :], 1.0,
                                           gwb[:, c * 512:(c + 1) * 512],
                                           op0=ALU.mult, op1=ALU.mult,
                                           accum_out=gacc[:, t_i, c:c + 1])
                    ins = v.tensor_tensor(xt5[:, t_i, c * 512:(c + 1) * 512],
                                          xt5[:, t_i, c * 512:(c + 1) * 512],
                                          ps5[idx % 2][:, :], op=ALU.add)
                    VS.inc(ins)
                sec("vector", p5_v)
                vs_ol[idx] = VS.n

                if c == 3:
                    def p5_st(sy, t_i=t_i, w=VS.n):
                        sy.wait_ge(VS.h, w)
                        sy.dma_start(out=x1_hbm[t_i * 128:(t_i + 1) * 128, :],
                                     in_=xt5[:, t_i, :]).then_inc(QS.h, 16)
                        QS.n += 16
                    sec("sync", p5_st)
                    qs_x1[t_i] = QS.n
            ts_ol[c] = TS.n

        def p5_vg(v):
            for t_i in range(NT):
                ins = v.reduce_sum(small[:, 52 + t_i:53 + t_i], gacc[:, t_i, :],
                                   axis=AX.X)
            VS.inc(ins)
        sec("vector", p5_vg)
        vs_gp = VS.n

        def p5_sg(sc, w=vs_gp):
            sc.wait_ge(VS.h, w)
            for t_i in range(NT):
                ins = sc.activation(small[:, 18 + t_i:19 + t_i],
                                    small[:, 52 + t_i:53 + t_i],
                                    AF.Sigmoid, bias=small[:, 1:2], scale=1.0)
            SS.inc(ins)
        sec("scalar", p5_sg)
        ss_byp_all = SS.n
        vs_ol_last = vs_ol[4 * NT - 1]

        ts_p5_last = TS.n
        p5.close()
        sc_ctx.close()

        sc_r = ExitStack()
        rT = sc_r.enter_context(nc.sbuf_tensor("rT", [128, ND, TOK], BF16, side="right"))

        p6a = ExitStack()
        mbk = p6a.enter_context(nc.sbuf_tensor("mbk", [128, 32, D], BF16))
        simt = p6a.enter_context(nc.sbuf_tensor("simt", [128, 2048], BF16))
        emsk = p6a.enter_context(nc.sbuf_tensor("emsk", [128, 2048], BF16))
        pmem = p6a.enter_context(nc.sbuf_tensor("pmem", [128, 2048], BF16))
        pTm8 = p6a.enter_context(nc.sbuf_tensor("pTm8", [128, 2, 1024], BF16))
        psT6 = [p6a.enter_context(nc.psum_tensor(f"p6t{j}", [128, 1024], BF16))
                for j in range(2)]
        psR = [p6a.enter_context(nc.psum_tensor(f"p6r{j}", [128, 512], F32))
               for j in range(4)]

        qw_mbk = {}

        def p6_g(g):
            g.wait_ge(TS.h, ts_p5_last)
            g.wait_ge(QS.h, qs_x1[NT - 1])
            g.wait_ge(VS.h, vs_ol_last)
            for q in range(4):
                g.dma_start(out=mbk[:, q * 8:(q + 1) * 8, :],
                            in_=memb[q * 1024:(q + 1) * 1024, :]
                            .rearrange("(s p) d -> p s d", p=128)).then_inc(QW.h, 16)
                QW.n += 16
                qw_mbk[q] = QW.n
        sec("gpsimd", p6_g)

        vs_p6, ss_p6, ts_r6, vs_r6 = {}, {}, {}, {}
        ts_mm, vs_te6, ts_mm6 = {}, {}, {}
        for t_i in range(NT):
            def p6_seed(t, t_i=t_i):
                if t_i == 0:
                    t.wait_ge(VS.h, vs_ol_last)
                    t.wait_ge(VS.h, vs_gw)
                if t_i >= 1:
                    t.wait_ge(VS.h, vs_r6[t_i - 1])
                for bank in range(4):
                    ins = t.matmul(psR[bank][:, :], zrow[0:1, 0:128], zrow[:, :],
                                   start=True, stop=True)
                TS.inc(ins)
            sec("tensor", p6_seed)

            for hf in range(2):
                hx = t_i * 2 + hf

                def p6_ld(sy, t_i=t_i, hf=hf, hx=hx):
                    if hx == 0:
                        sy.wait_ge(QS.h, qs_x1[NT - 1])
                    if hx >= 1:
                        sy.wait_ge(VS.h, vs_p6[hx - 1])
                        sy.wait_ge(SS.h, ss_p6[hx - 1])
                    sy.dma_start(out=simt[:, :],
                                 in_=sim_hbm[t_i, :, hf * 2048:(hf + 1) * 2048]
                                 ).then_inc(QX.h, 16)
                    QX.n += 16
                sec("sync", p6_ld)
                w_ld6 = QX.n

                def p6_msk(v, t_i=t_i, w=w_ld6):
                    v.wait_ge(QX.h, w)
                    ins = v.tensor_scalar(emsk[:, :], simt[:, :],
                                          top8f[:, t_i, 7:8], None, op0=ALU.is_ge)
                    VS.inc(ins)
                sec("vector", p6_msk)

                def p6_exp(sc, t_i=t_i, hx=hx, w=w_ld6):
                    sc.wait_ge(QX.h, w)
                    if hx >= 1:
                        sc.wait_ge(TS.h, ts_mm[hx - 1])
                    ins = sc.activation(pmem[:, :], simt[:, :], AF.Exp,
                                        bias=small[:, 2 + t_i:3 + t_i], scale=ISQD)
                    SS.inc(ins)
                sec("scalar", p6_exp)
                ss_p6[hx] = SS.n

                def p6_p(v, t_i=t_i, w=SS.n):
                    v.wait_ge(SS.h, w)
                    ins = v.scalar_tensor_tensor(pmem[:, :], pmem[:, :],
                                                 small[:, 10 + t_i:11 + t_i], emsk[:, :],
                                                 op0=ALU.mult, op1=ALU.mult)
                    VS.inc(ins)
                sec("vector", p6_p)
                vs_p6[hx] = VS.n

                for gg in range(2):
                    g6 = hx * 2 + gg
                    pp = g6 % 2

                    def p6_tr(t, gg=gg, g6=g6, pp=pp, w=vs_p6[hx]):
                        if gg == 0:
                            t.wait_ge(VS.h, w)
                        if g6 >= 2:
                            t.wait_ge(VS.h, vs_te6[g6 - 2])
                        for k in range(8):
                            ins = t.transpose(psT6[pp][:, k * 128:(k + 1) * 128],
                                              pmem[:, (gg * 8 + k) * 128:
                                                   (gg * 8 + k + 1) * 128], eyeb[:, :])
                        TS.inc(ins)
                    sec("tensor", p6_tr)
                    w_tr6 = TS.n

                    def p6_te(v, g6=g6, pp=pp, w=w_tr6):
                        v.wait_ge(TS.h, w)
                        if g6 >= 2:
                            v.wait_ge(TS.h, ts_mm6[g6 - 2])
                        ins = v.tensor_copy(pTm8[:, pp, :], psT6[pp][:, :])
                        VS.inc(ins)
                    sec("vector", p6_te)
                    vs_te6[g6] = VS.n

                    def p6_mm(t, t_i=t_i, hf=hf, gg=gg, pp=pp, w=VS.n):
                        t.wait_ge(VS.h, w)
                        if t_i == 0:
                            t.wait_ge(QW.h, qw_mbk[hf * 2 + gg])
                        for k in range(8):
                            sk2 = hf * 16 + gg * 8 + k
                            for bank in range(4):
                                for dd in range(4):
                                    d = bank * 4 + dd
                                    ins = t.matmul(psR[bank][:, dd * 128:(dd + 1) * 128],
                                                   mbk[:, sk2, d * 128:(d + 1) * 128],
                                                   pTm8[:, pp, k * 128:(k + 1) * 128],
                                                   start=False, stop=(sk2 == 31),
                                                   skip_group_check=True)
                        TS.inc(ins)
                    sec("tensor", p6_mm)
                    ts_mm6[g6] = TS.n
                ts_mm[hx] = TS.n
            ts_r6[t_i] = TS.n

            def p6_re(v, t_i=t_i, w=TS.n):
                v.wait_ge(TS.h, w)
                for bank in range(4):
                    ins = v.tensor_copy(rT[:, bank * 4:(bank + 1) * 4,
                                           t_i * 128:(t_i + 1) * 128],
                                        psR[bank][:, :].rearrange("p (a b) -> p a b", a=4))
                VS.inc(ins, 4)
            sec("vector", p6_re)
            vs_r6[t_i] = VS.n

        p6a.close()

        sc_om = ExitStack()
        omT = sc_om.enter_context(nc.sbuf_tensor("omT", [128, ND, TOK], BF16))

        p6b = ExitStack()
        wvc2 = p6b.enter_context(nc.sbuf_tensor("wvc2", [128, 2, ND, 128], BF16))
        ps6b = [p6b.enter_context(nc.psum_tensor(f"p6b{j}", [128, 512], F32))
                for j in range(2)]

        qw_wv2, ts_om, vs_om = {}, {}, {}
        for m in range(ND):
            def p6b_g(g, m=m):
                if m == 0:
                    g.wait_ge(TS.h, ts_r6[NT - 1])
                if m >= 2:
                    g.wait_ge(TS.h, ts_om[m - 2])
                g.dma_start(out=wvc2[:, m % 2, :, :],
                            in_=wv_t[:, m * 128:(m + 1) * 128]
                            .rearrange("(kt p) n -> p kt n", p=128)).then_inc(QW.h, 16)
                QW.n += 16
            sec("gpsimd", p6b_g)
            qw_wv2[m] = QW.n

            def p6b_t(t, m=m):
                t.wait_ge(QW.h, qw_wv2[m])
                if m == 0:
                    t.wait_ge(VS.h, vs_r6[NT - 1])
                for c in range(2):
                    idx = m * 2 + c
                    if idx >= 2:
                        t.wait_ge(VS.h, vs_om[idx - 2])
                    for d in range(ND):
                        ins = t.matmul(ps6b[c][:, :],
                                       wvc2[:, m % 2, d, :],
                                       rT[:, d, c * 512:(c + 1) * 512],
                                       start=(d == 0), stop=(d == ND - 1))
                    TS.inc(ins)
            sec("tensor", p6b_t)
            ts_om[m] = TS.n

            def p6b_v(v, m=m, w=TS.n):
                v.wait_ge(TS.h, w)
                for c in range(2):
                    ins = v.tensor_copy(omT[:, m, c * 512:(c + 1) * 512], ps6b[c][:, :])
                    VS.inc(ins)
            sec("vector", p6b_v)
            vs_om[m * 2] = VS.n - 1
            vs_om[m * 2 + 1] = VS.n

        p6b.close()
        sc_r.close()

        sc_x2 = ExitStack()
        xhat2T = sc_x2.enter_context(nc.sbuf_tensor("xhat2T", [128, ND, TOK], BF16, side="right"))

        p6c = ExitStack()
        woT2 = p6c.enter_context(nc.sbuf_tensor("woT2", [128, ND, D], BF16))
        xt6 = p6c.enter_context(nc.sbuf_tensor("xt6", [128, 2, D], F32))
        xh6 = p6c.enter_context(nc.sbuf_tensor("xh6", [128, 2, D], BF16))
        sqs6 = p6c.enter_context(nc.sbuf_tensor("sqs6", [128, 2, D], F32))
        ps6c = [p6c.enter_context(nc.psum_tensor(f"p6c{j}", [128, 512], F32))
                for j in range(2)]
        psX6 = [p6c.enter_context(nc.psum_tensor(f"p6x{j}", [128, 1024], BF16))
                for j in range(2)]
        psL = p6c.enter_context(nc.psum_tensor("p6l", [128, 512], F32))

        def p6c_g(g):
            g.wait_ge(TS.h, ts_om[ND - 1])
            g.dma_start(out=woT2[:, :, :],
                        in_=wo_t.rearrange("(kt p) n -> p kt n", p=128)).then_inc(QW.h, 16)
            QW.n += 16
        sec("gpsimd", p6c_g)
        qw_wo2 = QW.n

        qx_x6, vs_om6, qs_x1f = {}, {}, {}
        vs_l3d, ts_xh2, vs_xh2, vs_lg = {}, {}, {}, {}
        for t_i in range(NT):
            def p6c_x(sy, t_i=t_i):
                if t_i == 0:
                    sy.wait_ge(TS.h, ts_om[ND - 1])
                if t_i >= 2:
                    sy.wait_ge(QS.h, qs_x1f[t_i - 2])
                    sy.wait_ge(VS.h, vs_l3d[t_i - 2])
                sy.dma_start(out=xt6[:, t_i % 2, :],
                             in_=x1_hbm[t_i * 128:(t_i + 1) * 128, :]).then_inc(QX.h, 16)
                QX.n += 16
            sec("sync", p6c_x)
            qx_x6[t_i] = QX.n

            for c in range(4):
                idx = t_i * 4 + c

                def p6c_t(t, t_i=t_i, c=c, idx=idx):
                    if idx == 0:
                        t.wait_ge(QW.h, qw_wo2)
                        t.wait_ge(VS.h, vs_om[2 * ND - 1])
                    if idx >= 2:
                        t.wait_ge(VS.h, vs_om6[idx - 2])
                    for d in range(ND):
                        ins = t.matmul(ps6c[idx % 2][:, :],
                                       omT[:, d, t_i * 128:(t_i + 1) * 128],
                                       woT2[:, d, c * 512:(c + 1) * 512],
                                       start=(d == 0), stop=(d == ND - 1))
                    TS.inc(ins)
                sec("tensor", p6c_t)
                w_om = TS.n

                def p6c_v(v, t_i=t_i, c=c, idx=idx, w=w_om):
                    v.wait_ge(TS.h, w)
                    if c == 0:
                        v.wait_ge(QX.h, qx_x6[t_i])
                        if t_i == 0:
                            v.wait_ge(SS.h, ss_byp_all)
                    ins = v.scalar_tensor_tensor(xt6[:, t_i % 2, c * 512:(c + 1) * 512],
                                                 ps6c[idx % 2][:, :],
                                                 small[:, 18 + t_i:19 + t_i],
                                                 xt6[:, t_i % 2, c * 512:(c + 1) * 512],
                                                 op0=ALU.mult, op1=ALU.add)
                    VS.inc(ins)
                sec("vector", p6c_v)
                vs_om6[idx] = VS.n

            def p6c_st(sy, t_i=t_i, w=VS.n):
                sy.wait_ge(VS.h, w)
                sy.dma_start(out=x1_hbm[t_i * 128:(t_i + 1) * 128, :],
                             in_=xt6[:, t_i % 2, :]).then_inc(QS.h, 16)
                QS.n += 16
            sec("sync", p6c_st)
            qs_x1f[t_i] = QS.n

            def p7_v1(v, t_i=t_i):
                ins = v.reduce_sum(lnst[:, t_i, 0:1], xt6[:, t_i % 2, :], axis=AX.X)
                VS.inc(ins)
            sec("vector", p7_v1)
            w_l1 = VS.n

            def p7_s1(sc, t_i=t_i, w=w_l1):
                sc.wait_ge(VS.h, w)
                ins = sc.activation(sqs6[:, t_i % 2, :], xt6[:, t_i % 2, :], AF.Square,
                                    accum_out=lnst[:, t_i, 1:2])
                SS.inc(ins)
            sec("scalar", p7_s1)
            w_s1 = SS.n

            def p7_v2(v, t_i=t_i, w=w_s1):
                v.wait_ge(SS.h, w)
                v.tensor_scalar(lnst[:, t_i, 2:3], lnst[:, t_i, 0:1], 1.0 / D, None,
                                op0=ALU.mult)
                v.tensor_tensor(lnst[:, t_i, 3:4], lnst[:, t_i, 2:3], lnst[:, t_i, 2:3],
                                op=ALU.mult)
                ins = v.scalar_tensor_tensor(lnst[:, t_i, 4:5], lnst[:, t_i, 1:2], 1.0 / D,
                                             lnst[:, t_i, 3:4], op0=ALU.mult,
                                             op1=ALU.subtract)
                VS.inc(ins)
            sec("vector", p7_v2)
            w_v2 = VS.n

            def p7_s2(sc, t_i=t_i, w=w_v2):
                sc.wait_ge(VS.h, w)
                ins = sc.activation(lnst[:, t_i, 5:6], lnst[:, t_i, 4:5], AF.Sqrt,
                                    bias=small[:, 0:1], scale=1.0)
                SS.inc(ins)
            sec("scalar", p7_s2)
            w_s2 = SS.n

            def p7_v3(v, t_i=t_i, w=w_s2):
                v.wait_ge(SS.h, w)
                if t_i >= 2:
                    v.wait_ge(TS.h, ts_xh2[t_i - 2])
                v.reciprocal(lnst[:, t_i, 6:7], lnst[:, t_i, 5:6])
                v.tensor_scalar(small[:, 38:39], lnst[:, t_i, 6:7], 1.0, None, op0=ALU.mult)
                ins = v.tensor_scalar(xh6[:, t_i % 2, :], xt6[:, t_i % 2, :],
                                      lnst[:, t_i, 2:3], lnst[:, t_i, 6:7],
                                      op0=ALU.subtract, op1=ALU.mult)
                VS.inc(ins)
            sec("vector", p7_v3)
            vs_l3d[t_i] = VS.n

            def p7_t(t, t_i=t_i, w=VS.n):
                t.wait_ge(VS.h, w)
                if t_i >= 2:
                    t.wait_ge(VS.h, vs_xh2[t_i - 2])
                for d in range(ND):
                    bank = psX6[0] if d < 8 else psX6[1]
                    ins = t.transpose(bank[:, (d % 8) * 128:(d % 8 + 1) * 128],
                                      xh6[:, t_i % 2, d * 128:(d + 1) * 128], eyeb[:, :])
                TS.inc(ins)
            sec("tensor", p7_t)
            ts_xh2[t_i] = TS.n

            def p7_se(v, t_i=t_i, w=TS.n):
                v.wait_ge(TS.h, w)
                v.tensor_copy(xhat2T[:, 0:8, t_i * 128:(t_i + 1) * 128],
                              psX6[0][:, :].rearrange("p (a b) -> p a b", a=8))
                ins = v.tensor_copy(xhat2T[:, 8:16, t_i * 128:(t_i + 1) * 128],
                                    psX6[1][:, :].rearrange("p (a b) -> p a b", a=8))
                VS.inc(ins)
            sec("vector", p7_se)
            vs_xh2[t_i] = VS.n

            def p7_rt(t, t_i=t_i, w=VS.n):
                t.wait_ge(VS.h, w)
                if t_i >= 1:
                    t.wait_ge(VS.h, vs_lg[t_i - 1])
                for d in range(ND):
                    t.matmul(psL[:, 0:NE], xhat2T[:, d, t_i * 128:(t_i + 1) * 128],
                             wrT[:, d, :], start=(d == 0), stop=False)
                ins = t.matmul(psL[:, 0:NE], onesb[:, :], fbias_b[:, :],
                               start=False, stop=True)
                TS.inc(ins)
            sec("tensor", p7_rt)
            w_rt = TS.n

            def p7_le(v, t_i=t_i, w=w_rt):
                v.wait_ge(TS.h, w)
                ins = v.tensor_copy(logit_sb[:, t_i, :], psL[:, 0:NE])
                VS.inc(ins)
            sec("vector", p7_le)
            vs_lg[t_i] = VS.n

        p6c.close()

        def p7_r1(v):
            for t_i in range(NT):
                ins = v.max(out=top8f[:, t_i, :], in_=logit_sb[:, t_i, :])
            VS.inc(ins)
        sec("vector", p7_r1)
        vs_r1 = VS.n

        def p7_r2(sc):
            sc.wait_ge(VS.h, vs_r1)
            for t_i in range(NT):
                sc.activation(rst[:, t_i, 0:1], top8f[:, t_i, 0:1], AF.Identity, scale=-1.0)
            for t_i in range(NT):
                sc.activation(rst[:, t_i, 1:2], top8f[:, t_i, 1:2], AF.Exp,
                              bias=rst[:, t_i, 0:1], scale=1.0)
            for t_i in range(NT):
                ins = sc.activation(lexp[:, t_i, :], logit_sb[:, t_i, :], AF.Exp,
                                    bias=rst[:, t_i, 0:1], scale=1.0,
                                    accum_out=rst[:, t_i, 2:3])
            SS.inc(ins)
        sec("scalar", p7_r2)
        ss_r2 = SS.n

        def p7_r3(v):
            v.wait_ge(SS.h, ss_r2)
            for t_i in range(NT):
                v.scalar_tensor_tensor(rst[:, t_i, 3:4], rst[:, t_i, 2:3], EPS_RT,
                                       rst[:, t_i, 1:2], op0=ALU.mult, op1=ALU.add)
            for t_i in range(NT):
                v.tensor_scalar(rst[:, t_i, 3:4], rst[:, t_i, 3:4], 1.0, None, op0=ALU.add)
            for t_i in range(NT):
                v.reciprocal(rst[:, t_i, 4:5], rst[:, t_i, 3:4])
            for t_i in range(NT):
                ins = v.tensor_tensor(rst[:, t_i, 5:6], rst[:, t_i, 1:2], rst[:, t_i, 4:5],
                                      op=ALU.mult)
            VS.inc(ins)
            for t_i in range(NT):
                v.tensor_scalar(msk1[:, t_i, :], logit_sb[:, t_i, :],
                                top8f[:, t_i, 0:1], None, op0=ALU.is_ge)
            for t_i in range(NT):
                v.tensor_scalar(msk2[:, t_i, :], logit_sb[:, t_i, :],
                                top8f[:, t_i, 1:2], None, op0=ALU.is_ge)
            for t_i in range(NT):
                v.tensor_tensor(msk2[:, t_i, :], msk2[:, t_i, :], msk1[:, t_i, :],
                                op=ALU.subtract)
            for t_i in range(NT):
                v.tensor_scalar(msk1[:, t_i, :], msk1[:, t_i, :], rst[:, t_i, 4:5],
                                None, op0=ALU.mult)
            for t_i in range(NT):
                v.tensor_scalar(msk2[:, t_i, :], msk2[:, t_i, :], rst[:, t_i, 5:6],
                                None, op0=ALU.mult)
            for t_i in range(NT):
                ins = v.tensor_tensor(wmat_tok[:, t_i, :], msk1[:, t_i, :],
                                      msk2[:, t_i, :], op=ALU.add)
            VS.inc(ins)
        sec("vector", p7_r3)
        vs_wm = VS.n

        p_w = ExitStack()
        pwt = p_w.enter_context(nc.psum_tensor("pwt", [128, 1024], BF16))

        def p7_wt(t):
            t.wait_ge(VS.h, vs_wm)
            for t_i in range(NT):
                ins = t.transpose(pwt[0:NE, t_i * 128:(t_i + 1) * 128],
                                  wmat_tok[:, t_i, :], eyeb[:, :])
            TS.inc(ins)
        sec("tensor", p7_wt)
        ts_wt = TS.n

        def p7_we(sc):
            sc.wait_ge(TS.h, ts_wt)
            ins = sc.copy(WmatT[:, :], pwt[0:NE, :])
            SS.inc(ins)
        sec("scalar", p7_we)
        ss_wm = SS.n
        p_w.close()
        sc_om.close()

        sc_acc = ExitStack()
        acc = sc_acc.enter_context(nc.sbuf_tensor("acc", [128, ND, TOK], F32))

        p8 = ExitStack()
        wsl = p8.enter_context(nc.sbuf_tensor("wsl", [128, 4, ND * 512], BF16))
        z_sb = p8.enter_context(nc.sbuf_tensor("z_sb", [128, NI, TOK], BF16))
        gs_sb = p8.enter_context(nc.sbuf_tensor("gs_sb", [128, 2, TOK], BF16))
        wb_sb = p8.enter_context(nc.sbuf_tensor("wb_sb", [128, TOK], BF16))
        tmp8 = p8.enter_context(nc.sbuf_tensor("tmp8", [128, 512], BF16))
        ps8 = [p8.enter_context(nc.psum_tensor(f"p8_{j}", [128, 512], F32))
               for j in range(8)]

        def wsl_gu(buf, d, itl):
            return wsl[:, buf, d * 512 + itl * 128:d * 512 + itl * 128 + 128]

        def wsl_wd(buf, itl, col):
            return wsl[:, buf, itl * 2048 + col:itl * 2048 + col + 128]

        def slot_buf(e, name):
            P = (e % 2) * 2
            return {"g0": P, "u0": P + 1, "g1": 2 - P, "u1": 3 - P,
                    "d1": P, "d2": P + 1}[name]

        qw_slot = {}

        def emit_slot_load(e, name, wait_ts=None, wait_ss=None):
            buf = slot_buf(e, name)
            kind, hf = name[0], int(name[1]) - (1 if name[0] == "d" else 0)

            def _g(g, e=e, kind=kind, hf=hf, buf=buf, wait_ts=wait_ts,
                   wait_ss=wait_ss):
                if wait_ss is not None:
                    g.wait_ge(SS.h, wait_ss)
                if wait_ts is not None:
                    g.wait_ge(TS.h, wait_ts)
                if kind == "g":
                    ap = (sWg_d if e == NE else Wg_d[e, :, :])
                    ap = ap[:, hf * 512:(hf + 1) * 512].rearrange("(kt p) n -> p kt n", p=128)
                elif kind == "u":
                    ap = (sWu_d if e == NE else Wu_d[e, :, :])
                    ap = ap[:, hf * 512:(hf + 1) * 512].rearrange("(kt p) n -> p kt n", p=128)
                else:
                    ap = (sWd_d if e == NE else Wd_d[e, :, :])
                    ap = ap[hf * 512:(hf + 1) * 512, :].rearrange("(it p) h -> p it h", p=128)
                dst = wsl[:, buf, :].rearrange("p (a b) -> p a b",
                                               a=ND if kind != "d" else 4)
                g.dma_start(out=dst, in_=ap).then_inc(QW.h, 16)
                QW.n += 16
            sec("gpsimd", _g)
            qw_slot[(e, name)] = QW.n

        qw_wrow = {}

        def emit_wrow(e):
            def _g(g, e=e):
                if e == 0:
                    g.wait_ge(SS.h, ss_wm)
                if e >= 2:
                    g.wait_ge(TS.h, ts_wbd[e - 2])
                g.dma_start(out=wrow[0:1, e % 2, :],
                            in_=WmatT[e:e + 1, :]).then_inc(QW.h, 16)
                QW.n += 16
            sec("gpsimd", _g)
            qw_wrow[e] = QW.n

        emit_wrow(0)
        emit_slot_load(0, "g0", wait_ss=ss_wm)
        emit_slot_load(0, "u0")
        emit_slot_load(0, "g1")
        emit_slot_load(0, "u1")
        emit_wrow(1)

        vs_z, vs_acc, ss_silu, ss_wbd = {}, {}, {}, {}
        ts_wbd, ts_gu, ts_d = {}, {}, {}
        for e in range(NE + 1):
            shared = (e == NE)

            if not shared:
                def p8_wb(t, e=e):
                    t.wait_ge(QW.h, qw_wrow[e])
                    if e == 0:
                        t.wait_ge(SS.h, ss_wm)
                    else:
                        t.wait_ge(VS.h, vs_acc[(e - 1, 15)])
                    for c in range(2):
                        ins = t.matmul(ps8[6 + c][:, :], onesb[:, :],
                                       wrow[0:1, e % 2, c * 512:(c + 1) * 512],
                                       start=True, stop=True)
                    TS.inc(ins)
                sec("tensor", p8_wb)
                w_wb = TS.n
                ts_wbd[e] = TS.n

                def p8_wbe(sc, w=w_wb):
                    sc.wait_ge(TS.h, w)
                    for c in range(2):
                        ins = sc.copy(wb_sb[:, c * 512:(c + 1) * 512], ps8[6 + c][:, :])
                        SS.inc(ins)
                sec("scalar", p8_wbe)
            ss_wbd[e] = SS.n

            for it in range(NI):
                hf = it // 4
                par = it % 2
                gb0, ub0 = (0, 2) if par == 0 else (4, 6)
                bg = slot_buf(e, "g0" if hf == 0 else "g1")
                bu = slot_buf(e, "u0" if hf == 0 else "u1")

                def p8_gu(t, e=e, it=it, hf=hf, par=par, gb0=gb0, ub0=ub0,
                          bg=bg, bu=bu):
                    if it == 0:
                        t.wait_ge(QW.h, qw_slot[(e, "u0")])
                    if it == 4:
                        t.wait_ge(QW.h, qw_slot[(e, "u1")])
                    if it == 0 and e == 0:
                        t.wait_ge(VS.h, vs_xh2[NT - 1])
                    if it == 0 and e > 0:
                        t.wait_ge(VS.h, vs_acc[(e - 1, 13)])
                    if it == 1 and e > 0:
                        t.wait_ge(VS.h, vs_acc[(e - 1, 15)])
                    if it == 1 and not (e == NE):
                        t.wait_ge(SS.h, ss_wbd[e])
                    if it >= 2:
                        t.wait_ge(SS.h, ss_silu[(e, it - 2)])
                        t.wait_ge(VS.h, vs_z[(e, it - 2)])
                    for th in range(2):
                        for d in range(ND):
                            t.matmul(ps8[gb0 + th][:, :],
                                     wsl_gu(bg, d, it % 4),
                                     xhat2T[:, d, th * 512:(th + 1) * 512],
                                     start=(d == 0), stop=(d == ND - 1))
                    for th in range(2):
                        for d in range(ND):
                            ins = t.matmul(ps8[ub0 + th][:, :],
                                           wsl_gu(bu, d, it % 4),
                                           xhat2T[:, d, th * 512:(th + 1) * 512],
                                           start=(d == 0), stop=(d == ND - 1))
                    TS.inc(ins)
                sec("tensor", p8_gu)
                w_gu = TS.n
                ts_gu[(e, it)] = TS.n
                if it == 3:
                    emit_slot_load(e, "d1", wait_ts=TS.n)
                    emit_slot_load(e, "d2")
                if it == 7 and e + 1 <= NE:
                    emit_slot_load(e + 1, "g0", wait_ts=TS.n)
                    emit_slot_load(e + 1, "u0")
                    if e + 2 <= NE - 1:
                        emit_wrow(e + 2)

                def p8_silu(sc, e=e, it=it, par=par, gb0=gb0, w=w_gu):
                    sc.wait_ge(TS.h, w)
                    for th in range(2):
                        ins = sc.activation(gs_sb[:, par, th * 512:(th + 1) * 512],
                                            ps8[gb0 + th][:, :], AF.Silu)
                    SS.inc(ins)
                sec("scalar", p8_silu)
                ss_silu[(e, it)] = SS.n

                def p8_z(v, e=e, it=it, par=par, ub0=ub0, shared=shared, w=SS.n):
                    v.wait_ge(SS.h, w)
                    if it == 0 and e > 0:
                        v.wait_ge(TS.h, ts_d[(e - 1, 15)])
                    for th in range(2):
                        if shared:
                            ins = v.tensor_tensor(z_sb[:, it, th * 512:(th + 1) * 512],
                                                  gs_sb[:, par, th * 512:(th + 1) * 512],
                                                  ps8[ub0 + th][:, :], op=ALU.mult)
                        else:
                            v.tensor_tensor(tmp8[:, :],
                                            gs_sb[:, par, th * 512:(th + 1) * 512],
                                            ps8[ub0 + th][:, :], op=ALU.mult)
                            ins = v.tensor_tensor(z_sb[:, it, th * 512:(th + 1) * 512],
                                                  tmp8[:, :],
                                                  wb_sb[:, th * 512:(th + 1) * 512],
                                                  op=ALU.mult)
                    VS.inc(ins)
                sec("vector", p8_z)
                vs_z[(e, it)] = VS.n

            bd1 = slot_buf(e, "d1")
            bd2 = slot_buf(e, "d2")
            for hb in range(16):
                pr = hb % 4

                def p8_d(t, e=e, hb=hb, pr=pr, bd1=bd1, bd2=bd2):
                    if hb == 0:
                        t.wait_ge(VS.h, vs_z[(e, NI - 1)])
                        t.wait_ge(SS.h, ss_silu[(e, NI - 1)])
                        t.wait_ge(QW.h, qw_slot[(e, "d2")])
                    if hb >= 4:
                        t.wait_ge(VS.h, vs_acc[(e, hb - 4)])
                    for th in range(2):
                        for it in range(NI):
                            buf = bd1 if it < 4 else bd2
                            ins = t.matmul(ps8[2 * pr + th][:, :],
                                           wsl_wd(buf, it % 4, hb * 128),
                                           z_sb[:, it, th * 512:(th + 1) * 512],
                                           start=(it == 0), stop=(it == NI - 1))
                    TS.inc(ins)
                sec("tensor", p8_d)
                ts_d[(e, hb)] = TS.n

                def p8_acc(v, e=e, hb=hb, pr=pr, w=TS.n):
                    v.wait_ge(TS.h, w)
                    for th in range(2):
                        if e == 0:
                            ins = v.tensor_copy(acc[:, hb, th * 512:(th + 1) * 512],
                                                ps8[2 * pr + th][:, :])
                        else:
                            ins = v.tensor_tensor(acc[:, hb, th * 512:(th + 1) * 512],
                                                  acc[:, hb, th * 512:(th + 1) * 512],
                                                  ps8[2 * pr + th][:, :], op=ALU.add)
                    VS.inc(ins)
                sec("vector", p8_acc)
                vs_acc[(e, hb)] = VS.n

            if e + 1 <= NE:
                emit_slot_load(e + 1, "g1", wait_ts=ts_d[(e, 15)])
                emit_slot_load(e + 1, "u1")

        p8.close()
        sc_x2.close()

        p9 = ExitStack()
        xt9 = p9.enter_context(nc.sbuf_tensor("xt9", [128, 2, D], F32))
        o9 = p9.enter_context(nc.sbuf_tensor("o9", [128, 2, D], F32))
        ps9 = [p9.enter_context(nc.psum_tensor(f"p9_{j}", [128, 512], F32))
               for j in range(8)]

        qx_x9, vs_o9, qs_out = {}, {}, {}
        for t_i in range(NT):
            def p9_x(sy, t_i=t_i):
                if t_i == 0:
                    sy.wait_ge(QS.h, qs_x1f[NT - 1])
                    sy.wait_ge(VS.h, vs_acc[(NE, 15)])
                if t_i >= 2:
                    sy.wait_ge(QS.h, qs_out[t_i - 2])
                sy.dma_start(out=xt9[:, t_i % 2, :],
                             in_=x1_hbm[t_i * 128:(t_i + 1) * 128, :]).then_inc(QX.h, 16)
                QX.n += 16
            sec("sync", p9_x)
            qx_x9[t_i] = QX.n

            def p9_t(t, t_i=t_i):
                if t_i == 0:
                    t.wait_ge(VS.h, vs_acc[(NE, 15)])
                if t_i >= 2:
                    t.wait_ge(VS.h, vs_o9[t_i - 2])
                for d in range(ND):
                    ins = t.transpose(ps9[4 * (t_i % 2) + d // 4]
                                      [:, (d % 4) * 128:(d % 4 + 1) * 128],
                                      acc[:, d, t_i * 128:(t_i + 1) * 128], eyef[:, :])
                TS.inc(ins)
            sec("tensor", p9_t)
            w_o9 = TS.n

            def p9_v(v, t_i=t_i, w=w_o9):
                v.wait_ge(TS.h, w)
                v.wait_ge(QX.h, qx_x9[t_i])
                for grp in range(4):
                    ins = v.tensor_tensor(o9[:, t_i % 2, grp * 512:(grp + 1) * 512],
                                          xt9[:, t_i % 2, grp * 512:(grp + 1) * 512],
                                          ps9[4 * (t_i % 2) + grp][:, :], op=ALU.add)
                VS.inc(ins)
            sec("vector", p9_v)
            vs_o9[t_i] = VS.n

            def p9_s(sy, t_i=t_i, w=VS.n):
                sy.wait_ge(VS.h, w)
                sy.dma_start(out=out_d[t_i * 128:(t_i + 1) * 128, :],
                             in_=o9[:, t_i % 2, :]).then_inc(QS.h, 16)
                QS.n += 16
            sec("sync", p9_s)
            qs_out[t_i] = QS.n

        p9.close()
        sc_acc.close()

        def p_final(sy):
            sy.wait_ge(QS.h, QS.n)
        sec("sync", p_final)

    return nc



_NC_CACHE = {}


def _get_nc():
    if "nc" not in _NC_CACHE:
        _NC_CACHE["nc"] = build_nc()
    return _NC_CACHE["nc"]


def kernel(L0_attn_hidden_states, L0_attn_wq, L0_attn_wk, L0_attn_wv, L0_attn_wo,
           L0_attn_gate_w, L0_attn_gate_b, L0_attn_mem_bank,
           L0_ffn_Wr, L0_ffn_bias, L0_ffn_Wg, L0_ffn_Wu, L0_ffn_Wd,
           L0_ffn_sWg, L0_ffn_sWu, L0_ffn_sWd):
    x = np.ascontiguousarray(np.asarray(L0_attn_hidden_states, dtype=np.float32))
    wq = np.asarray(L0_attn_wq, np.float32)
    wk = np.asarray(L0_attn_wk, np.float32)
    wv = np.asarray(L0_attn_wv, np.float32)
    wo = np.asarray(L0_attn_wo, np.float32)
    gw = np.ascontiguousarray(np.asarray(L0_attn_gate_w, np.float32).reshape(1, D))
    gb = np.ascontiguousarray(np.asarray(L0_attn_gate_b, np.float32).reshape(1, 1))
    mb = np.ascontiguousarray(np.asarray(L0_attn_mem_bank, np.float32))
    Wr = np.asarray(L0_ffn_Wr, np.float32)
    fb = np.ascontiguousarray(np.asarray(L0_ffn_bias, np.float32).reshape(1, NE))
    Wg = np.ascontiguousarray(np.asarray(L0_ffn_Wg, np.float32))
    Wu = np.ascontiguousarray(np.asarray(L0_ffn_Wu, np.float32))
    Wd = np.ascontiguousarray(np.asarray(L0_ffn_Wd, np.float32))
    sWg = np.ascontiguousarray(np.asarray(L0_ffn_sWg, np.float32))
    sWu = np.ascontiguousarray(np.asarray(L0_ffn_sWu, np.float32))
    sWd = np.ascontiguousarray(np.asarray(L0_ffn_sWd, np.float32))

    wq_t = np.ascontiguousarray(wq.T)
    wk_t = np.ascontiguousarray(wk.T)
    wv_t = np.ascontiguousarray(wv.T)
    wo_t = np.ascontiguousarray(wo.T)
    wr_t = np.ascontiguousarray(Wr.T)
    memb_t = np.ascontiguousarray(mb.T)

    eye = np.eye(128, dtype=np.float32)
    ones = np.ones((1, 128), np.float32)
    epsc = np.full((128, 1), EPS_LN, np.float32)

    i = np.arange(128)[:, None]
    j = np.arange(384)[None, :]
    band = (j > i) & (j <= i + 256)

    in_maps = []
    B, T = 2, 4096
    for core in range(8):
        b = core // 4
        t0 = (core % 4) * 1024
        lo = t0 - 256
        if lo < 0:
            xe = np.concatenate([np.zeros((-lo, D), np.float32),
                                 x[b, 0:t0 + 1024]], axis=0)
        else:
            xe = x[b, lo:t0 + 1024]
        xe = np.ascontiguousarray(xe)
        edge = (t0 == 0)
        mks = np.empty((3, 128, 384), np.float32)
        for mi in range(3):
            allowed = band.copy()
            if edge and mi < 2:
                allowed = allowed & (j >= (256 - mi * 128))
            mks[mi] = np.where(allowed, 0.0, -1e30)
        in_maps.append({
            "x_ext": xe, "wq_t": wq_t, "wk_t": wk_t, "wk_d": wk, "wv_t": wv_t,
            "wo_t": wo_t,
            "wr_t": wr_t, "gate_w": gw, "gate_b": gb, "memb_t": memb_t, "memb": mb,
            "Wg_d": Wg, "Wu_d": Wu, "Wd_d": Wd, "sWg_d": sWg, "sWu_d": sWu,
            "sWd_d": sWd, "fbias": fb, "masks": np.ascontiguousarray(mks),
            "eye_f": eye, "ones_r": ones, "eps_c": epsc,
        })

    nc = _get_nc()
    res = run_bass_kernel_spmd(nc, in_maps, core_ids=list(range(8)))
    outs = [np.asarray(r["out"]) for r in res.results]
    full = np.empty((B, T, D), np.float32)
    for core in range(8):
        b = core // 4
        t0 = (core % 4) * 1024
        full[b, t0:t0 + 1024] = outs[core]
    return full
